# version 01
# baseline (speedup 1.0000x reference)
import contextlib
import numpy as np
import ml_dtypes
import concourse.bass as bass
import concourse.mybir as mybir
from concourse.bass_utils import run_bass_kernel_spmd

F32 = mybir.dt.float32
BF16 = mybir.dt.bfloat16
I32 = mybir.dt.int32
AF = mybir.ActivationFunctionType
ALU = mybir.AluOpType
AX = mybir.AxisListType

B, S, D, H, DH = 4, 4096, 1024, 16, 64
SH = S // 2
NT = SH // 128
NTF = S // 128
NKT = S // 128
C_LUT = float(np.float32(4096.0 / (2.0 * np.pi)))
DELTA = float(np.float32(np.float32(2.0 * np.pi) / 4096.0))
BIAS_COS = float(np.float32(1024.0) * np.float32(DELTA))
SCALE_A = float(np.float32(1.0 / np.sqrt(np.float32(128.0))))
SUBLAYERS = 4
ATTNB_STAGE = 60
_LAST_CNT = None
_CACHE = {}


def build_nc(n_sub=4, bstage=None):
    if bstage is None:
        bstage = ATTNB_STAGE
    nc = bass.Bass("TRN2", target_bir_lowering=False)

    ein = lambda n, sh, dt=F32: nc.dram_tensor(n, sh, dt, kind="ExternalInput").ap()
    x_in = ein("x_in", [SH, D])
    tprm = ein("tprm", [128, 16])
    bffi = ein("bffi", [128, 8])
    vw_a = ein("vw_a", [8, 128, 512])
    wo_a = ein("wo_a", [8, 128, 1024])
    wq_b = ein("wq_b", [8, 128, 512])
    wk_b = ein("wk_b", [8, 128, 512])
    wv_b = ein("wv_b", [8, 128, 512])
    wo_b = ein("wo_b", [8, 128, 1024])
    wff = ein("wff", [8, 128, 1024])
    masks_i = ein("masks_i", [4, 128, 512], BF16)
    ident_i = ein("ident_i", [128, 128], BF16)
    ind2_i = ein("ind2_i", [2, 128], BF16)
    ones2_i = ein("ones2_i", [2, 128], BF16)
    ones1_i = ein("ones1_i", [1, 128], BF16)
    ones1f_i = ein("ones1f_i", [1, 128])
    bigm_i = ein("bigm_i", [128, 192], BF16)
    selb_i = ein("selb_i", [128, 4096], BF16)
    cvals_i = ein("cvals_i", [128, 4])
    y_out = nc.dram_tensor("y_out", [SH, D], F32, kind="ExternalOutput").ap()
    dbg_p = nc.dram_tensor("dbg_p", [128, 2048], BF16, kind="ExternalOutput").ap()
    dbg_l = nc.dram_tensor("dbg_l", [128, 512], F32, kind="ExternalOutput").ap()
    dbg_q = nc.dram_tensor("dbg_q", [128, 512], BF16, kind="ExternalOutput").ap()
    dbg_k = nc.dram_tensor("dbg_k", [128, 512], BF16, kind="ExternalOutput").ap()
    dbg_h = nc.dram_tensor("dbg_h", [128, 512], BF16, kind="ExternalOutput").ap()
    dbg_h2 = nc.dram_tensor("dbg_h2", [128, 512], BF16, kind="ExternalOutput").ap()
    dbg_ln = nc.dram_tensor("dbg_ln", [128, 8], F32, kind="ExternalOutput").ap()
    dbg_hd = nc.dram_tensor("dbg_hd", [128, 512], BF16, kind="ExternalOutput").ap()
    dbg_om = nc.dram_tensor("dbg_om", [128, 512], BF16, kind="ExternalOutput").ap()
    dbg_oa = nc.dram_tensor("dbg_oa", [128, 512], BF16, kind="ExternalOutput").ap()
    dbg_ob = nc.dram_tensor("dbg_ob", [128, 1024], BF16, kind="ExternalOutput").ap()

    din = lambda n, sh, dt=BF16, **kw: nc.dram_tensor(n, sh, dt, kind="Internal", **kw).ap()
    h_sh = din("h_sh", [2, 8, 128, SH], BF16, addr_space="Shared")
    o_sh = din("o_sh", [8, 128, S], BF16, addr_space="Shared")
    h_full = din("h_full", [8, 128, S])
    qa_d = din("qa_d", [8, 128, S])
    ka_d = din("ka_d", [8, 128, S])
    va_d = din("va_d", [NKT, 128, 512])
    qb_d = din("qb_d", [4, 128, S])
    kb_d = din("kb_d", [4, 128, S])
    vb_d = din("vb_d", [NKT, 128, 512])
    WCOLS = {"vw_a": 512, "wo_a": 1024, "wq_b": 512, "wk_b": 512,
             "wv_b": 512, "wo_b": 1024, "wffT": 1024}
    w_bf = {n: din("bf_" + n, [8, 128, c]) for n, c in WCOLS.items()}
    bar_i = din("bar_i", [1, 64], F32)
    bar_o = din("bar_o", [2, 64], F32)

    st = contextlib.ExitStack()
    sb = lambda n, sh, dt: st.enter_context(nc.sbuf_tensor(n, sh, dt))
    pst = lambda n, sh, dt=F32: st.enter_context(nc.psum_tensor(n, sh, dt))

    x_res = sb("x_res", [128, NT * D], F32)
    SA = sb("SA", [128, 16384], BF16)
    OM = sb("OM", [128, 4 * S], BF16)
    SC = sb("SC", [128, 3 * 1024], F32)
    SD = sb("SD", [128, 8192], BF16)
    SEb = sb("SEb", [128, 1024], BF16)
    SEf = sb("SEf", [128, 1024], F32)
    SI = sb("SI", [128, 1024], I32)
    msk = sb("msk", [128, 4 * 512], BF16)
    idn = sb("idn", [128, 128], BF16)
    i2 = sb("i2", [2, 128], BF16)
    o2 = sb("o2", [2, 128], BF16)
    on1 = sb("on1", [1, 128], BF16)
    on1f = sb("on1f", [1, 128], F32)
    bgm = sb("bgm", [128, 192], BF16)
    selb = sb("selb", [128, 4096], BF16)
    prm = sb("prm", [128, 16], F32)
    prw = sb("prw", [128, 16], F32)
    bfc = sb("bfc", [128, 8], F32)
    pvs = sb("pvs", [128, 2 * 65], BF16)
    vst = sb("vst", [128, 2 * 128], BF16)
    wst = sb("wst", [128, 2 * 1024], BF16)
    smf = sb("smf", [128, 512], F32)
    smb = sb("smb", [128, 512], BF16)
    pay = sb("pay", [128, 132], BF16)
    cva = sb("cva", [128, 4], F32)
    dbs = sb("dbs", [128, 8], F32)

    PS_BIG = pst("PS_BIG", [128, 2048], F32)
    PS_A = pst("PS_A", [128, 512], F32)
    PS_B = pst("PS_B", [128, 512], F32)
    PS_T = pst("PS_T", [128, 1024], BF16)
    PS_C = pst("PS_C", [128, 512], F32)

    sem = {n: st.enter_context(nc.semaphore(name="sem_" + n)) for n in
           ["in", "in2", "v", "s", "t", "g", "bar"]}
    cnt = {n: 0 for n in sem}
    EM = {k: [] for k in ["sync", "vector", "scalar", "tensor", "gpsimd"]}
    ES = {"sync": "in", "vector": "v", "scalar": "s", "tensor": "t", "gpsimd": "g"}

    def em(eng, fn):
        EM[eng].append(fn)

    def dma(eng, out, in_, sname=None):
        sname = sname or ES[eng]
        cnt[sname] += 16
        em(eng, lambda e, o=out, i=in_, s=sem[sname]: e.dma_start(out=o, in_=i).then_inc(s, 16))
        return cnt[sname]

    def wt(eng, sname, val):
        if val > 0:
            em(eng, lambda e, s=sem[sname], v=val: e.wait_ge(s, v))

    def op(eng, fn):
        sname = ES[eng]
        cnt[sname] += 1
        em(eng, lambda e, f=fn, s=sem[sname]: f(e).then_inc(s, 1))
        return cnt[sname]

    def opn(eng, fn):
        em(eng, fn)

    def dv():
        em("vector", lambda e: e.drain())

    def now(s):
        return cnt[s]

    def guard_dma(eng):
        for s in ("v", "s", "t", "g"):
            wt(eng, s, cnt[s])

    def sw_dma(fn_of_p):
        cnt["g"] += 16
        def run(e, f=fn_of_p):
            for arm in e.switch_core_id(n=128):
                f(e, (int(arm) % 128) % 2)
        em("gpsimd", run)
        return cnt["g"]

    def barrier():
        cnt["bar"] += 1
        em("gpsimd", lambda e, s=sem["bar"]: e.collective_compute(
            "AllGather", ALU.bypass, replica_groups=[[0, 1], [2, 3], [4, 5], [6, 7]],
            ins=[bar_i], outs=[bar_o]).then_inc(s, 1))
        return cnt["bar"]

    for r in range(4):
        dma("sync", msk[:, r * 512:(r + 1) * 512], masks_i[r])
    dma("sync", idn[:, :], ident_i)
    dma("sync", i2[:, :], ind2_i)
    dma("sync", o2[:, :], ones2_i)
    dma("sync", on1[:, :], ones1_i)
    dma("sync", on1f[:, :], ones1f_i)
    dma("sync", bgm[:, :], bigm_i)
    dma("sync", selb[:, :], selb_i)
    dma("sync", prm[:, :], tprm)
    dma("sync", cva[:, :], cvals_i)
    nc.const_aps.register(F32, 1e-5, cva[:, 0:1])
    nc.const_aps.register(F32, BIAS_COS, cva[:, 1:2])
    t_c = dma("sync", bfc[:, :], bffi)
    for tt in range(NT):
        dma("sync", x_res[:, tt * D:(tt + 1) * D],
            x_in.rearrange("(n p) d -> n p d", p=128)[tt])
    t_x = now("in")

    opn("vector", lambda e: e.memset(pvs[:, :], 0.0))
    dv()
    opn("vector", lambda e: e.memset(pvs[:, 64:65], 1.0))
    dv()
    op("vector", lambda e: e.memset(pvs[:, 129:130], 1.0))

    wt("scalar", "in", t_c)
    wt("vector", "in", t_c)
    for dst, s_w, s_b in [(0, 0, 4), (8, 8, 12)]:
        t1 = op("scalar", lambda e, i=prm[:, s_w:s_w + 4], o=SC[:, 0:4]:
                e.activation(out=o, in_=i, func=AF.Abs, bias=1.0))
        wt("vector", "s", t1)
        opn("vector", lambda e, i=SC[:, 0:4], o=SC[:, 4:8]: e.reciprocal(out=o, in_=i))
        dv()
        opn("vector", lambda e, i=SC[:, 4:8], o=prw[:, dst:dst + 4]:
            e.tensor_scalar(out=o, in0=i, scalar1=C_LUT, scalar2=None, op0=ALU.mult))
        op("vector", lambda e, i=prm[:, s_b:s_b + 4], o=prw[:, dst + 4:dst + 8]:
           e.tensor_scalar(out=o, in0=i, scalar1=C_LUT, scalar2=None, op0=ALU.mult))

    SRC_F32 = {"vw_a": vw_a, "wo_a": wo_a, "wq_b": wq_b, "wk_b": wk_b,
               "wv_b": wv_b, "wo_b": wo_b}
    for name, src in SRC_F32.items():
        ncol = WCOLS[name]
        for ch in range(8):
            guard_dma("sync")
            t1 = dma("sync", SC[:, 0:ncol], src[ch])
            wt("vector", "in", t1)
            t2 = op("vector", lambda e, i=SC[:, 0:ncol], o=wst[:, 0:ncol]:
                    e.tensor_copy(out=o, in_=i))
            wt("gpsimd", "v", t2)
            t3 = dma("gpsimd", w_bf[name][ch], wst[:, 0:ncol])
            wt("sync", "g", t3)
            wt("vector", "g", t3)

    def emit_quant(src, dst, n):
        nb = n // 32
        amx = SEf[:, 0:nb]
        opn("vector", lambda e, i=src.rearrange("p (a b) -> p a b", b=32), o=amx:
            e.reduce_max(out=o, in_=i, axis=AX.X, apply_absolute_value=True))
        scl = SEf[:, nb:2 * nb]
        rsc = SEf[:, 2 * nb:3 * nb]
        dv()
        opn("vector", lambda e, i=amx, o=scl:
            e.tensor_scalar(out=o, in0=i, scalar1=float(np.float32(1.0) / np.float32(7.5)),
                            scalar2=1e-30, op0=ALU.mult, op1=ALU.max))
        dv()
        opn("vector", lambda e, i=scl, o=rsc: e.reciprocal(out=o, in_=i))
        dv()
        yv = SC[:, 1024:1024 + n]
        tv = op("vector", lambda e, x=src.rearrange("p (a b) -> p a b", b=32),
                r=rsc.unsqueeze(2).broadcast_to((128, nb, 32)),
                o=yv.rearrange("p (a b) -> p a b", b=32):
                e.tensor_tensor(out=o, in0=x, in1=r, op=ALU.mult))
        ay = SC[:, 2048:2048 + n]
        wt("scalar", "v", tv)
        ts = op("scalar", lambda e, i=yv, o=ay: e.activation(out=o, in_=i, func=AF.Abs))
        wt("vector", "s", ts)
        f1 = SD[:, 6144:6144 + n]
        f2 = SD[:, 7168:7168 + n]
        opn("vector", lambda e, i=ay, o=f1:
            e.tensor_scalar(out=o, in0=i, scalar1=1.9375, scalar2=None, op0=ALU.is_gt))
        opn("vector", lambda e, i=ay, o=f2:
            e.tensor_scalar(out=o, in0=i, scalar1=3.875, scalar2=None, op0=ALU.is_gt))
        t3v = ay
        dv()
        opn("vector", lambda e, i=f1, o=t3v:
            e.tensor_scalar(out=o, in0=i, scalar1=-4.0, scalar2=8.0, op0=ALU.mult, op1=ALU.add))
        inv = ay
        dv()
        opn("vector", lambda e, a=f2, b=t3v, o=inv:
            e.scalar_tensor_tensor(out=o, in0=a, scalar=-2.0, in1=b, op0=ALU.mult, op1=ALU.add))
        opn("vector", lambda e, a=yv, b=inv, o=yv: e.tensor_tensor(out=o, in0=a, in1=b, op=ALU.mult))
        opn("vector", lambda e, i=yv, o=SI[:, 0:n]: e.tensor_copy(out=o, in_=i))
        opn("vector", lambda e, i=SI[:, 0:n], o=yv: e.tensor_copy(out=o, in_=i))
        dv()
        opn("vector", lambda e, i=f1, o=f1:
            e.tensor_scalar(out=o, in0=i, scalar1=0.125, scalar2=0.125, op0=ALU.mult, op1=ALU.add))
        stp = f2
        dv()
        opn("vector", lambda e, a=f2, b=f1, o=stp:
            e.scalar_tensor_tensor(out=o, in0=a, scalar=0.25, in1=b, op0=ALU.mult, op1=ALU.add))
        dv()
        opn("vector", lambda e, a=yv, b=stp, o=yv: e.tensor_tensor(out=o, in0=a, in1=b, op=ALU.mult))
        dv()
        return op("vector", lambda e, a=yv.rearrange("p (a b) -> p a b", b=32),
                  s_=scl.unsqueeze(2).broadcast_to((128, nb, 32)),
                  o=dst.rearrange("p (a b) -> p a b", b=32):
                  e.tensor_tensor(out=o, in0=a, in1=s_, op=ALU.mult))

    for ch in range(8):
        guard_dma("sync")
        t1 = dma("sync", SC[:, 0:D], wff[ch])
        wt("vector", "in", t1)
        emit_quant(SC[:, 0:D], SA[:, ch * 1024:(ch + 1) * 1024], D)
    wt("tensor", "v", now("v"))
    wt("tensor", "in", t_c)
    for fi in range(8):
        wt("tensor", "g", now("g"))
        for fo in range(8):
            op("tensor", lambda e, i=SA[:, fo * 1024 + fi * 128:fo * 1024 + (fi + 1) * 128],
               o=PS_T[:, fo * 128:(fo + 1) * 128]:
               e.transpose(out=o, in_=i, identity=idn[:, :]))
        wt("vector", "t", now("t"))
        tv_ = op("vector", lambda e, i=PS_T[:, :], o=wst[:, 0:1024]: e.tensor_copy(out=o, in_=i))
        wt("gpsimd", "v", tv_)
        dma("gpsimd", w_bf["wffT"][fi], wst[:, 0:1024])

    wt("vector", "in", t_x)

    def ln_tile(tt, out_ap, dbg=False):
        xt = x_res[:, tt * D:(tt + 1) * D]
        opn("vector", lambda e, i=xt, o=SEf[:, 0:1]: e.reduce_sum(out=o, in_=i, axis=AX.X))
        dv()
        opn("vector", lambda e, i=SEf[:, 0:1], o=SEf[:, 1:2]:
            e.tensor_scalar(out=o, in0=i, scalar1=1.0 / 1024.0, scalar2=None, op0=ALU.mult))
        dv()
        xm = SC[:, 0:D]
        tv = op("vector", lambda e, i=xt, m=SEf[:, 1:2], o=xm:
                e.tensor_scalar(out=o, in0=i, scalar1=m, scalar2=None, op0=ALU.subtract))
        wt("scalar", "v", tv)
        tsq = op("scalar", lambda e, i=xm, o=SC[:, 1024:2048]:
                 e.activation(out=o, in_=i, func=AF.Square))
        wt("vector", "s", tsq)
        tss = op("vector", lambda e, i=SC[:, 1024:2048], o=SEf[:, 2:3]:
                 e.reduce_sum(out=o, in_=i, axis=AX.X))
        wt("scalar", "v", tss)
        t2 = op("scalar", lambda e, i=SEf[:, 2:3], o=SEf[:, 3:4]:
                e.activation(out=o, in_=i, func=AF.Sqrt, scale=1.0 / 1024.0, bias=1e-5))
        wt("vector", "s", t2)
        opn("vector", lambda e, i=SEf[:, 3:4], o=SEf[:, 4:5]: e.reciprocal(out=o, in_=i))
        dv()
        if dbg:
            opn("vector", lambda e: e.tensor_copy(out=dbs[:, 0:5], in_=SEf[:, 0:5]))
            opn("vector", lambda e: e.tensor_copy(out=dbs[:, 5:6], in_=xm[:, 0:1]))
        rr = op("vector", lambda e, i=xm, r=SEf[:, 4:5], o=out_ap:
                e.tensor_scalar(out=o, in0=i, scalar1=r, scalar2=None, op0=ALU.mult))
        if dbg:
            opn("vector", lambda e, o=out_ap: e.tensor_copy(out=dbs[:, 6:7], in_=o[:, 0:1]))
        return rr

    def emit_h_production(dbg=False):
        for tt in range(NT):
            wt("vector", "t", now("t"))
            tln = ln_tile(tt, SD[:, 0:D], dbg=(dbg and tt == 0))
            if dbg and tt == 0:
                wt("gpsimd", "v", tln)
                dma("gpsimd", dbg_ln, dbs[:, 0:8])
                dma("gpsimd", dbg_hd, SD[:, 0:512])
            wt("tensor", "v", tln)
            for ch in range(8):
                op("tensor", lambda e, i=SD[:, ch * 128:(ch + 1) * 128],
                   o=PS_T[:, ch * 128:(ch + 1) * 128]:
                   e.transpose(out=o, in_=i, identity=idn[:, :]))
            wt("vector", "t", now("t"))
            for ch in range(8):
                op("vector", lambda e, i=PS_T[:, ch * 128:(ch + 1) * 128],
                   o=SA[:, ch * 2048 + tt * 128:ch * 2048 + (tt + 1) * 128]:
                   e.tensor_copy(out=o, in_=i))
        wt("gpsimd", "v", now("v"))
        h_own = SA[:, 0:16384].rearrange("p (c t) -> p c t", c=8)
        sw_dma(lambda e, p: e.dma_start(out=h_sh[p].rearrange("c p t -> p c t"),
                                        in_=h_own).then_inc(sem["g"], 16))
        sw_dma(lambda e, p: e.dma_start(
            out=h_full[:, :, p * SH:(p + 1) * SH].rearrange("c p t -> p c t"),
            in_=h_own).then_inc(sem["g"], 16))
        wt("gpsimd", "g", now("g"))
        bv = barrier()
        wt("gpsimd", "bar", bv)
        sw_dma(lambda e, p: e.dma_start(out=h_full[0:4, :, (1 - p) * SH:(2 - p) * SH],
                                        in_=h_sh[1 - p][4:8]).then_inc(sem["g"], 16))
        tg = sw_dma(lambda e, p: e.dma_start(out=h_full[4:8, :, (1 - p) * SH:(2 - p) * SH],
                                             in_=h_sh[1 - p][0:4]).then_inc(sem["g"], 16))
        return tg

    def emit_vproj(wname, dst, t_hready):
        for ch in range(8):
            guard_dma("sync")
            dma("sync", SD[:, ch * 512:(ch + 1) * 512], w_bf[wname][ch], sname="in2")
        wt("sync", "g", t_hready)
        for tt in range(NTF):
            buf = tt % 2
            guard_dma("sync")
            th = dma("sync", wst[:, buf * 1024:(buf + 1) * 1024].rearrange(
                "p (c q) -> p c q", c=8),
                h_full[:, :, tt * 128:(tt + 1) * 128].rearrange("c p q -> p c q"),
                sname="in2")
            wt("tensor", "in2", th)
            for ch in range(8):
                op("tensor", lambda e, i=wst[:, buf * 1024 + ch * 128:buf * 1024 + (ch + 1) * 128],
                   r=SD[:, ch * 512:(ch + 1) * 512], o=PS_C[:, :], s0=(ch == 0), s1=(ch == 7):
                   e.matmul(out=o, lhsT=i, rhs=r, start=s0, stop=s1))
            wt("scalar", "t", now("t"))
            te = op("scalar", lambda e, i=PS_C[:, :], o=smb[:, :]:
                    e.activation(out=o, in_=i, func=AF.Identity))
            wt("gpsimd", "s", te)
            tg = dma("gpsimd", dst[tt], smb[:, :])
            wt("scalar", "g", tg)
            wt("tensor", "g", tg)
        return now("g")

    def emit_kqproj_b(t_hready):
        for ch in range(8):
            guard_dma("sync")
            dma("sync", SD[:, ch * 512:(ch + 1) * 512], w_bf["wk_b"][ch], sname="in2")
            dma("sync", SD[:, 4096 + ch * 512:4096 + (ch + 1) * 512], w_bf["wv_b"][ch] if False else w_bf["wq_b"][ch], sname="in2")
        wt("sync", "g", t_hready)
        for sl in range(8):
            for pr in range(4):
                for kq in range(2):
                    base = 0 if kq == 0 else 4096
                    for ch in range(8):
                        buf = ch % 2
                        guard_dma("sync")
                        th = dma("sync", wst[:, buf * 1024:buf * 1024 + 512],
                                 h_full[ch, :, sl * 512:(sl + 1) * 512], sname="in2")
                        wt("tensor", "in2", th)
                        op("tensor", lambda e,
                           i=SD[:, base + ch * 512 + pr * 128:base + ch * 512 + (pr + 1) * 128],
                           r=wst[:, buf * 1024:buf * 1024 + 512], o=PS_C[:, :],
                           s0=(ch == 0), s1=(ch == 7):
                           e.matmul(out=o, lhsT=i, rhs=r, start=s0, stop=s1))
                    wt("scalar", "t", now("t"))
                    te = op("scalar", lambda e, i=PS_C[:, :], o=smb[:, :],
                            sc=(1.0 if kq == 0 else 0.125):
                            e.activation(out=o, in_=i, func=AF.Identity, scale=sc))
                    wt("gpsimd", "s", te)
                    dd = kb_d if kq == 0 else qb_d
                    tg = dma("gpsimd", dd[pr][:, sl * 512:(sl + 1) * 512], smb[:, :])
                    wt("scalar", "g", tg)
                    wt("tensor", "g", tg)
        return now("g")

    def emit_trig_a(t_hready):
        wt("sync", "g", t_hready)
        n = 1024
        for ch in range(4):
            for blk in range(4):
                guard_dma("sync")
                th = dma("sync", SD[:, 0:n], h_full[ch, :, blk * n:(blk + 1) * n], sname="in2")
                for side in range(2):
                    iw, ib = (0, 4) if side == 0 else (8, 12)
                    wt("scalar", "in2", th)
                    t1 = op("scalar", lambda e, i=SD[:, 0:n], o=SC[:, 0:n],
                            w=prw[:, iw + ch:iw + ch + 1], b=prw[:, ib + ch:ib + ch + 1]:
                            e.activation(out=o, in_=i, func=AF.Identity, scale=w, bias=b))
                    wt("vector", "s", t1)
                    opn("vector", lambda e, i=SC[:, 0:n], o=SI[:, 0:n]: e.tensor_copy(out=o, in_=i))
                    dv()
                    kf = SC[:, 1024:1024 + n]
                    opn("vector", lambda e, i=SI[:, 0:n], o=kf: e.tensor_copy(out=o, in_=i))
                    dv()
                    gp = SC[:, 2048:2048 + n]
                    opn("vector", lambda e, i=kf, o=gp:
                        e.tensor_scalar(out=o, in0=i, scalar1=2048.0, scalar2=None, op0=ALU.is_gt))
                    dv()
                    opn("vector", lambda e, a=gp, b=kf, o=kf:
                        e.scalar_tensor_tensor(out=o, in0=a, scalar=-4096.0, in1=b,
                                               op0=ALU.mult, op1=ALU.add))
                    dv()
                    opn("vector", lambda e, i=kf, o=gp:
                        e.tensor_scalar(out=o, in0=i, scalar1=-2048.0, scalar2=None, op0=ALU.is_lt))
                    dv()
                    opn("vector", lambda e, a=gp, b=kf, o=kf:
                        e.scalar_tensor_tensor(out=o, in0=a, scalar=4096.0, in1=b,
                                               op0=ALU.mult, op1=ALU.add))
                    dv()
                    k2 = SC[:, 0:n]
                    opn("vector", lambda e, i=kf, o=gp:
                        e.tensor_scalar(out=o, in0=i, scalar1=1024.0, scalar2=None, op0=ALU.is_gt))
                    dv()
                    tv = op("vector", lambda e, a=gp, b=kf, o=k2:
                            e.scalar_tensor_tensor(out=o, in0=a, scalar=-4096.0, in1=b,
                                                   op0=ALU.mult, op1=ALU.add))
                    wt("scalar", "v", tv)
                    sn = SD[:, 1024:1024 + n]
                    cs = SD[:, 2048:2048 + n]
                    op("scalar", lambda e, i=kf, o=sn:
                       e.activation(out=o, in_=i, func=AF.Sin, scale=DELTA))
                    t3 = op("scalar", lambda e, i=k2, o=cs:
                            e.activation(out=o, in_=i, func=AF.Sin, scale=DELTA, bias=BIAS_COS))
                    wt("gpsimd", "s", t3)
                    dd = qa_d if side == 0 else ka_d
                    dma("gpsimd", dd[2 * ch][0:64, blk * n:(blk + 1) * n], cs[0:64, :])
                    dma("gpsimd", dd[2 * ch][64:128, blk * n:(blk + 1) * n], sn[0:64, :])
                    dma("gpsimd", dd[2 * ch + 1][0:64, blk * n:(blk + 1) * n], cs[64:128, :])
                    tg = dma("gpsimd", dd[2 * ch + 1][64:128, blk * n:(blk + 1) * n], sn[64:128, :])
                    wt("scalar", "g", tg)
                    wt("vector", "g", tg)
        return now("g")

    def emit_o_exchange():
        wt("gpsimd", "v", now("v"))
        if n_sub in (-3, 3):
            dma("gpsimd", dbg_om, OM[:, 0:512])
        wt("gpsimd", "g", now("g"))
        om4 = OM.rearrange("p (c t) -> p c t", c=4)
        sw_dma(lambda e, p: e.dma_start(
            out=o_sh[4 * p:4 * (p + 1)].rearrange("c p t -> p c t"),
            in_=om4).then_inc(sem["g"], 16))
        bv = barrier()
        wt("gpsimd", "bar", bv)
        if n_sub in (-3, 3):
            dma("gpsimd", dbg_oa, o_sh[0][:, 0:512])
        def _oread(e, p):
            for c in range(4):
                e.dma_start(out=SA[:, c * 2048:(c + 1) * 2048],
                            in_=o_sh[4 * p + c][:, p * SH:(p + 1) * SH]).then_inc(sem["g"], 16)
                e.dma_start(out=SA[:, 8192 + c * 2048:8192 + (c + 1) * 2048],
                            in_=o_sh[4 * (1 - p) + c][:, p * SH:(p + 1) * SH]).then_inc(sem["g"], 16)
        cnt["g"] += 16 * 8
        def _run_oread(e):
            for arm in e.switch_core_id(n=128):
                _oread(e, (int(arm) % 128) % 2)
        em("gpsimd", _run_oread)
        tg = cnt["g"]
        if n_sub in (-3, 3):
            wt("gpsimd", "g", tg)
            dma("gpsimd", dbg_ob[:, 0:512], SA[:, 0:512])
            dma("gpsimd", dbg_ob[:, 512:1024], SA[:, 8192:8704])
        return tg

    def emit_gemm_residual(wname, rhs_fn, bias, t_rhs_ready):
        wt("tensor", "g", t_rhs_ready)
        for do in range(8):
            guard_dma("sync")
            tw = 0
            for ch in range(8):
                tw = dma("sync", wst[:, ch * 128:(ch + 1) * 128],
                         w_bf[wname][ch][:, do * 128:(do + 1) * 128], sname="in2")
            wt("tensor", "in2", tw)
            for ts4 in range(4):
                wt("tensor", "s", now("s"))
                for ch in range(8):
                    op("tensor", lambda e, i=wst[:, ch * 128:(ch + 1) * 128],
                       r=rhs_fn(ch)[:, ts4 * 512:(ts4 + 1) * 512], o=PS_C[:, :],
                       s0=(ch == 0), s1=(ch == 7):
                       e.matmul(out=o, lhsT=i, rhs=r, start=s0, stop=s1))
                wt("scalar", "t", now("t"))
                if bias is not None:
                    te = op("scalar", lambda e, i=PS_C[:, :], o=smb[:, :], b=bias[:, do:do + 1]:
                            e.activation(out=o, in_=i, func=AF.Identity, bias=b))
                else:
                    te = op("scalar", lambda e, i=PS_C[:, :], o=smb[:, :]:
                            e.activation(out=o, in_=i, func=AF.Identity))
                wt("tensor", "s", te)
                wt("tensor", "v", now("v"))
                for q in range(4):
                    op("tensor", lambda e, i=smb[:, q * 128:(q + 1) * 128],
                       o=PS_T[:, q * 128:(q + 1) * 128]:
                       e.transpose(out=o, in_=i, identity=idn[:, :]))
                wt("vector", "t", now("t"))
                for q in range(4):
                    tt_ = ts4 * 4 + q
                    op("vector", lambda e,
                       a=x_res[:, tt_ * D + do * 128:tt_ * D + (do + 1) * 128],
                       b=PS_T[:, q * 128:(q + 1) * 128],
                       o=x_res[:, tt_ * D + do * 128:tt_ * D + (do + 1) * 128]:
                       e.tensor_tensor(out=o, in0=a, in1=b, op=ALU.add))

    def emit_attn_a():
        t_h = emit_h_production(dbg=True)
        t_v = emit_vproj("vw_a", va_d, t_h)
        t_t = emit_trig_a(t_h)
        for h in range(8):
            guard_dma("sync")
            dma("sync", SD[:, 0:4096], ka_d[h], sname="in2")
            tq = dma("sync", SD[:, 4096:8192], qa_d[h], sname="in2")
            wt("tensor", "in2", tq)
            for qc in range(8):
                nkt = 4 * qc + 4
                for g in range((nkt + 1) // 2):
                    half = g % 2
                    base = half * 1024
                    kts = [2 * g + j for j in range(2) if 2 * g + j < nkt]
                    wt("tensor", "v", now("v"))
                    wt("tensor", "s", now("s"))
                    for j, kt in enumerate(kts):
                        op("tensor", lambda e, i=SD[:, kt * 128:(kt + 1) * 128],
                           r=SD[:, 4096 + qc * 512:4096 + (qc + 1) * 512],
                           o=PS_BIG[:, base + j * 512:base + (j + 1) * 512]:
                           e.matmul(out=o, lhsT=i, rhs=r, start=True, stop=True))
                    wt("scalar", "t", now("t"))
                    ts = op("scalar", lambda e, i=PS_BIG[:, base:base + len(kts) * 512],
                            o=SA[:, kts[0] * 512:(kts[0] + len(kts)) * 512]:
                            e.activation(out=o, in_=i, func=AF.Exp, scale=SCALE_A))
                    wt("vector", "s", ts)
                    for kt in kts:
                        if kt >= 4 * qc:
                            r = kt - 4 * qc
                            op("vector", lambda e, a=SA[:, kt * 512:(kt + 1) * 512],
                               m=msk[:, r * 512:(r + 1) * 512],
                               o=SA[:, kt * 512:(kt + 1) * 512]:
                               e.tensor_tensor(out=o, in0=a, in1=m, op=ALU.mult))
                if h == 0 and qc == 0:
                    wt("gpsimd", "v", now("v"))
                    wt("gpsimd", "s", now("s"))
                    dma("gpsimd", dbg_p, SA[:, 0:2048])
                    dma("gpsimd", dbg_q, SD[:, 4096:4608])
                    dma("gpsimd", dbg_k, SD[:, 0:512])
                    dma("gpsimd", dbg_h, h_full[0][:, 0:512])
                    dma("gpsimd", dbg_h2, h_full[0][:, 2048:2560])
                wt("tensor", "v", now("v"))
                wt("tensor", "s", now("s"))
                for kt in range(nkt):
                    buf = kt % 2
                    guard_dma("sync")
                    tvd = dma("sync", pvs[:, buf * 65:buf * 65 + 64],
                              va_d[kt][:, h * 64:(h + 1) * 64], sname="in2")
                    wt("tensor", "in2", tvd)
                    op("tensor", lambda e, i=pvs[:, buf * 65:(buf + 1) * 65],
                       r=SA[:, kt * 512:(kt + 1) * 512], o=PS_A[0:65, :],
                       s0=(kt == 0), s1=(kt == nkt - 1):
                       e.matmul(out=o, lhsT=i, rhs=r, start=s0, stop=s1))
                wt("vector", "t", now("t"))
                if h == 0 and qc == 0:
                    tvx = op("vector", lambda e, i=PS_A[:, :], o=smf[:, :]:
                             e.tensor_copy(out=o, in_=i))
                    wt("gpsimd", "v", tvx)
                    dma("gpsimd", dbg_l, smf[:, :])
                opn("vector", lambda e, i=PS_A[64:65, :], o=smf[0:1, :]:
                    e.reciprocal(out=o, in_=i))
                dv()
                tvr = op("vector", lambda e, i=PS_A[0:64, :], o=smb[0:64, :]:
                         e.tensor_copy(out=o, in_=i))
                wt("tensor", "v", tvr)
                tb = op("tensor", lambda e, i=on1f[:, 0:64], r=smf[0:1, :], o=PS_B[0:64, :]:
                        e.matmul(out=o, lhsT=i, rhs=r, start=True, stop=True))
                wt("vector", "t", tb)
                if h % 2 == 0:
                    dst = OM[0:64, (h // 2) * S + qc * 512:(h // 2) * S + (qc + 1) * 512]
                    op("vector", lambda e, a=smb[0:64, :], b=PS_B[0:64, :], o=dst:
                       e.tensor_tensor(out=o, in0=a, in1=b, op=ALU.mult))
                else:
                    tvo = op("vector", lambda e, a=smb[0:64, :], b=PS_B[0:64, :], o=smf[0:64, :]:
                             e.tensor_tensor(out=o, in0=a, in1=b, op=ALU.mult))
                    tvo2 = op("vector", lambda e, i=smf[0:64, :], o=smb[0:64, :]:
                              e.tensor_copy(out=o, in_=i))
                    wt("gpsimd", "v", tvo2)
                    tg = dma("gpsimd", OM[64:128, (h // 2) * S + qc * 512:
                                          (h // 2) * S + (qc + 1) * 512], smb[0:64, :])
                    wt("vector", "g", tg)
                wt("tensor", "v", now("v"))
        t_o = emit_o_exchange()
        emit_gemm_residual("wo_a", lambda ch: SA[:, ch * 2048:(ch + 1) * 2048], None, t_o)

    def emit_ffn():
        for tt in range(NT):
            wt("vector", "t", now("t"))
            wt("vector", "s", now("s"))
            tln = ln_tile(tt, SC[:, 0:D])
            emit_quant(SC[:, 0:D], SD[:, 0:D], D)
            wt("tensor", "v", now("v"))
            for ch in range(8):
                op("tensor", lambda e, i=SD[:, ch * 128:(ch + 1) * 128],
                   o=PS_T[:, ch * 128:(ch + 1) * 128]:
                   e.transpose(out=o, in_=i, identity=idn[:, :]))
            wt("vector", "t", now("t"))
            for ch in range(8):
                op("vector", lambda e, i=PS_T[:, ch * 128:(ch + 1) * 128],
                   o=SA[:, ch * 2048 + tt * 128:ch * 2048 + (tt + 1) * 128]:
                   e.tensor_copy(out=o, in_=i))
        emit_gemm_residual("wffT", lambda ch: SA[:, ch * 2048:(ch + 1) * 2048], bfc, now("g"))

    def emit_attn_b():
        t_h = emit_h_production()
        emit_vproj("wv_b", vb_d, t_h)
        t_kq = emit_kqproj_b(t_h)
        for pr in range(4):
            guard_dma("sync")
            dma("sync", SD[:, 0:4096], kb_d[pr], sname="in2")
            tq = dma("sync", SD[:, 4096:8192], qb_d[pr], sname="in2")
            wt("tensor", "in2", tq)
            for h01 in range(2):
                hr = lambda apx, h01=h01: apx[h01 * 64:(h01 + 1) * 64, :]
                for qc in range(8):
                    q0 = 4096 + qc * 512
                    if bstage < 1:
                        continue
                    if bstage == 51 and not (pr == 0 and h01 == 0 and qc == 0):
                        pass
                    for qt2 in range(4):
                        qt = 4 * qc + qt2
                        for hf in range(2):
                            wt("tensor", "v", now("v"))
                            wt("tensor", "s", now("s"))
                            for ksl in range(4):
                                op("tensor", lambda e,
                                   i=SD[:, 4096 + qt * 128:4096 + (qt + 1) * 128][h01 * 64:(h01 + 1) * 64, :],
                                   r=SD[:, hf * 2048 + ksl * 512:hf * 2048 + (ksl + 1) * 512][h01 * 64:(h01 + 1) * 64, :],
                                   o=PS_BIG[:, ksl * 512:(ksl + 1) * 512]:
                                   e.matmul(out=o, lhsT=i, rhs=r, start=True, stop=True))
                            wt("vector", "t", now("t"))
                            op("vector", lambda e,
                               i=PS_BIG.rearrange("p (a b) -> p a b", b=64),
                               o=SEb[:, qt2 * 64 + hf * 32:qt2 * 64 + hf * 32 + 32]:
                               e.reduce_max(out=o, in_=i, axis=AX.X))
                    if bstage < 2:
                        continue
                    wt("tensor", "v", now("v"))
                    for g in range(8):
                        wt("tensor", "s", now("s"))
                        for j in range(4):
                            kt = 4 * g + j
                            op("tensor", lambda e,
                               i=SD[:, kt * 128:(kt + 1) * 128][h01 * 64:(h01 + 1) * 64, :],
                               r=SD[:, q0:q0 + 512][h01 * 64:(h01 + 1) * 64, :],
                               o=PS_BIG[:, j * 512:(j + 1) * 512]:
                               e.matmul(out=o, lhsT=i, rhs=r, start=True, stop=True))
                        wt("scalar", "t", now("t"))
                        tsx = op("scalar", lambda e, i=PS_BIG[:, :], o=SA[:, 0:2048]:
                                 e.activation(out=o, in_=i, func=AF.Exp))
                        wt("tensor", "s", tsx)
                        for j in range(4):
                            kt = 4 * g + j
                            op("tensor", lambda e, i=bgm[:, 126 - 2 * kt:190 - 2 * kt],
                               r=SA[:, j * 512:(j + 1) * 512], o=PS_B[0:64, :],
                               s0=(kt == 0), s1=(kt == 31):
                               e.matmul(out=o, lhsT=i, rhs=r, start=s0, stop=s1))
                    if bstage < 3:
                        continue
                    wt("vector", "t", now("t"))
                    tev = op("vector", lambda e, i=PS_B[0:64, :], o=smb[0:64, :]:
                             e.tensor_copy(out=o, in_=i))
                    wt("tensor", "v", tev)
                    for qt2 in range(4):
                        op("tensor", lambda e, i=smb[0:64, qt2 * 128:(qt2 + 1) * 128],
                           o=PS_T[:, qt2 * 128:qt2 * 128 + 64],
                           iy=idn[0:64, 0:64]:
                           e.transpose(out=o, in_=i, identity=iy))
                    wt("vector", "t", now("t"))
                    for qt2 in range(4):
                        op("vector", lambda e, i=PS_T[:, qt2 * 128:qt2 * 128 + 64],
                           o=SEb[:, 256 + qt2 * 64:256 + (qt2 + 1) * 64]:
                           e.tensor_copy(out=o, in_=i))
                    if bstage < 4 or bstage == 58:
                        continue
                    wt("tensor", "v", now("v"))
                    for qt2 in range(4):
                        mx_ = SEb[:, qt2 * 64:(qt2 + 1) * 64]
                        lam_ = SEb[:, 256 + qt2 * 64:256 + (qt2 + 1) * 64]
                        lcum = SEf[:, 0:64]
                        dv()
                        if bstage == 59:
                            opn("vector", lambda e, i=lam_, o=lcum: e.tensor_copy(out=o, in_=i))
                        else:
                            opn("vector", lambda e, i=lam_, o=lcum:
                                e.tensor_tensor_scan(out=o, data0=i, data1=i, initial=0.0,
                                                     op0=ALU.add, op1=ALU.bypass))
                        mrun = SEf[:, 64:128]
                        dv()
                        if bstage == 59:
                            tv1 = op("vector", lambda e, i=mx_, o=mrun: e.tensor_copy(out=o, in_=i))
                        else:
                            tv1 = op("vector", lambda e, i=mx_, o=mrun:
                                     e.tensor_tensor_scan(out=o, data0=i, data1=i, initial=-1e30,
                                                          op0=ALU.max, op1=ALU.bypass))
                        lnl = SEf[:, 128:192]
                        wt("scalar", "v", tv1)
                        t2 = op("scalar", lambda e, i=lcum, o=lnl:
                                e.activation(out=o, in_=i, func=AF.Ln))
                        wt("vector", "s", t2)
                        d_ = SEf[:, 192:256]
                        dv()
                        opn("vector", lambda e, a=mrun, b=lnl, o=d_:
                            e.tensor_tensor(out=o, in0=a, in1=b, op=ALU.subtract))
                        incl = SEf[:, 256:320]
                        dv()
                        if bstage == 59:
                            opn("vector", lambda e, i=d_, o=incl: e.tensor_copy(out=o, in_=i))
                        else:
                            opn("vector", lambda e, i=d_, o=incl:
                                e.tensor_tensor_scan(out=o, data0=i, data1=i, initial=0.0,
                                                     op0=ALU.add, op1=ALU.bypass))
                        if bstage == 592:
                            continue
                        t_ = SEf[:, 320:384]
                        dv()
                        opn("vector", lambda e, a=incl, b=d_, o=t_:
                            e.tensor_tensor(out=o, in0=a, in1=b, op=ALU.subtract))
                        suf = SEf[:, 384:448]
                        dv()
                        opn("vector", lambda e, a=t_, s_=incl[:, 63:64].broadcast_to((128, 64)), o=suf:
                            e.scalar_tensor_tensor(out=o, in0=a, scalar=-1.0, in1=s_,
                                                   op0=ALU.mult, op1=ALU.add))
                        if bstage == 591:
                            continue
                        dv()
                        opn("vector", lambda e, i=suf, c=d_[:, 63:64], o=pay[:, 0:64]:
                            e.tensor_scalar(out=o, in0=i, scalar1=c, scalar2=None,
                                            op0=ALU.subtract))
                        nrm = SEf[:, 448:449]
                        dv()
                        opn("vector", lambda e, i=lnl[:, 63:64], o=nrm:
                            e.tensor_scalar(out=o, in0=i, scalar1=-1.0, scalar2=None, op0=ALU.mult))
                        dv()
                        opn("vector", lambda e, i=nrm, o=pay[:, 64:65]: e.tensor_copy(out=o, in_=i))
                        dv()
                        tv3 = op("vector", lambda e, a=nrm, b=pay[:, 64:65], o=pay[:, 65:66]:
                                 e.tensor_tensor(out=o, in0=a, in1=b, op=ALU.subtract))
                        wt("tensor", "v", tv3)
                        tt4 = op("tensor", lambda e, i=pay[:, 0:128],
                                 o=PS_T[:, qt2 * 128:(qt2 + 1) * 128]:
                                 e.transpose(out=o, in_=i, identity=idn[:, :]))
                        wt("vector", "t", tt4)
                        op("vector", lambda e, i=PS_T[0:66, qt2 * 128:qt2 * 128 + 128] if False else PS_T[:, qt2 * 128:(qt2 + 1) * 128][0:66, :],
                           o=smb[0:66, qt2 * 128:(qt2 + 1) * 128]:
                           e.tensor_copy(out=o, in_=i))
                    if pr == 0 and h01 == 0 and qc == 0 and n_sub == -3:
                        wt("gpsimd", "v", now("v"))
                        dma("gpsimd", dbg_hd, SEb[:, 0:512])
                        dma("gpsimd", dbg_q, smb[:, 0:512])
                        dma("gpsimd", dbg_k, SD[:, 0:512])
                        dma("gpsimd", dbg_p[:, 0:512], SD[:, 4096:4608])
                    if bstage == 41 and pr == 0 and h01 == 0 and qc == 0:
                        for _ in range(8):
                            opn("tensor", lambda e: e.nop())
                            opn("scalar", lambda e: e.nop())
                    if bstage < 5:
                        continue
                    if bstage in (51, 52, 53, 54, 56, 57, 58, 59, 591, 592) and not (pr == 0 and h01 == 0 and qc == 0):
                        continue
                    for g in range(1 if bstage in (52, 53, 54, 56, 57, 58, 59) else 8):
                        wt("tensor", "s", now("s"))
                        wt("tensor", "v", now("v"))
                        for j in range(4):
                            kt = 4 * g + j
                            if bstage == 54:
                                continue
                            op("tensor", lambda e,
                               i=SD[:, kt * 128:(kt + 1) * 128][h01 * 64:(h01 + 1) * 64, :],
                               r=SD[:, q0:q0 + 512][h01 * 64:(h01 + 1) * 64, :],
                               o=PS_BIG[:, j * 512:(j + 1) * 512]:
                               e.matmul(out=o, lhsT=i, rhs=r, start=True, stop=(bstage < 55)))
                        for j in range(4):
                            kt = 4 * g + j
                            if bstage >= 55:
                                op("tensor", lambda e, i=selb[0:66, kt * 128:(kt + 1) * 128],
                                   r=smb[0:66, 0:512],
                                   o=PS_BIG[:, j * 512:(j + 1) * 512]:
                                   e.matmul(out=o, lhsT=i, rhs=r, start=False, stop=True))

                        wt("scalar", "t", now("t"))
                        if bstage != 53:
                            _fn = AF.Identity if bstage == 56 else AF.Exp
                            _oA = smb[:, 0:512] if bstage == 57 else SA[:, g * 2048:g * 2048 + 1024]
                            if bstage == 57:
                                op("scalar", lambda e, i=PS_BIG[:, 0:512], o=smb[:, 0:512]:
                                   e.activation(out=o, in_=i, func=AF.Exp))
                                tsx = op("scalar", lambda e, i=PS_BIG[:, 512:1024], o=smb[:, 0:512]:
                                         e.activation(out=o, in_=i, func=AF.Exp))
                            else:
                                op("scalar", lambda e, i=PS_BIG[:, 0:1024], o=SA[:, g * 2048:g * 2048 + 1024], f=_fn:
                                   e.activation(out=o, in_=i, func=f))
                                tsx = op("scalar", lambda e, i=PS_BIG[:, 1024:2048],
                                         o=SA[:, g * 2048 + 1024:(g + 1) * 2048], f=_fn:
                                         e.activation(out=o, in_=i, func=f))
                            wt("vector", "s", tsx)
                    if pr == 0 and h01 == 0 and qc == 0 and n_sub == -3:
                        wt("gpsimd", "s", now("s"))
                        dma("gpsimd", dbg_h, SA[:, 15360:15872])
                        dma("gpsimd", dbg_h2, SA[:, 15872:16384])
                    if bstage < 60 and bstage != 601:
                        continue
                    wt("tensor", "s", now("s"))
                    wt("tensor", "v", now("v"))
                    for kt in range(32):
                        buf = kt % 2
                        guard_dma("sync")
                        tvd = dma("sync", vst[:, buf * 128:buf * 128 + 64],
                                  vb_d[kt][:, (pr * 2 + h01) * 64:(pr * 2 + h01 + 1) * 64],
                                  sname="in2")
                        wt("tensor", "in2", tvd)
                        op("tensor", lambda e, i=vst[0:64, buf * 128:buf * 128 + 64],
                           r=SA[:, kt * 512:(kt + 1) * 512][0:64, :], o=PS_A[0:64, :],
                           s0=(kt == 0), s1=(kt == 31):
                           e.matmul(out=o, lhsT=i, rhs=r, start=s0, stop=s1))
                        if bstage != 601:
                            op("tensor", lambda e, i=vst[64:128, buf * 128:buf * 128 + 64],
                               r=SA[:, kt * 512:(kt + 1) * 512][64:128, :], o=PS_B[0:64, :],
                               s0=(kt == 0), s1=(kt == 31), tp=(64, 0):
                               e.matmul(out=o, lhsT=i, rhs=r, start=s0, stop=s1, tile_position=tp))
                    wt("vector", "t", now("t"))
                    op("vector", lambda e, i=PS_A[0:64, :], o=smf[0:64, :]:
                       e.tensor_copy(out=o, in_=i))
                    dv()
                    if h01 == 0:
                        op("vector", lambda e, a=smf[0:64, :], b=PS_B[0:64, :],
                           o=OM[0:64, pr * S + qc * 512:pr * S + (qc + 1) * 512]:
                           e.tensor_tensor(out=o, in0=a, in1=b, op=ALU.add))
                    else:
                        tvo = op("vector", lambda e, a=smf[0:64, :], b=PS_B[0:64, :],
                                 o=smb[64:128, 0:512]:
                                 e.tensor_tensor(out=o, in0=a, in1=b, op=ALU.add))
                        wt("gpsimd", "v", tvo)
                        tg = dma("gpsimd", OM[64:128, pr * S + qc * 512:pr * S + (qc + 1) * 512],
                                 smb[64:128, 0:512])
                        wt("vector", "g", tg)
                    wt("tensor", "v", now("v"))
        t_o = emit_o_exchange()
        emit_gemm_residual("wo_b", lambda ch: SA[:, ch * 2048:(ch + 1) * 2048], None, t_o)

    if n_sub == -3:
        emit_attn_b()
    if n_sub >= 1:
        emit_attn_a()
    if n_sub >= 2:
        emit_ffn()
    if n_sub >= 3:
        emit_attn_b()
    if n_sub >= 4:
        emit_ffn()

    wt("gpsimd", "v", now("v"))
    for tt in range(NT):
        dma("gpsimd", y_out.rearrange("(n p) d -> n p d", p=128)[tt],
            x_res[:, tt * D:(tt + 1) * D])
    wt("gpsimd", "g", now("g"))

    blk = st.enter_context(nc.Block())

    @blk.sync
    def _(e):
        for fn in EM["sync"]:
            fn(e)

    @blk.vector
    def _(e):
        for fn in EM["vector"]:
            fn(e)

    @blk.scalar
    def _(e):
        for fn in EM["scalar"]:
            fn(e)

    @blk.tensor
    def _(e):
        for fn in EM["tensor"]:
            fn(e)

    @blk.gpsimd
    def _(e):
        for fn in EM["gpsimd"]:
            fn(e)

    st.close()
    global _LAST_CNT
    _LAST_CNT = dict(cnt)
    _LAST_CNT["_em"] = {k: len(v) for k, v in EM.items()}
    try:
        _LAST_CNT["_insts"] = len(list(nc.all_instructions()))
    except Exception:
        pass
    return nc


def _host_prep(inputs, core):
    p = core % 2
    b = core // 2
    perm = np.arange(D)
    if p == 1:
        perm = np.concatenate([perm[512:], perm[:512]])
    hsl = slice(p * 8, p * 8 + 8)

    x = inputs["L0_attn_a_x"]
    xc = np.ascontiguousarray(x[b, p * SH:(p + 1) * SH, :][:, perm], np.float32)

    def chunkT(w):
        return np.ascontiguousarray(w.reshape(8, 128, -1), np.float32)

    wq = inputs["L0_attn_a_w_q"][hsl].reshape(512)
    bq = inputs["L0_attn_a_b_q"][hsl].reshape(512)
    wk = inputs["L0_attn_a_w_k"][hsl].reshape(512)
    bk = inputs["L0_attn_a_b_k"][hsl].reshape(512)
    tprm = np.stack([wq.reshape(4, 128), bq.reshape(4, 128),
                     wk.reshape(4, 128), bk.reshape(4, 128)], 0)
    tprm = np.ascontiguousarray(tprm.reshape(16, 128).T, np.float32)

    vw = inputs["L0_attn_a_v_w"][p * 512:(p + 1) * 512, :][:, perm]
    vw_a = chunkT(np.ascontiguousarray(vw.T))
    ow = inputs["L0_attn_a_out_w"]
    wo_a = chunkT(np.ascontiguousarray(ow.T[perm][:, perm]))

    wqb = inputs["L0_attn_b_Wq"][perm][:, p * 512:(p + 1) * 512]
    wkb = inputs["L0_attn_b_Wk"][perm][:, p * 512:(p + 1) * 512]
    wvb = inputs["L0_attn_b_Wv"][perm][:, p * 512:(p + 1) * 512]
    wob = inputs["L0_attn_b_Wo"][perm][:, perm]
    wff = inputs["L0_ffn_weight"][perm][:, perm]
    bff = inputs["L0_ffn_bias"][perm]

    masks = np.zeros((4, 128, 512), ml_dtypes.bfloat16)
    for r in range(4):
        i_ = np.arange(128)[:, None] + 128 * r
        j_ = np.arange(512)[None, :]
        masks[r] = (i_ <= j_).astype(ml_dtypes.bfloat16)
    ident = np.eye(128, dtype=ml_dtypes.bfloat16)
    ind2 = np.zeros((2, 128), ml_dtypes.bfloat16)
    ind2[0, 0:64] = 1
    ind2[1, 64:128] = 1
    ones2 = np.ones((2, 128), ml_dtypes.bfloat16)
    ones1 = np.ones((1, 128), ml_dtypes.bfloat16)
    bigm = np.zeros((128, 192), ml_dtypes.bfloat16)
    bigm[0:64, 126] = 1
    bigm[64:128, 127] = 1
    selb = np.zeros((128, 4096), ml_dtypes.bfloat16)
    for kt in range(32):
        for c in range(128):
            selb[2 * kt + (c >= 64), kt * 128 + c] = 1
        selb[64, kt * 128:(kt + 1) * 128] = 1
        selb[65, kt * 128:(kt + 1) * 128] = 1

    return {
        "x_in": xc, "tprm": tprm,
        "bffi": np.ascontiguousarray(bff.reshape(8, 128).T, np.float32),
        "vw_a": vw_a, "wo_a": wo_a,
        "wq_b": chunkT(wqb), "wk_b": chunkT(wkb), "wv_b": chunkT(wvb),
        "wo_b": chunkT(wob), "wff": chunkT(wff),
        "masks_i": masks, "ident_i": ident, "ind2_i": ind2, "ones2_i": ones2,
        "ones1_i": ones1, "ones1f_i": np.ones((1, 128), np.float32), "bigm_i": bigm,
        "selb_i": selb,
        "cvals_i": np.ascontiguousarray(
            np.broadcast_to(np.array([1e-5, BIAS_COS, 0.0, 0.0], np.float32), (128, 4))),
    }, perm, b, p


def kernel(**inputs):
    key = ("nc", SUBLAYERS, ATTNB_STAGE)
    if key not in _CACHE:
        _CACHE[key] = build_nc(SUBLAYERS, ATTNB_STAGE)
    nc = _CACHE[key]
    in_maps = []
    metas = []
    for c in range(8):
        m, perm, b, p = _host_prep(inputs, c)
        in_maps.append(m)
        metas.append((perm, b, p))
    res = run_bass_kernel_spmd(nc, in_maps, core_ids=list(range(8)))
    out = np.empty((B, S, D), np.float32)
    for c in range(8):
        perm, b, p = metas[c]
        inv = np.argsort(perm)
        out[b, p * SH:(p + 1) * SH, :] = res.results[c]["y_out"][:, inv]
    return out


if __name__ == "__main__":
    nc = build_nc()
    print("built ok")


# version 02
# speedup vs baseline: 1.2980x; 1.2980x over previous
import contextlib
import numpy as np
import ml_dtypes
import concourse.bass as bass
import concourse.mybir as mybir
from concourse.bass_utils import run_bass_kernel_spmd

F32 = mybir.dt.float32
BF16 = mybir.dt.bfloat16
I32 = mybir.dt.int32
AF = mybir.ActivationFunctionType
ALU = mybir.AluOpType
AX = mybir.AxisListType

B, S, D, H, DH = 4, 4096, 1024, 16, 64
SH = S // 2
NT = SH // 128
NTF = S // 128
NKT = S // 128
C_LUT = float(np.float32(4096.0 / (2.0 * np.pi)))
DELTA = float(np.float32(np.float32(2.0 * np.pi) / 4096.0))
BIAS_COS = float(np.float32(1024.0) * np.float32(DELTA))
SCALE_A = float(np.float32(1.0 / np.sqrt(np.float32(128.0))))
SUBLAYERS = 4
ATTNB_STAGE = 60
_LAST_CNT = None
_CACHE = {}


def build_nc(n_sub=4, bstage=None):
    if bstage is None:
        bstage = ATTNB_STAGE
    nc = bass.Bass("TRN2", target_bir_lowering=False)

    ein = lambda n, sh, dt=F32: nc.dram_tensor(n, sh, dt, kind="ExternalInput").ap()
    x_in = ein("x_in", [SH, D])
    tprm = ein("tprm", [128, 16])
    bffi = ein("bffi", [128, 8])
    vw_a = ein("vw_a", [8, 128, 512])
    wo_a = ein("wo_a", [8, 128, 1024])
    wq_b = ein("wq_b", [8, 128, 512])
    wk_b = ein("wk_b", [8, 128, 512])
    wv_b = ein("wv_b", [8, 128, 512])
    wo_b = ein("wo_b", [8, 128, 1024])
    wff = ein("wff", [8, 128, 1024])
    masks_i = ein("masks_i", [4, 128, 512], BF16)
    ident_i = ein("ident_i", [128, 128], BF16)
    ind2_i = ein("ind2_i", [2, 128], BF16)
    ones2_i = ein("ones2_i", [2, 128], BF16)
    ones1_i = ein("ones1_i", [1, 128], BF16)
    ones1f_i = ein("ones1f_i", [1, 128])
    bigm_i = ein("bigm_i", [128, 192], BF16)
    selb_i = ein("selb_i", [128, 4096], BF16)
    cvals_i = ein("cvals_i", [128, 4])
    y_out = nc.dram_tensor("y_out", [SH, D], F32, kind="ExternalOutput").ap()

    din = lambda n, sh, dt=BF16, **kw: nc.dram_tensor(n, sh, dt, kind="Internal", **kw).ap()
    h_sh = din("h_sh", [2, 8, 128, SH], BF16, addr_space="Shared")
    o_sh = din("o_sh", [8, 128, S], BF16, addr_space="Shared")
    h_full = din("h_full", [8, 128, S])
    qa_d = din("qa_d", [8, 128, S])
    ka_d = din("ka_d", [8, 128, S])
    va_d = din("va_d", [NKT, 128, 512])
    qb_d = din("qb_d", [4, 128, S])
    kb_d = din("kb_d", [4, 128, S])
    vb_d = din("vb_d", [NKT, 128, 512])
    WCOLS = {"vw_a": 512, "wo_a": 1024, "wq_b": 512, "wk_b": 512,
             "wv_b": 512, "wo_b": 1024, "wffT": 1024}
    w_bf = {n: din("bf_" + n, [8, 128, c]) for n, c in WCOLS.items()}
    bar_i = din("bar_i", [1, 64], F32)
    bar_o = din("bar_o", [2, 64], F32)

    st = contextlib.ExitStack()
    sb = lambda n, sh, dt: st.enter_context(nc.sbuf_tensor(n, sh, dt))
    pst = lambda n, sh, dt=F32: st.enter_context(nc.psum_tensor(n, sh, dt))

    x_res = sb("x_res", [128, NT * D], F32)
    SA = sb("SA", [128, 16384], BF16)
    OM = sb("OM", [128, 4 * S], BF16)
    SC = sb("SC", [128, 3 * 1024], F32)
    SD = sb("SD", [128, 8192], BF16)
    SEb = sb("SEb", [128, 1024], BF16)
    SEf = sb("SEf", [128, 1024], F32)
    SI = sb("SI", [128, 1024], I32)
    msk = sb("msk", [128, 4 * 512], BF16)
    idn = sb("idn", [128, 128], BF16)
    i2 = sb("i2", [2, 128], BF16)
    o2 = sb("o2", [2, 128], BF16)
    on1 = sb("on1", [1, 128], BF16)
    on1f = sb("on1f", [1, 128], F32)
    bgm = sb("bgm", [128, 192], BF16)
    selb = sb("selb", [128, 4096], BF16)
    prm = sb("prm", [128, 16], F32)
    prw = sb("prw", [128, 16], F32)
    bfc = sb("bfc", [128, 8], F32)
    pvs = sb("pvs", [128, 2 * 65], BF16)
    vst = sb("vst", [128, 2 * 128], BF16)
    wst = sb("wst", [128, 2 * 1024], BF16)
    smf = sb("smf", [128, 512], F32)
    smb = sb("smb", [128, 512], BF16)
    pay = sb("pay", [128, 132], BF16)
    cva = sb("cva", [128, 4], F32)

    PS_BIG = pst("PS_BIG", [128, 2048], F32)
    PS_A = pst("PS_A", [128, 512], F32)
    PS_B = pst("PS_B", [128, 512], F32)
    PS_T = pst("PS_T", [128, 1024], BF16)
    PS_C = pst("PS_C", [128, 512], F32)

    sem = {n: st.enter_context(nc.semaphore(name="sem_" + n)) for n in
           ["in", "in2", "v", "s", "t", "g", "bar"]}
    cnt = {n: 0 for n in sem}
    EM = {k: [] for k in ["sync", "vector", "scalar", "tensor", "gpsimd"]}
    ES = {"sync": "in", "vector": "v", "scalar": "s", "tensor": "t", "gpsimd": "g"}

    def em(eng, fn):
        EM[eng].append(fn)

    def dma(eng, out, in_, sname=None):
        sname = sname or ES[eng]
        cnt[sname] += 16
        em(eng, lambda e, o=out, i=in_, s=sem[sname]: e.dma_start(out=o, in_=i).then_inc(s, 16))
        return cnt[sname]

    def wt(eng, sname, val):
        if val > 0:
            em(eng, lambda e, s=sem[sname], v=val: e.wait_ge(s, v))

    def op(eng, fn):
        sname = ES[eng]
        cnt[sname] += 1
        em(eng, lambda e, f=fn, s=sem[sname]: f(e).then_inc(s, 1))
        return cnt[sname]

    def opn(eng, fn):
        em(eng, fn)

    def dv():
        em("vector", lambda e: e.drain())

    def now(s):
        return cnt[s]

    def guard_dma(eng):
        for s in ("v", "s", "t", "g"):
            wt(eng, s, cnt[s])

    def sw_dma(fn_of_p):
        cnt["g"] += 16
        def run(e, f=fn_of_p):
            for arm in e.switch_core_id(n=128):
                f(e, (int(arm) % 128) % 2)
        em("gpsimd", run)
        return cnt["g"]

    def barrier():
        cnt["bar"] += 1
        em("gpsimd", lambda e, s=sem["bar"]: e.collective_compute(
            "AllGather", ALU.bypass, replica_groups=[[0, 1], [2, 3], [4, 5], [6, 7]],
            ins=[bar_i], outs=[bar_o]).then_inc(s, 1))
        return cnt["bar"]

    for r in range(4):
        dma("sync", msk[:, r * 512:(r + 1) * 512], masks_i[r])
    dma("sync", idn[:, :], ident_i)
    dma("sync", i2[:, :], ind2_i)
    dma("sync", o2[:, :], ones2_i)
    dma("sync", on1[:, :], ones1_i)
    dma("sync", on1f[:, :], ones1f_i)
    dma("sync", bgm[:, :], bigm_i)
    dma("sync", selb[:, :], selb_i)
    dma("sync", prm[:, :], tprm)
    dma("sync", cva[:, :], cvals_i)
    nc.const_aps.register(F32, 1e-5, cva[:, 0:1])
    nc.const_aps.register(F32, BIAS_COS, cva[:, 1:2])
    t_c = dma("sync", bfc[:, :], bffi)
    for tt in range(NT):
        dma("sync", x_res[:, tt * D:(tt + 1) * D],
            x_in.rearrange("(n p) d -> n p d", p=128)[tt])
    t_x = now("in")

    opn("vector", lambda e: e.memset(pvs[:, :], 0.0))
    dv()
    opn("vector", lambda e: e.memset(pvs[:, 64:65], 1.0))
    dv()
    op("vector", lambda e: e.memset(pvs[:, 129:130], 1.0))

    wt("scalar", "in", t_c)
    wt("vector", "in", t_c)
    for dst, s_w, s_b in [(0, 0, 4), (8, 8, 12)]:
        t1 = op("scalar", lambda e, i=prm[:, s_w:s_w + 4], o=SC[:, 0:4]:
                e.activation(out=o, in_=i, func=AF.Abs, bias=1.0))
        wt("vector", "s", t1)
        opn("vector", lambda e, i=SC[:, 0:4], o=SC[:, 4:8]: e.reciprocal(out=o, in_=i))
        dv()
        opn("vector", lambda e, i=SC[:, 4:8], o=prw[:, dst:dst + 4]:
            e.tensor_scalar(out=o, in0=i, scalar1=C_LUT, scalar2=None, op0=ALU.mult))
        op("vector", lambda e, i=prm[:, s_b:s_b + 4], o=prw[:, dst + 4:dst + 8]:
           e.tensor_scalar(out=o, in0=i, scalar1=C_LUT, scalar2=None, op0=ALU.mult))

    SRC_F32 = {"vw_a": vw_a, "wo_a": wo_a, "wq_b": wq_b, "wk_b": wk_b,
               "wv_b": wv_b, "wo_b": wo_b}
    for name, src in SRC_F32.items():
        ncol = WCOLS[name]
        for ch in range(8):
            guard_dma("sync")
            t1 = dma("sync", SC[:, 0:ncol], src[ch])
            wt("vector", "in", t1)
            t2 = op("vector", lambda e, i=SC[:, 0:ncol], o=wst[:, 0:ncol]:
                    e.tensor_copy(out=o, in_=i))
            wt("gpsimd", "v", t2)
            t3 = dma("gpsimd", w_bf[name][ch], wst[:, 0:ncol])
            wt("sync", "g", t3)
            wt("vector", "g", t3)

    def emit_quant(src, dst, n):
        nb = n // 32
        amx = SEf[:, 0:nb]
        opn("vector", lambda e, i=src.rearrange("p (a b) -> p a b", b=32), o=amx:
            e.reduce_max(out=o, in_=i, axis=AX.X, apply_absolute_value=True))
        scl = SEf[:, nb:2 * nb]
        rsc = SEf[:, 2 * nb:3 * nb]
        dv()
        opn("vector", lambda e, i=amx, o=scl:
            e.tensor_scalar(out=o, in0=i, scalar1=float(np.float32(1.0) / np.float32(7.5)),
                            scalar2=1e-30, op0=ALU.mult, op1=ALU.max))
        dv()
        opn("vector", lambda e, i=scl, o=rsc: e.reciprocal(out=o, in_=i))
        dv()
        yv = SC[:, 1024:1024 + n]
        tv = op("vector", lambda e, x=src.rearrange("p (a b) -> p a b", b=32),
                r=rsc.unsqueeze(2).broadcast_to((128, nb, 32)),
                o=yv.rearrange("p (a b) -> p a b", b=32):
                e.tensor_tensor(out=o, in0=x, in1=r, op=ALU.mult))
        ay = SC[:, 2048:2048 + n]
        wt("scalar", "v", tv)
        ts = op("scalar", lambda e, i=yv, o=ay: e.activation(out=o, in_=i, func=AF.Abs))
        wt("vector", "s", ts)
        f1 = SD[:, 6144:6144 + n]
        f2 = SD[:, 7168:7168 + n]
        opn("vector", lambda e, i=ay, o=f1:
            e.tensor_scalar(out=o, in0=i, scalar1=1.9375, scalar2=None, op0=ALU.is_gt))
        opn("vector", lambda e, i=ay, o=f2:
            e.tensor_scalar(out=o, in0=i, scalar1=3.875, scalar2=None, op0=ALU.is_gt))
        t3v = ay
        dv()
        opn("vector", lambda e, i=f1, o=t3v:
            e.tensor_scalar(out=o, in0=i, scalar1=-4.0, scalar2=8.0, op0=ALU.mult, op1=ALU.add))
        inv = ay
        dv()
        opn("vector", lambda e, a=f2, b=t3v, o=inv:
            e.scalar_tensor_tensor(out=o, in0=a, scalar=-2.0, in1=b, op0=ALU.mult, op1=ALU.add))
        opn("vector", lambda e, a=yv, b=inv, o=yv: e.tensor_tensor(out=o, in0=a, in1=b, op=ALU.mult))
        opn("vector", lambda e, i=yv, o=SI[:, 0:n]: e.tensor_copy(out=o, in_=i))
        opn("vector", lambda e, i=SI[:, 0:n], o=yv: e.tensor_copy(out=o, in_=i))
        dv()
        opn("vector", lambda e, i=f1, o=f1:
            e.tensor_scalar(out=o, in0=i, scalar1=0.125, scalar2=0.125, op0=ALU.mult, op1=ALU.add))
        stp = f2
        dv()
        opn("vector", lambda e, a=f2, b=f1, o=stp:
            e.scalar_tensor_tensor(out=o, in0=a, scalar=0.25, in1=b, op0=ALU.mult, op1=ALU.add))
        dv()
        opn("vector", lambda e, a=yv, b=stp, o=yv: e.tensor_tensor(out=o, in0=a, in1=b, op=ALU.mult))
        dv()
        return op("vector", lambda e, a=yv.rearrange("p (a b) -> p a b", b=32),
                  s_=scl.unsqueeze(2).broadcast_to((128, nb, 32)),
                  o=dst.rearrange("p (a b) -> p a b", b=32):
                  e.tensor_tensor(out=o, in0=a, in1=s_, op=ALU.mult))

    for ch in range(8):
        guard_dma("sync")
        t1 = dma("sync", SC[:, 0:D], wff[ch])
        wt("vector", "in", t1)
        emit_quant(SC[:, 0:D], SA[:, ch * 1024:(ch + 1) * 1024], D)
    wt("tensor", "v", now("v"))
    wt("tensor", "in", t_c)
    for fi in range(8):
        wt("tensor", "g", now("g"))
        for fo in range(8):
            op("tensor", lambda e, i=SA[:, fo * 1024 + fi * 128:fo * 1024 + (fi + 1) * 128],
               o=PS_T[:, fo * 128:(fo + 1) * 128]:
               e.transpose(out=o, in_=i, identity=idn[:, :]))
        wt("vector", "t", now("t"))
        tv_ = op("vector", lambda e, i=PS_T[:, :], o=wst[:, 0:1024]: e.tensor_copy(out=o, in_=i))
        wt("gpsimd", "v", tv_)
        dma("gpsimd", w_bf["wffT"][fi], wst[:, 0:1024])

    wt("vector", "in", t_x)

    def ln_tile(tt, out_ap):
        xt = x_res[:, tt * D:(tt + 1) * D]
        opn("vector", lambda e, i=xt, o=SEf[:, 0:1]: e.reduce_sum(out=o, in_=i, axis=AX.X))
        dv()
        opn("vector", lambda e, i=SEf[:, 0:1], o=SEf[:, 1:2]:
            e.tensor_scalar(out=o, in0=i, scalar1=1.0 / 1024.0, scalar2=None, op0=ALU.mult))
        dv()
        xm = SC[:, 0:D]
        tv = op("vector", lambda e, i=xt, m=SEf[:, 1:2], o=xm:
                e.tensor_scalar(out=o, in0=i, scalar1=m, scalar2=None, op0=ALU.subtract))
        wt("scalar", "v", tv)
        tsq = op("scalar", lambda e, i=xm, o=SC[:, 1024:2048]:
                 e.activation(out=o, in_=i, func=AF.Square))
        wt("vector", "s", tsq)
        tss = op("vector", lambda e, i=SC[:, 1024:2048], o=SEf[:, 2:3]:
                 e.reduce_sum(out=o, in_=i, axis=AX.X))
        wt("scalar", "v", tss)
        t2 = op("scalar", lambda e, i=SEf[:, 2:3], o=SEf[:, 3:4]:
                e.activation(out=o, in_=i, func=AF.Sqrt, scale=1.0 / 1024.0, bias=1e-5))
        wt("vector", "s", t2)
        opn("vector", lambda e, i=SEf[:, 3:4], o=SEf[:, 4:5]: e.reciprocal(out=o, in_=i))
        dv()
        rr = op("vector", lambda e, i=xm, r=SEf[:, 4:5], o=out_ap:
                e.tensor_scalar(out=o, in0=i, scalar1=r, scalar2=None, op0=ALU.mult))
        return rr

    def emit_h_production():
        for tt in range(NT):
            wt("vector", "t", now("t"))
            tln = ln_tile(tt, SD[:, 0:D])
            wt("tensor", "v", tln)
            for ch in range(8):
                op("tensor", lambda e, i=SD[:, ch * 128:(ch + 1) * 128],
                   o=PS_T[:, ch * 128:(ch + 1) * 128]:
                   e.transpose(out=o, in_=i, identity=idn[:, :]))
            wt("vector", "t", now("t"))
            for ch in range(8):
                op("vector", lambda e, i=PS_T[:, ch * 128:(ch + 1) * 128],
                   o=SA[:, ch * 2048 + tt * 128:ch * 2048 + (tt + 1) * 128]:
                   e.tensor_copy(out=o, in_=i))
        wt("gpsimd", "v", now("v"))
        h_own = SA[:, 0:16384].rearrange("p (c t) -> p c t", c=8)
        sw_dma(lambda e, p: e.dma_start(out=h_sh[p].rearrange("c p t -> p c t"),
                                        in_=h_own).then_inc(sem["g"], 16))
        sw_dma(lambda e, p: e.dma_start(
            out=h_full[:, :, p * SH:(p + 1) * SH].rearrange("c p t -> p c t"),
            in_=h_own).then_inc(sem["g"], 16))
        wt("gpsimd", "g", now("g"))
        bv = barrier()
        wt("gpsimd", "bar", bv)
        sw_dma(lambda e, p: e.dma_start(out=h_full[0:4, :, (1 - p) * SH:(2 - p) * SH],
                                        in_=h_sh[1 - p][4:8]).then_inc(sem["g"], 16))
        tg = sw_dma(lambda e, p: e.dma_start(out=h_full[4:8, :, (1 - p) * SH:(2 - p) * SH],
                                             in_=h_sh[1 - p][0:4]).then_inc(sem["g"], 16))
        return tg

    def emit_vproj(wname, dst, t_hready):
        for ch in range(8):
            guard_dma("sync")
            dma("sync", SD[:, ch * 512:(ch + 1) * 512], w_bf[wname][ch], sname="in2")
        wt("sync", "g", t_hready)
        for tt in range(NTF):
            buf = tt % 2
            guard_dma("sync")
            th = dma("sync", wst[:, buf * 1024:(buf + 1) * 1024].rearrange(
                "p (c q) -> p c q", c=8),
                h_full[:, :, tt * 128:(tt + 1) * 128].rearrange("c p q -> p c q"),
                sname="in2")
            wt("tensor", "in2", th)
            for ch in range(8):
                op("tensor", lambda e, i=wst[:, buf * 1024 + ch * 128:buf * 1024 + (ch + 1) * 128],
                   r=SD[:, ch * 512:(ch + 1) * 512], o=PS_C[:, :], s0=(ch == 0), s1=(ch == 7):
                   e.matmul(out=o, lhsT=i, rhs=r, start=s0, stop=s1))
            wt("scalar", "t", now("t"))
            te = op("scalar", lambda e, i=PS_C[:, :], o=smb[:, :]:
                    e.activation(out=o, in_=i, func=AF.Identity))
            wt("gpsimd", "s", te)
            tg = dma("gpsimd", dst[tt], smb[:, :])
            wt("scalar", "g", tg)
            wt("tensor", "g", tg)
        return now("g")

    def emit_kqproj_b(t_hready):
        for ch in range(8):
            guard_dma("sync")
            dma("sync", SD[:, ch * 512:(ch + 1) * 512], w_bf["wk_b"][ch], sname="in2")
            dma("sync", SD[:, 4096 + ch * 512:4096 + (ch + 1) * 512], w_bf["wv_b"][ch] if False else w_bf["wq_b"][ch], sname="in2")
        wt("sync", "g", t_hready)
        for sl in range(8):
            for pr in range(4):
                for kq in range(2):
                    base = 0 if kq == 0 else 4096
                    for ch in range(8):
                        buf = ch % 2
                        guard_dma("sync")
                        th = dma("sync", wst[:, buf * 1024:buf * 1024 + 512],
                                 h_full[ch, :, sl * 512:(sl + 1) * 512], sname="in2")
                        wt("tensor", "in2", th)
                        op("tensor", lambda e,
                           i=SD[:, base + ch * 512 + pr * 128:base + ch * 512 + (pr + 1) * 128],
                           r=wst[:, buf * 1024:buf * 1024 + 512], o=PS_C[:, :],
                           s0=(ch == 0), s1=(ch == 7):
                           e.matmul(out=o, lhsT=i, rhs=r, start=s0, stop=s1))
                    wt("scalar", "t", now("t"))
                    te = op("scalar", lambda e, i=PS_C[:, :], o=smb[:, :],
                            sc=(1.0 if kq == 0 else 0.125):
                            e.activation(out=o, in_=i, func=AF.Identity, scale=sc))
                    wt("gpsimd", "s", te)
                    dd = kb_d if kq == 0 else qb_d
                    tg = dma("gpsimd", dd[pr][:, sl * 512:(sl + 1) * 512], smb[:, :])
                    wt("scalar", "g", tg)
                    wt("tensor", "g", tg)
        return now("g")

    def emit_trig_a(t_hready):
        wt("sync", "g", t_hready)
        n = 1024
        for ch in range(4):
            for blk in range(4):
                guard_dma("sync")
                th = dma("sync", SD[:, 0:n], h_full[ch, :, blk * n:(blk + 1) * n], sname="in2")
                for side in range(2):
                    iw, ib = (0, 4) if side == 0 else (8, 12)
                    wt("scalar", "in2", th)
                    t1 = op("scalar", lambda e, i=SD[:, 0:n], o=SC[:, 0:n],
                            w=prw[:, iw + ch:iw + ch + 1], b=prw[:, ib + ch:ib + ch + 1]:
                            e.activation(out=o, in_=i, func=AF.Identity, scale=w, bias=b))
                    wt("vector", "s", t1)
                    opn("vector", lambda e, i=SC[:, 0:n], o=SI[:, 0:n]: e.tensor_copy(out=o, in_=i))
                    dv()
                    kf = SC[:, 1024:1024 + n]
                    opn("vector", lambda e, i=SI[:, 0:n], o=kf: e.tensor_copy(out=o, in_=i))
                    dv()
                    gp = SC[:, 2048:2048 + n]
                    opn("vector", lambda e, i=kf, o=gp:
                        e.tensor_scalar(out=o, in0=i, scalar1=2048.0, scalar2=None, op0=ALU.is_gt))
                    dv()
                    opn("vector", lambda e, a=gp, b=kf, o=kf:
                        e.scalar_tensor_tensor(out=o, in0=a, scalar=-4096.0, in1=b,
                                               op0=ALU.mult, op1=ALU.add))
                    dv()
                    opn("vector", lambda e, i=kf, o=gp:
                        e.tensor_scalar(out=o, in0=i, scalar1=-2048.0, scalar2=None, op0=ALU.is_lt))
                    dv()
                    opn("vector", lambda e, a=gp, b=kf, o=kf:
                        e.scalar_tensor_tensor(out=o, in0=a, scalar=4096.0, in1=b,
                                               op0=ALU.mult, op1=ALU.add))
                    dv()
                    k2 = SC[:, 0:n]
                    opn("vector", lambda e, i=kf, o=gp:
                        e.tensor_scalar(out=o, in0=i, scalar1=1024.0, scalar2=None, op0=ALU.is_gt))
                    dv()
                    tv = op("vector", lambda e, a=gp, b=kf, o=k2:
                            e.scalar_tensor_tensor(out=o, in0=a, scalar=-4096.0, in1=b,
                                                   op0=ALU.mult, op1=ALU.add))
                    wt("scalar", "v", tv)
                    sn = SD[:, 1024:1024 + n]
                    cs = SD[:, 2048:2048 + n]
                    op("scalar", lambda e, i=kf, o=sn:
                       e.activation(out=o, in_=i, func=AF.Sin, scale=DELTA))
                    t3 = op("scalar", lambda e, i=k2, o=cs:
                            e.activation(out=o, in_=i, func=AF.Sin, scale=DELTA, bias=BIAS_COS))
                    wt("gpsimd", "s", t3)
                    dd = qa_d if side == 0 else ka_d
                    dma("gpsimd", dd[2 * ch][0:64, blk * n:(blk + 1) * n], cs[0:64, :])
                    dma("gpsimd", dd[2 * ch][64:128, blk * n:(blk + 1) * n], sn[0:64, :])
                    dma("gpsimd", dd[2 * ch + 1][0:64, blk * n:(blk + 1) * n], cs[64:128, :])
                    tg = dma("gpsimd", dd[2 * ch + 1][64:128, blk * n:(blk + 1) * n], sn[64:128, :])
                    wt("scalar", "g", tg)
                    wt("vector", "g", tg)
        return now("g")

    def emit_o_exchange():
        wt("gpsimd", "v", now("v"))
        wt("gpsimd", "g", now("g"))
        om4 = OM.rearrange("p (c t) -> p c t", c=4)
        sw_dma(lambda e, p: e.dma_start(
            out=o_sh[4 * p:4 * (p + 1)].rearrange("c p t -> p c t"),
            in_=om4).then_inc(sem["g"], 16))
        bv = barrier()
        wt("gpsimd", "bar", bv)
        def _oread(e, p):
            for c in range(4):
                e.dma_start(out=SA[:, c * 2048:(c + 1) * 2048],
                            in_=o_sh[4 * p + c][:, p * SH:(p + 1) * SH]).then_inc(sem["g"], 16)
                e.dma_start(out=SA[:, 8192 + c * 2048:8192 + (c + 1) * 2048],
                            in_=o_sh[4 * (1 - p) + c][:, p * SH:(p + 1) * SH]).then_inc(sem["g"], 16)
        cnt["g"] += 16 * 8
        def _run_oread(e):
            for arm in e.switch_core_id(n=128):
                _oread(e, (int(arm) % 128) % 2)
        em("gpsimd", _run_oread)
        tg = cnt["g"]
        return tg

    def emit_gemm_residual(wname, rhs_fn, bias, t_rhs_ready):
        wt("tensor", "g", t_rhs_ready)
        for do in range(8):
            guard_dma("sync")
            tw = 0
            for ch in range(8):
                tw = dma("sync", wst[:, ch * 128:(ch + 1) * 128],
                         w_bf[wname][ch][:, do * 128:(do + 1) * 128], sname="in2")
            wt("tensor", "in2", tw)
            for ts4 in range(4):
                wt("tensor", "s", now("s"))
                for ch in range(8):
                    op("tensor", lambda e, i=wst[:, ch * 128:(ch + 1) * 128],
                       r=rhs_fn(ch)[:, ts4 * 512:(ts4 + 1) * 512], o=PS_C[:, :],
                       s0=(ch == 0), s1=(ch == 7):
                       e.matmul(out=o, lhsT=i, rhs=r, start=s0, stop=s1))
                wt("scalar", "t", now("t"))
                if bias is not None:
                    te = op("scalar", lambda e, i=PS_C[:, :], o=smb[:, :], b=bias[:, do:do + 1]:
                            e.activation(out=o, in_=i, func=AF.Identity, bias=b))
                else:
                    te = op("scalar", lambda e, i=PS_C[:, :], o=smb[:, :]:
                            e.activation(out=o, in_=i, func=AF.Identity))
                wt("tensor", "s", te)
                wt("tensor", "v", now("v"))
                for q in range(4):
                    op("tensor", lambda e, i=smb[:, q * 128:(q + 1) * 128],
                       o=PS_T[:, q * 128:(q + 1) * 128]:
                       e.transpose(out=o, in_=i, identity=idn[:, :]))
                wt("vector", "t", now("t"))
                for q in range(4):
                    tt_ = ts4 * 4 + q
                    op("vector", lambda e,
                       a=x_res[:, tt_ * D + do * 128:tt_ * D + (do + 1) * 128],
                       b=PS_T[:, q * 128:(q + 1) * 128],
                       o=x_res[:, tt_ * D + do * 128:tt_ * D + (do + 1) * 128]:
                       e.tensor_tensor(out=o, in0=a, in1=b, op=ALU.add))

    def emit_attn_a():
        t_h = emit_h_production()
        t_v = emit_vproj("vw_a", va_d, t_h)
        t_t = emit_trig_a(t_h)
        half_s = [0, 0]
        half_v = [0, 0]
        pv_last = [0, 0]
        for h in range(8):
            guard_dma("sync")
            dma("sync", SD[:, 0:4096], ka_d[h], sname="in2")
            tq = dma("sync", SD[:, 4096:8192], qa_d[h], sname="in2")
            wt("tensor", "in2", tq)
            for qc in range(8):
                nkt = 4 * qc + 4
                for g in range((nkt + 1) // 2):
                    half = g % 2
                    base = half * 1024
                    kts = [2 * g + j for j in range(2) if 2 * g + j < nkt]
                    wt("tensor", "v", half_v[half])
                    wt("tensor", "s", half_s[half])
                    for j, kt in enumerate(kts):
                        op("tensor", lambda e, i=SD[:, kt * 128:(kt + 1) * 128],
                           r=SD[:, 4096 + qc * 512:4096 + (qc + 1) * 512],
                           o=PS_BIG[:, base + j * 512:base + (j + 1) * 512]:
                           e.matmul(out=o, lhsT=i, rhs=r, start=True, stop=True))
                    wt("scalar", "t", now("t"))
                    ts = op("scalar", lambda e, i=PS_BIG[:, base:base + len(kts) * 512],
                            o=SA[:, kts[0] * 512:(kts[0] + len(kts)) * 512]:
                            e.activation(out=o, in_=i, func=AF.Exp, scale=SCALE_A))
                    half_s[half] = ts
                    half_v[half] = now("v")
                    for kt in kts:
                        if kt >= 4 * qc:
                            r = kt - 4 * qc
                            wt("vector", "s", ts)
                            half_v[half] = op("vector",
                               lambda e, a=SA[:, kt * 512:(kt + 1) * 512],
                               m=msk[:, r * 512:(r + 1) * 512],
                               o=SA[:, kt * 512:(kt + 1) * 512]:
                               e.tensor_tensor(out=o, in0=a, in1=m, op=ALU.mult))
                wt("tensor", "v", now("v"))
                wt("tensor", "s", now("s"))
                for kt in range(nkt):
                    buf = kt % 2
                    wt("sync", "t", pv_last[buf])
                    tvd = dma("sync", pvs[:, buf * 65:buf * 65 + 64],
                              va_d[kt][:, h * 64:(h + 1) * 64], sname="in2")
                    wt("tensor", "in2", tvd)
                    pv_last[buf] = op("tensor", lambda e, i=pvs[:, buf * 65:(buf + 1) * 65],
                       r=SA[:, kt * 512:(kt + 1) * 512], o=PS_A[0:65, :],
                       s0=(kt == 0), s1=(kt == nkt - 1):
                       e.matmul(out=o, lhsT=i, rhs=r, start=s0, stop=s1))
                wt("vector", "t", now("t"))
                opn("vector", lambda e, i=PS_A[64:65, :], o=smf[0:1, :]:
                    e.reciprocal(out=o, in_=i))
                dv()
                tvr = op("vector", lambda e, i=PS_A[0:64, :], o=smb[0:64, :]:
                         e.tensor_copy(out=o, in_=i))
                wt("tensor", "v", tvr)
                tb = op("tensor", lambda e, i=on1f[:, 0:64], r=smf[0:1, :], o=PS_B[0:64, :]:
                        e.matmul(out=o, lhsT=i, rhs=r, start=True, stop=True))
                wt("vector", "t", tb)
                if h % 2 == 0:
                    dst = OM[0:64, (h // 2) * S + qc * 512:(h // 2) * S + (qc + 1) * 512]
                    op("vector", lambda e, a=smb[0:64, :], b=PS_B[0:64, :], o=dst:
                       e.tensor_tensor(out=o, in0=a, in1=b, op=ALU.mult))
                else:
                    tvo = op("vector", lambda e, a=smb[0:64, :], b=PS_B[0:64, :], o=smf[0:64, :]:
                             e.tensor_tensor(out=o, in0=a, in1=b, op=ALU.mult))
                    tvo2 = op("vector", lambda e, i=smf[0:64, :], o=smb[0:64, :]:
                              e.tensor_copy(out=o, in_=i))
                    wt("gpsimd", "v", tvo2)
                    tg = dma("gpsimd", OM[64:128, (h // 2) * S + qc * 512:
                                          (h // 2) * S + (qc + 1) * 512], smb[0:64, :])
                    wt("vector", "g", tg)
                wt("tensor", "v", now("v"))
        t_o = emit_o_exchange()
        emit_gemm_residual("wo_a", lambda ch: SA[:, ch * 2048:(ch + 1) * 2048], None, t_o)

    def emit_ffn():
        for tt in range(NT):
            wt("vector", "t", now("t"))
            wt("vector", "s", now("s"))
            tln = ln_tile(tt, SC[:, 0:D])
            emit_quant(SC[:, 0:D], SD[:, 0:D], D)
            wt("tensor", "v", now("v"))
            for ch in range(8):
                op("tensor", lambda e, i=SD[:, ch * 128:(ch + 1) * 128],
                   o=PS_T[:, ch * 128:(ch + 1) * 128]:
                   e.transpose(out=o, in_=i, identity=idn[:, :]))
            wt("vector", "t", now("t"))
            for ch in range(8):
                op("vector", lambda e, i=PS_T[:, ch * 128:(ch + 1) * 128],
                   o=SA[:, ch * 2048 + tt * 128:ch * 2048 + (tt + 1) * 128]:
                   e.tensor_copy(out=o, in_=i))
        emit_gemm_residual("wffT", lambda ch: SA[:, ch * 2048:(ch + 1) * 2048], bfc, now("g"))

    def emit_attn_b():
        t_h = emit_h_production()
        emit_vproj("wv_b", vb_d, t_h)
        t_kq = emit_kqproj_b(t_h)
        pvb_last = [0, 0]
        for pr in range(4):
            guard_dma("sync")
            dma("sync", SD[:, 0:4096], kb_d[pr], sname="in2")
            tq = dma("sync", SD[:, 4096:8192], qb_d[pr], sname="in2")
            wt("tensor", "in2", tq)
            for h01 in range(2):
                hr = lambda apx, h01=h01: apx[h01 * 64:(h01 + 1) * 64, :]
                for qc in range(8):
                    q0 = 4096 + qc * 512
                    if bstage < 1:
                        continue
                    if bstage == 51 and not (pr == 0 and h01 == 0 and qc == 0):
                        pass
                    for qt2 in range(4):
                        qt = 4 * qc + qt2
                        for hf in range(2):
                            wt("tensor", "v", now("v"))
                            wt("tensor", "s", now("s"))
                            for ksl in range(4):
                                op("tensor", lambda e,
                                   i=SD[:, 4096 + qt * 128:4096 + (qt + 1) * 128][h01 * 64:(h01 + 1) * 64, :],
                                   r=SD[:, hf * 2048 + ksl * 512:hf * 2048 + (ksl + 1) * 512][h01 * 64:(h01 + 1) * 64, :],
                                   o=PS_BIG[:, ksl * 512:(ksl + 1) * 512]:
                                   e.matmul(out=o, lhsT=i, rhs=r, start=True, stop=True))
                            wt("vector", "t", now("t"))
                            op("vector", lambda e,
                               i=PS_BIG.rearrange("p (a b) -> p a b", b=64),
                               o=SEb[:, qt2 * 64 + hf * 32:qt2 * 64 + hf * 32 + 32]:
                               e.reduce_max(out=o, in_=i, axis=AX.X))
                    if bstage < 2:
                        continue
                    wt("tensor", "v", now("v"))
                    bh_s = [now("s"), now("s")]
                    for g in range(16):
                        half = g % 2
                        base = half * 1024
                        wt("tensor", "s", bh_s[half])
                        for j in range(2):
                            kt = 2 * g + j
                            op("tensor", lambda e,
                               i=SD[:, kt * 128:(kt + 1) * 128][h01 * 64:(h01 + 1) * 64, :],
                               r=SD[:, q0:q0 + 512][h01 * 64:(h01 + 1) * 64, :],
                               o=PS_BIG[:, base + j * 512:base + (j + 1) * 512]:
                               e.matmul(out=o, lhsT=i, rhs=r, start=True, stop=True))
                        wt("scalar", "t", now("t"))
                        tsx = op("scalar", lambda e, i=PS_BIG[:, base:base + 1024],
                                 o=SA[:, base:base + 1024]:
                                 e.activation(out=o, in_=i, func=AF.Exp))
                        bh_s[half] = tsx
                        wt("tensor", "s", tsx)
                        for j in range(2):
                            kt = 2 * g + j
                            op("tensor", lambda e, i=bgm[:, 126 - 2 * kt:190 - 2 * kt],
                               r=SA[:, base + j * 512:base + (j + 1) * 512], o=PS_B[0:64, :],
                               s0=(kt == 0), s1=(kt == 31):
                               e.matmul(out=o, lhsT=i, rhs=r, start=s0, stop=s1))
                    if bstage < 3:
                        continue
                    wt("vector", "t", now("t"))
                    tev = op("vector", lambda e, i=PS_B[0:64, :], o=smb[0:64, :]:
                             e.tensor_copy(out=o, in_=i))
                    wt("tensor", "v", tev)
                    for qt2 in range(4):
                        op("tensor", lambda e, i=smb[0:64, qt2 * 128:(qt2 + 1) * 128],
                           o=PS_T[:, qt2 * 128:qt2 * 128 + 64],
                           iy=idn[0:64, 0:64]:
                           e.transpose(out=o, in_=i, identity=iy))
                    wt("vector", "t", now("t"))
                    for qt2 in range(4):
                        op("vector", lambda e, i=PS_T[:, qt2 * 128:qt2 * 128 + 64],
                           o=SEb[:, 256 + qt2 * 64:256 + (qt2 + 1) * 64]:
                           e.tensor_copy(out=o, in_=i))
                    if bstage < 4 or bstage == 58:
                        continue
                    wt("tensor", "v", now("v"))
                    for qt2 in range(4):
                        mx_ = SEb[:, qt2 * 64:(qt2 + 1) * 64]
                        lam_ = SEb[:, 256 + qt2 * 64:256 + (qt2 + 1) * 64]
                        lcum = SEf[:, 0:64]
                        dv()
                        if bstage == 59:
                            opn("vector", lambda e, i=lam_, o=lcum: e.tensor_copy(out=o, in_=i))
                        else:
                            opn("vector", lambda e, i=lam_, o=lcum:
                                e.tensor_tensor_scan(out=o, data0=i, data1=i, initial=0.0,
                                                     op0=ALU.add, op1=ALU.bypass))
                        mrun = SEf[:, 64:128]
                        dv()
                        if bstage == 59:
                            tv1 = op("vector", lambda e, i=mx_, o=mrun: e.tensor_copy(out=o, in_=i))
                        else:
                            tv1 = op("vector", lambda e, i=mx_, o=mrun:
                                     e.tensor_tensor_scan(out=o, data0=i, data1=i, initial=-1e30,
                                                          op0=ALU.max, op1=ALU.bypass))
                        lnl = SEf[:, 128:192]
                        wt("scalar", "v", tv1)
                        t2 = op("scalar", lambda e, i=lcum, o=lnl:
                                e.activation(out=o, in_=i, func=AF.Ln))
                        wt("vector", "s", t2)
                        d_ = SEf[:, 192:256]
                        dv()
                        opn("vector", lambda e, a=mrun, b=lnl, o=d_:
                            e.tensor_tensor(out=o, in0=a, in1=b, op=ALU.subtract))
                        incl = SEf[:, 256:320]
                        dv()
                        if bstage == 59:
                            opn("vector", lambda e, i=d_, o=incl: e.tensor_copy(out=o, in_=i))
                        else:
                            opn("vector", lambda e, i=d_, o=incl:
                                e.tensor_tensor_scan(out=o, data0=i, data1=i, initial=0.0,
                                                     op0=ALU.add, op1=ALU.bypass))
                        if bstage == 592:
                            continue
                        t_ = SEf[:, 320:384]
                        dv()
                        opn("vector", lambda e, a=incl, b=d_, o=t_:
                            e.tensor_tensor(out=o, in0=a, in1=b, op=ALU.subtract))
                        suf = SEf[:, 384:448]
                        dv()
                        opn("vector", lambda e, a=t_, s_=incl[:, 63:64].broadcast_to((128, 64)), o=suf:
                            e.scalar_tensor_tensor(out=o, in0=a, scalar=-1.0, in1=s_,
                                                   op0=ALU.mult, op1=ALU.add))
                        if bstage == 591:
                            continue
                        dv()
                        opn("vector", lambda e, i=suf, c=d_[:, 63:64], o=pay[:, 0:64]:
                            e.tensor_scalar(out=o, in0=i, scalar1=c, scalar2=None,
                                            op0=ALU.subtract))
                        nrm = SEf[:, 448:449]
                        dv()
                        opn("vector", lambda e, i=lnl[:, 63:64], o=nrm:
                            e.tensor_scalar(out=o, in0=i, scalar1=-1.0, scalar2=None, op0=ALU.mult))
                        dv()
                        opn("vector", lambda e, i=nrm, o=pay[:, 64:65]: e.tensor_copy(out=o, in_=i))
                        dv()
                        tv3 = op("vector", lambda e, a=nrm, b=pay[:, 64:65], o=pay[:, 65:66]:
                                 e.tensor_tensor(out=o, in0=a, in1=b, op=ALU.subtract))
                        wt("tensor", "v", tv3)
                        tt4 = op("tensor", lambda e, i=pay[:, 0:128],
                                 o=PS_T[:, qt2 * 128:(qt2 + 1) * 128]:
                                 e.transpose(out=o, in_=i, identity=idn[:, :]))
                        wt("vector", "t", tt4)
                        op("vector", lambda e, i=PS_T[0:66, qt2 * 128:qt2 * 128 + 128] if False else PS_T[:, qt2 * 128:(qt2 + 1) * 128][0:66, :],
                           o=smb[0:66, qt2 * 128:(qt2 + 1) * 128]:
                           e.tensor_copy(out=o, in_=i))
                    if bstage < 5:
                        continue
                    dh_s = [now("s"), now("s")]
                    for g in range(16):
                        half = g % 2
                        base = half * 1024
                        wt("tensor", "s", dh_s[half])
                        if g == 0:
                            wt("tensor", "v", now("v"))
                        for j in range(2):
                            kt = 2 * g + j
                            op("tensor", lambda e,
                               i=SD[:, kt * 128:(kt + 1) * 128][h01 * 64:(h01 + 1) * 64, :],
                               r=SD[:, q0:q0 + 512][h01 * 64:(h01 + 1) * 64, :],
                               o=PS_BIG[:, base + j * 512:base + (j + 1) * 512]:
                               e.matmul(out=o, lhsT=i, rhs=r, start=True, stop=False))
                        for j in range(2):
                            kt = 2 * g + j
                            op("tensor", lambda e, i=selb[0:66, kt * 128:(kt + 1) * 128],
                               r=smb[0:66, 0:512],
                               o=PS_BIG[:, base + j * 512:base + (j + 1) * 512]:
                               e.matmul(out=o, lhsT=i, rhs=r, start=False, stop=True))
                        wt("scalar", "t", now("t"))
                        tsx = op("scalar", lambda e, i=PS_BIG[:, base:base + 1024],
                                 o=SA[:, g * 1024:(g + 1) * 1024]:
                                 e.activation(out=o, in_=i, func=AF.Exp))
                        dh_s[half] = tsx
                    if bstage < 60 and bstage != 601:
                        continue
                    wt("tensor", "s", now("s"))
                    wt("tensor", "v", now("v"))
                    for kt in range(32):
                        buf = kt % 2
                        wt("sync", "t", pvb_last[buf])
                        tvd = dma("sync", vst[:, buf * 128:buf * 128 + 64],
                                  vb_d[kt][:, (pr * 2 + h01) * 64:(pr * 2 + h01 + 1) * 64],
                                  sname="in2")
                        wt("tensor", "in2", tvd)
                        op("tensor", lambda e, i=vst[0:64, buf * 128:buf * 128 + 64],
                           r=SA[:, kt * 512:(kt + 1) * 512][0:64, :], o=PS_A[0:64, :],
                           s0=(kt == 0), s1=(kt == 31):
                           e.matmul(out=o, lhsT=i, rhs=r, start=s0, stop=s1))
                        pvb_last[buf] = op("tensor",
                           lambda e, i=vst[64:128, buf * 128:buf * 128 + 64],
                           r=SA[:, kt * 512:(kt + 1) * 512][64:128, :], o=PS_B[0:64, :],
                           s0=(kt == 0), s1=(kt == 31), tp=(64, 0):
                           e.matmul(out=o, lhsT=i, rhs=r, start=s0, stop=s1, tile_position=tp))
                    wt("vector", "t", now("t"))
                    op("vector", lambda e, i=PS_A[0:64, :], o=smf[0:64, :]:
                       e.tensor_copy(out=o, in_=i))
                    dv()
                    if h01 == 0:
                        op("vector", lambda e, a=smf[0:64, :], b=PS_B[0:64, :],
                           o=OM[0:64, pr * S + qc * 512:pr * S + (qc + 1) * 512]:
                           e.tensor_tensor(out=o, in0=a, in1=b, op=ALU.add))
                    else:
                        tvo = op("vector", lambda e, a=smf[0:64, :], b=PS_B[0:64, :],
                                 o=smb[64:128, 0:512]:
                                 e.tensor_tensor(out=o, in0=a, in1=b, op=ALU.add))
                        wt("gpsimd", "v", tvo)
                        tg = dma("gpsimd", OM[64:128, pr * S + qc * 512:pr * S + (qc + 1) * 512],
                                 smb[64:128, 0:512])
                        wt("vector", "g", tg)
                    wt("tensor", "v", now("v"))
        t_o = emit_o_exchange()
        emit_gemm_residual("wo_b", lambda ch: SA[:, ch * 2048:(ch + 1) * 2048], None, t_o)

    if n_sub == -3:
        emit_attn_b()
    if n_sub >= 1:
        emit_attn_a()
    if n_sub >= 2:
        emit_ffn()
    if n_sub >= 3:
        emit_attn_b()
    if n_sub >= 4:
        emit_ffn()

    wt("gpsimd", "v", now("v"))
    for tt in range(NT):
        dma("gpsimd", y_out.rearrange("(n p) d -> n p d", p=128)[tt],
            x_res[:, tt * D:(tt + 1) * D])
    wt("gpsimd", "g", now("g"))

    blk = st.enter_context(nc.Block())

    @blk.sync
    def _(e):
        for fn in EM["sync"]:
            fn(e)

    @blk.vector
    def _(e):
        for fn in EM["vector"]:
            fn(e)

    @blk.scalar
    def _(e):
        for fn in EM["scalar"]:
            fn(e)

    @blk.tensor
    def _(e):
        for fn in EM["tensor"]:
            fn(e)

    @blk.gpsimd
    def _(e):
        for fn in EM["gpsimd"]:
            fn(e)

    st.close()
    global _LAST_CNT
    _LAST_CNT = dict(cnt)
    _LAST_CNT["_em"] = {k: len(v) for k, v in EM.items()}
    try:
        _LAST_CNT["_insts"] = len(list(nc.all_instructions()))
    except Exception:
        pass
    return nc


def _host_prep(inputs, core):
    p = core % 2
    b = core // 2
    perm = np.arange(D)
    if p == 1:
        perm = np.concatenate([perm[512:], perm[:512]])
    hsl = slice(p * 8, p * 8 + 8)

    x = inputs["L0_attn_a_x"]
    xc = np.ascontiguousarray(x[b, p * SH:(p + 1) * SH, :][:, perm], np.float32)

    def chunkT(w):
        return np.ascontiguousarray(w.reshape(8, 128, -1), np.float32)

    wq = inputs["L0_attn_a_w_q"][hsl].reshape(512)
    bq = inputs["L0_attn_a_b_q"][hsl].reshape(512)
    wk = inputs["L0_attn_a_w_k"][hsl].reshape(512)
    bk = inputs["L0_attn_a_b_k"][hsl].reshape(512)
    tprm = np.stack([wq.reshape(4, 128), bq.reshape(4, 128),
                     wk.reshape(4, 128), bk.reshape(4, 128)], 0)
    tprm = np.ascontiguousarray(tprm.reshape(16, 128).T, np.float32)

    vw = inputs["L0_attn_a_v_w"][p * 512:(p + 1) * 512, :][:, perm]
    vw_a = chunkT(np.ascontiguousarray(vw.T))
    ow = inputs["L0_attn_a_out_w"]
    wo_a = chunkT(np.ascontiguousarray(ow.T[perm][:, perm]))

    wqb = inputs["L0_attn_b_Wq"][perm][:, p * 512:(p + 1) * 512]
    wkb = inputs["L0_attn_b_Wk"][perm][:, p * 512:(p + 1) * 512]
    wvb = inputs["L0_attn_b_Wv"][perm][:, p * 512:(p + 1) * 512]
    wob = inputs["L0_attn_b_Wo"][perm][:, perm]
    wff = inputs["L0_ffn_weight"][perm][:, perm]
    bff = inputs["L0_ffn_bias"][perm]

    masks = np.zeros((4, 128, 512), ml_dtypes.bfloat16)
    for r in range(4):
        i_ = np.arange(128)[:, None] + 128 * r
        j_ = np.arange(512)[None, :]
        masks[r] = (i_ <= j_).astype(ml_dtypes.bfloat16)
    ident = np.eye(128, dtype=ml_dtypes.bfloat16)
    ind2 = np.zeros((2, 128), ml_dtypes.bfloat16)
    ind2[0, 0:64] = 1
    ind2[1, 64:128] = 1
    ones2 = np.ones((2, 128), ml_dtypes.bfloat16)
    ones1 = np.ones((1, 128), ml_dtypes.bfloat16)
    bigm = np.zeros((128, 192), ml_dtypes.bfloat16)
    bigm[0:64, 126] = 1
    bigm[64:128, 127] = 1
    selb = np.zeros((128, 4096), ml_dtypes.bfloat16)
    for kt in range(32):
        for c in range(128):
            selb[2 * kt + (c >= 64), kt * 128 + c] = 1
        selb[64, kt * 128:(kt + 1) * 128] = 1
        selb[65, kt * 128:(kt + 1) * 128] = 1

    return {
        "x_in": xc, "tprm": tprm,
        "bffi": np.ascontiguousarray(bff.reshape(8, 128).T, np.float32),
        "vw_a": vw_a, "wo_a": wo_a,
        "wq_b": chunkT(wqb), "wk_b": chunkT(wkb), "wv_b": chunkT(wvb),
        "wo_b": chunkT(wob), "wff": chunkT(wff),
        "masks_i": masks, "ident_i": ident, "ind2_i": ind2, "ones2_i": ones2,
        "ones1_i": ones1, "ones1f_i": np.ones((1, 128), np.float32), "bigm_i": bigm,
        "selb_i": selb,
        "cvals_i": np.ascontiguousarray(
            np.broadcast_to(np.array([1e-5, BIAS_COS, 0.0, 0.0], np.float32), (128, 4))),
    }, perm, b, p


def kernel(**inputs):
    key = ("nc", SUBLAYERS, ATTNB_STAGE)
    if key not in _CACHE:
        _CACHE[key] = build_nc(SUBLAYERS, ATTNB_STAGE)
    nc = _CACHE[key]
    in_maps = []
    metas = []
    for c in range(8):
        m, perm, b, p = _host_prep(inputs, c)
        in_maps.append(m)
        metas.append((perm, b, p))
    res = run_bass_kernel_spmd(nc, in_maps, core_ids=list(range(8)))
    out = np.empty((B, S, D), np.float32)
    for c in range(8):
        perm, b, p = metas[c]
        inv = np.argsort(perm)
        out[b, p * SH:(p + 1) * SH, :] = res.results[c]["y_out"][:, inv]
    return out


if __name__ == "__main__":
    nc = build_nc()
    print("built ok")


# version 04
# speedup vs baseline: 1.9280x; 1.4853x over previous
import contextlib
import numpy as np
import ml_dtypes
import concourse.bass as bass
import concourse.mybir as mybir
from concourse.bass_utils import run_bass_kernel_spmd

F32 = mybir.dt.float32
BF16 = mybir.dt.bfloat16
I32 = mybir.dt.int32
AF = mybir.ActivationFunctionType
ALU = mybir.AluOpType
AX = mybir.AxisListType

B, S, D, H, DH = 4, 4096, 1024, 16, 64
SH = S // 2
NT = SH // 128
NTF = S // 128
NKT = S // 128
C_LUT = float(np.float32(4096.0 / (2.0 * np.pi)))
DELTA = float(np.float32(np.float32(2.0 * np.pi) / 4096.0))
BIAS_COS = float(np.float32(1024.0) * np.float32(DELTA))
SCALE_A = float(np.float32(1.0 / np.sqrt(np.float32(128.0))))
SUBLAYERS = 4
ATTNB_STAGE = 60
_LAST_CNT = None
_CACHE = {}


def build_nc(n_sub=4, bstage=None):
    if bstage is None:
        bstage = ATTNB_STAGE
    nc = bass.Bass("TRN2", target_bir_lowering=False)

    ein = lambda n, sh, dt=F32: nc.dram_tensor(n, sh, dt, kind="ExternalInput").ap()
    x_in = ein("x_in", [SH, D])
    tprm = ein("tprm", [128, 16])
    bffi = ein("bffi", [128, 8])
    vw_a = ein("vw_a", [8, 128, 512])
    wo_a = ein("wo_a", [8, 128, 1024])
    wq_b = ein("wq_b", [8, 128, 512])
    wk_b = ein("wk_b", [8, 128, 512])
    wv_b = ein("wv_b", [8, 128, 512])
    wo_b = ein("wo_b", [8, 128, 1024])
    wff = ein("wff", [8, 128, 1024])
    masks_i = ein("masks_i", [4, 128, 512], BF16)
    ident_i = ein("ident_i", [128, 128], BF16)
    ind2_i = ein("ind2_i", [2, 128], BF16)
    ones2_i = ein("ones2_i", [2, 128], BF16)
    ones1_i = ein("ones1_i", [1, 128], BF16)
    ones1f_i = ein("ones1f_i", [1, 128])
    bigm_i = ein("bigm_i", [128, 192], BF16)
    selb_i = ein("selb_i", [128, 4096], BF16)
    cvals_i = ein("cvals_i", [128, 4])
    y_out = nc.dram_tensor("y_out", [SH, D], F32, kind="ExternalOutput").ap()

    din = lambda n, sh, dt=BF16, **kw: nc.dram_tensor(n, sh, dt, kind="Internal", **kw).ap()
    h_sh = din("h_sh", [2, 8, 128, SH], BF16, addr_space="Shared")
    o_sh = din("o_sh", [8, 128, S], BF16, addr_space="Shared")
    h_full = din("h_full", [8, 128, S])
    qa_d = din("qa_d", [8, 128, S])
    ka_d = din("ka_d", [8, 128, S])
    va_d = din("va_d", [NKT, 128, 512])
    qb_d = din("qb_d", [4, 128, S])
    kb_d = din("kb_d", [4, 128, S])
    vb_d = din("vb_d", [NKT, 128, 512])
    WCOLS = {"vw_a": 512, "wo_a": 1024, "wq_b": 512, "wk_b": 512,
             "wv_b": 512, "wo_b": 1024, "wffT": 1024}
    w_bf = {n: din("bf_" + n, [8, 128, c]) for n, c in WCOLS.items()}
    bar_i = din("bar_i", [1, 64], F32)
    bar_o = din("bar_o", [2, 64], F32)

    st = contextlib.ExitStack()
    sb = lambda n, sh, dt: st.enter_context(nc.sbuf_tensor(n, sh, dt))
    pst = lambda n, sh, dt=F32: st.enter_context(nc.psum_tensor(n, sh, dt))

    x_res = sb("x_res", [128, NT * D], F32)
    SA = sb("SA", [128, 16384], BF16)
    OM = sb("OM", [128, 4 * S], BF16)
    SC = sb("SC", [128, 3 * 1024], F32)
    SD = sb("SD", [128, 8192], BF16)
    SEb = sb("SEb", [128, 1024], BF16)
    SEf = sb("SEf", [128, 1056], F32)
    SI = sb("SI", [128, 1024], I32)
    msk = sb("msk", [128, 4 * 512], BF16)
    idn = sb("idn", [128, 128], BF16)
    i2 = sb("i2", [2, 128], BF16)
    o2 = sb("o2", [2, 128], BF16)
    on1 = sb("on1", [1, 128], BF16)
    on1f = sb("on1f", [1, 128], F32)
    bgm = sb("bgm", [128, 192], BF16)
    selb = sb("selb", [128, 4096], BF16)
    prm = sb("prm", [128, 16], F32)
    prw = sb("prw", [128, 16], F32)
    bfc = sb("bfc", [128, 8], F32)
    pvs = sb("pvs", [128, 4 * 65], BF16)
    vst = sb("vst", [128, 4 * 64], BF16)
    wst = sb("wst", [128, 2 * 1024], BF16)
    smf = sb("smf", [128, 512], F32)
    smb = sb("smb", [128, 512], BF16)
    pay = sb("pay", [128, 132], BF16)
    cva = sb("cva", [128, 4], F32)

    PS_BIG = pst("PS_BIG", [128, 2048], F32)
    PS_A = pst("PS_A", [128, 512], F32)
    PS_B = pst("PS_B", [128, 512], F32)
    PS_T = pst("PS_T", [128, 1024], BF16)
    PS_C = pst("PS_C", [128, 512], F32)

    sem = {n: st.enter_context(nc.semaphore(name="sem_" + n)) for n in
           ["in", "in2", "v", "s", "t", "g", "bar"]}
    cnt = {n: 0 for n in sem}
    EM = {k: [] for k in ["sync", "vector", "scalar", "tensor", "gpsimd"]}
    ES = {"sync": "in", "vector": "v", "scalar": "s", "tensor": "t", "gpsimd": "g"}

    def em(eng, fn):
        EM[eng].append(fn)

    def dma(eng, out, in_, sname=None):
        sname = sname or ES[eng]
        cnt[sname] += 16
        em(eng, lambda e, o=out, i=in_, s=sem[sname]: e.dma_start(out=o, in_=i).then_inc(s, 16))
        return cnt[sname]

    def wt(eng, sname, val):
        if val > 0:
            em(eng, lambda e, s=sem[sname], v=val: e.wait_ge(s, v))

    def op(eng, fn):
        sname = ES[eng]
        cnt[sname] += 1
        em(eng, lambda e, f=fn, s=sem[sname]: f(e).then_inc(s, 1))
        return cnt[sname]

    def opn(eng, fn):
        em(eng, fn)

    def dv():
        em("vector", lambda e: e.drain())

    def now(s):
        return cnt[s]

    def guard_dma(eng):
        for s in ("v", "s", "t", "g"):
            wt(eng, s, cnt[s])

    def sw_dma(fn_of_p):
        cnt["g"] += 16
        def run(e, f=fn_of_p):
            for arm in e.switch_core_id(n=128):
                f(e, (int(arm) % 128) % 2)
        em("gpsimd", run)
        return cnt["g"]

    def barrier():
        cnt["bar"] += 1
        em("gpsimd", lambda e, s=sem["bar"]: e.collective_compute(
            "AllGather", ALU.bypass, replica_groups=[[0, 1], [2, 3], [4, 5], [6, 7]],
            ins=[bar_i], outs=[bar_o]).then_inc(s, 1))
        return cnt["bar"]

    for r in range(4):
        dma("sync", msk[:, r * 512:(r + 1) * 512], masks_i[r])
    dma("sync", idn[:, :], ident_i)
    dma("sync", i2[:, :], ind2_i)
    dma("sync", o2[:, :], ones2_i)
    dma("sync", on1[:, :], ones1_i)
    dma("sync", on1f[:, :], ones1f_i)
    dma("sync", bgm[:, :], bigm_i)
    dma("sync", selb[:, :], selb_i)
    dma("sync", prm[:, :], tprm)
    dma("sync", cva[:, :], cvals_i)
    nc.const_aps.register(F32, 1e-5, cva[:, 0:1])
    nc.const_aps.register(F32, BIAS_COS, cva[:, 1:2])
    t_c = dma("sync", bfc[:, :], bffi)
    for tt in range(NT):
        dma("sync", x_res[:, tt * D:(tt + 1) * D],
            x_in.rearrange("(n p) d -> n p d", p=128)[tt])
    t_x = now("in")

    opn("vector", lambda e: e.memset(pvs[:, :], 0.0))
    dv()
    for b_ in range(3):
        opn("vector", lambda e, c=b_ * 65 + 64: e.memset(pvs[:, c:c + 1], 1.0))
    op("vector", lambda e: e.memset(pvs[:, 3 * 65 + 64:3 * 65 + 65], 1.0))

    wt("scalar", "in", t_c)
    wt("vector", "in", t_c)
    for dst, s_w, s_b in [(0, 0, 4), (8, 8, 12)]:
        t1 = op("scalar", lambda e, i=prm[:, s_w:s_w + 4], o=SC[:, 0:4]:
                e.activation(out=o, in_=i, func=AF.Abs, bias=1.0))
        wt("vector", "s", t1)
        opn("vector", lambda e, i=SC[:, 0:4], o=SC[:, 4:8]: e.reciprocal(out=o, in_=i))
        dv()
        opn("vector", lambda e, i=SC[:, 4:8], o=prw[:, dst:dst + 4]:
            e.tensor_scalar(out=o, in0=i, scalar1=C_LUT, scalar2=None, op0=ALU.mult))
        op("vector", lambda e, i=prm[:, s_b:s_b + 4], o=prw[:, dst + 4:dst + 8]:
           e.tensor_scalar(out=o, in0=i, scalar1=C_LUT, scalar2=None, op0=ALU.mult))

    SRC_F32 = {"vw_a": vw_a, "wo_a": wo_a, "wq_b": wq_b, "wk_b": wk_b,
               "wv_b": wv_b, "wo_b": wo_b}
    for name, src in SRC_F32.items():
        ncol = WCOLS[name]
        for ch in range(8):
            guard_dma("sync")
            t1 = dma("sync", SC[:, 0:ncol], src[ch])
            wt("vector", "in", t1)
            t2 = op("vector", lambda e, i=SC[:, 0:ncol], o=wst[:, 0:ncol]:
                    e.tensor_copy(out=o, in_=i))
            wt("gpsimd", "v", t2)
            t3 = dma("gpsimd", w_bf[name][ch], wst[:, 0:ncol])
            wt("sync", "g", t3)
            wt("vector", "g", t3)

    def emit_quant(src, dst, n):
        nb = n // 32
        amx = SEf[:, 0:nb]
        opn("vector", lambda e, i=src.rearrange("p (a b) -> p a b", b=32), o=amx:
            e.reduce_max(out=o, in_=i, axis=AX.X, apply_absolute_value=True))
        scl = SEf[:, nb:2 * nb]
        rsc = SEf[:, 2 * nb:3 * nb]
        dv()
        opn("vector", lambda e, i=amx, o=scl:
            e.tensor_scalar(out=o, in0=i, scalar1=float(np.float32(1.0) / np.float32(7.5)),
                            scalar2=1e-30, op0=ALU.mult, op1=ALU.max))
        dv()
        opn("vector", lambda e, i=scl, o=rsc: e.reciprocal(out=o, in_=i))
        dv()
        yv = SC[:, 1024:1024 + n]
        tv = op("vector", lambda e, x=src.rearrange("p (a b) -> p a b", b=32),
                r=rsc.unsqueeze(2).broadcast_to((128, nb, 32)),
                o=yv.rearrange("p (a b) -> p a b", b=32):
                e.tensor_tensor(out=o, in0=x, in1=r, op=ALU.mult))
        ay = SC[:, 2048:2048 + n]
        wt("scalar", "v", tv)
        ts = op("scalar", lambda e, i=yv, o=ay: e.activation(out=o, in_=i, func=AF.Abs))
        wt("vector", "s", ts)
        f1 = SD[:, 6144:6144 + n]
        f2 = SD[:, 7168:7168 + n]
        opn("vector", lambda e, i=ay, o=f1:
            e.tensor_scalar(out=o, in0=i, scalar1=1.9375, scalar2=None, op0=ALU.is_gt))
        opn("vector", lambda e, i=ay, o=f2:
            e.tensor_scalar(out=o, in0=i, scalar1=3.875, scalar2=None, op0=ALU.is_gt))
        t3v = ay
        dv()
        opn("vector", lambda e, i=f1, o=t3v:
            e.tensor_scalar(out=o, in0=i, scalar1=-4.0, scalar2=8.0, op0=ALU.mult, op1=ALU.add))
        inv = ay
        dv()
        opn("vector", lambda e, a=f2, b=t3v, o=inv:
            e.scalar_tensor_tensor(out=o, in0=a, scalar=-2.0, in1=b, op0=ALU.mult, op1=ALU.add))
        opn("vector", lambda e, a=yv, b=inv, o=yv: e.tensor_tensor(out=o, in0=a, in1=b, op=ALU.mult))
        opn("vector", lambda e, i=yv, o=SI[:, 0:n]: e.tensor_copy(out=o, in_=i))
        opn("vector", lambda e, i=SI[:, 0:n], o=yv: e.tensor_copy(out=o, in_=i))
        dv()
        opn("vector", lambda e, i=f1, o=f1:
            e.tensor_scalar(out=o, in0=i, scalar1=0.125, scalar2=0.125, op0=ALU.mult, op1=ALU.add))
        stp = f2
        dv()
        opn("vector", lambda e, a=f2, b=f1, o=stp:
            e.scalar_tensor_tensor(out=o, in0=a, scalar=0.25, in1=b, op0=ALU.mult, op1=ALU.add))
        dv()
        opn("vector", lambda e, a=yv, b=stp, o=yv: e.tensor_tensor(out=o, in0=a, in1=b, op=ALU.mult))
        dv()
        return op("vector", lambda e, a=yv.rearrange("p (a b) -> p a b", b=32),
                  s_=scl.unsqueeze(2).broadcast_to((128, nb, 32)),
                  o=dst.rearrange("p (a b) -> p a b", b=32):
                  e.tensor_tensor(out=o, in0=a, in1=s_, op=ALU.mult))

    for ch in range(8):
        guard_dma("sync")
        t1 = dma("sync", SC[:, 0:D], wff[ch])
        wt("vector", "in", t1)
        emit_quant(SC[:, 0:D], SA[:, ch * 1024:(ch + 1) * 1024], D)
    wt("tensor", "v", now("v"))
    wt("tensor", "in", t_c)
    for fi in range(8):
        wt("tensor", "g", now("g"))
        for fo in range(8):
            op("tensor", lambda e, i=SA[:, fo * 1024 + fi * 128:fo * 1024 + (fi + 1) * 128],
               o=PS_T[:, fo * 128:(fo + 1) * 128]:
               e.transpose(out=o, in_=i, identity=idn[:, :]))
        wt("vector", "t", now("t"))
        tv_ = op("vector", lambda e, i=PS_T[:, :], o=wst[:, 0:1024]: e.tensor_copy(out=o, in_=i))
        wt("gpsimd", "v", tv_)
        dma("gpsimd", w_bf["wffT"][fi], wst[:, 0:1024])

    wt("vector", "in", t_x)

    def ln_tile(tt, out_ap):
        xt = x_res[:, tt * D:(tt + 1) * D]
        opn("vector", lambda e, i=xt, o=SEf[:, 0:1]: e.reduce_sum(out=o, in_=i, axis=AX.X))
        dv()
        opn("vector", lambda e, i=SEf[:, 0:1], o=SEf[:, 1:2]:
            e.tensor_scalar(out=o, in0=i, scalar1=1.0 / 1024.0, scalar2=None, op0=ALU.mult))
        dv()
        xm = SC[:, 0:D]
        tv = op("vector", lambda e, i=xt, m=SEf[:, 1:2], o=xm:
                e.tensor_scalar(out=o, in0=i, scalar1=m, scalar2=None, op0=ALU.subtract))
        wt("scalar", "v", tv)
        tsq = op("scalar", lambda e, i=xm, o=SC[:, 1024:2048]:
                 e.activation(out=o, in_=i, func=AF.Square))
        wt("vector", "s", tsq)
        tss = op("vector", lambda e, i=SC[:, 1024:2048], o=SEf[:, 2:3]:
                 e.reduce_sum(out=o, in_=i, axis=AX.X))
        wt("scalar", "v", tss)
        t2 = op("scalar", lambda e, i=SEf[:, 2:3], o=SEf[:, 3:4]:
                e.activation(out=o, in_=i, func=AF.Sqrt, scale=1.0 / 1024.0, bias=1e-5))
        wt("vector", "s", t2)
        opn("vector", lambda e, i=SEf[:, 3:4], o=SEf[:, 4:5]: e.reciprocal(out=o, in_=i))
        dv()
        rr = op("vector", lambda e, i=xm, r=SEf[:, 4:5], o=out_ap:
                e.tensor_scalar(out=o, in0=i, scalar1=r, scalar2=None, op0=ALU.mult))
        return rr

    def emit_h_production():
        for tt in range(NT):
            wt("vector", "t", now("t"))
            tln = ln_tile(tt, SD[:, 0:D])
            wt("tensor", "v", tln)
            for ch in range(8):
                op("tensor", lambda e, i=SD[:, ch * 128:(ch + 1) * 128],
                   o=PS_T[:, ch * 128:(ch + 1) * 128]:
                   e.transpose(out=o, in_=i, identity=idn[:, :]))
            wt("vector", "t", now("t"))
            for ch in range(8):
                op("vector", lambda e, i=PS_T[:, ch * 128:(ch + 1) * 128],
                   o=SA[:, ch * 2048 + tt * 128:ch * 2048 + (tt + 1) * 128]:
                   e.tensor_copy(out=o, in_=i))
        wt("gpsimd", "v", now("v"))
        h_own = SA[:, 0:16384].rearrange("p (c t) -> p c t", c=8)
        sw_dma(lambda e, p: e.dma_start(out=h_sh[p].rearrange("c p t -> p c t"),
                                        in_=h_own).then_inc(sem["g"], 16))
        sw_dma(lambda e, p: e.dma_start(
            out=h_full[:, :, p * SH:(p + 1) * SH].rearrange("c p t -> p c t"),
            in_=h_own).then_inc(sem["g"], 16))
        wt("gpsimd", "g", now("g"))
        bv = barrier()
        wt("gpsimd", "bar", bv)
        sw_dma(lambda e, p: e.dma_start(out=h_full[0:4, :, (1 - p) * SH:(2 - p) * SH],
                                        in_=h_sh[1 - p][4:8]).then_inc(sem["g"], 16))
        tg = sw_dma(lambda e, p: e.dma_start(out=h_full[4:8, :, (1 - p) * SH:(2 - p) * SH],
                                             in_=h_sh[1 - p][0:4]).then_inc(sem["g"], 16))
        return tg

    def emit_vproj(wname, dst, t_hready):
        for ch in range(8):
            guard_dma("sync")
            dma("sync", SD[:, ch * 512:(ch + 1) * 512], w_bf[wname][ch], sname="in2")
        wt("sync", "g", t_hready)
        wst_rd = [now("t"), now("t")]
        smb_wb = 0
        for tt in range(NTF):
            buf = tt % 2
            wt("sync", "t", wst_rd[buf])
            th = dma("sync", wst[:, buf * 1024:(buf + 1) * 1024].rearrange(
                "p (c q) -> p c q", c=8),
                h_full[:, :, tt * 128:(tt + 1) * 128].rearrange("c p q -> p c q"),
                sname="in2")
            wt("tensor", "in2", th)
            for ch in range(8):
                op("tensor", lambda e, i=wst[:, buf * 1024 + ch * 128:buf * 1024 + (ch + 1) * 128],
                   r=SD[:, ch * 512:(ch + 1) * 512], o=PS_C[:, :], s0=(ch == 0), s1=(ch == 7):
                   e.matmul(out=o, lhsT=i, rhs=r, start=s0, stop=s1))
            wst_rd[buf] = now("t")
            wt("scalar", "t", now("t"))
            wt("scalar", "g", smb_wb)
            te = op("scalar", lambda e, i=PS_C[:, :], o=smb[:, :]:
                    e.activation(out=o, in_=i, func=AF.Identity))
            wt("gpsimd", "s", te)
            smb_wb = dma("gpsimd", dst[tt], smb[:, :])
            wt("tensor", "s", te)
        return now("g")

    def emit_kqproj_b(t_hready):
        for ch in range(8):
            guard_dma("sync")
            dma("sync", SD[:, ch * 512:(ch + 1) * 512], w_bf["wk_b"][ch], sname="in2")
            dma("sync", SD[:, 4096 + ch * 512:4096 + (ch + 1) * 512], w_bf["wv_b"][ch] if False else w_bf["wq_b"][ch], sname="in2")
        wt("sync", "g", t_hready)
        wst_rd = [now("t")] * 4
        sa_wb = [t_hready] * 4
        ps_ev = [0] * 4
        for kq in range(2):
            base = 0 if kq == 0 else 4096
            dd = kb_d if kq == 0 else qb_d
            for sl in range(8):
                for ch in range(8):
                    buf = ch % 4
                    eng, sn = ("sync", "in2") if ch % 2 == 0 else ("gpsimd", "g")
                    wt(eng, "t", wst_rd[buf])
                    th = dma(eng, wst[:, buf * 512:(buf + 1) * 512],
                             h_full[ch, :, sl * 512:(sl + 1) * 512], sname=sn)
                    wt("tensor", sn, th)
                    if ch == 0:
                        for pr in range(4):
                            wt("tensor", "s", ps_ev[pr])
                    for pr in range(4):
                        op("tensor", lambda e,
                           i=SD[:, base + ch * 512 + pr * 128:base + ch * 512 + (pr + 1) * 128],
                           r=wst[:, buf * 512:(buf + 1) * 512],
                           o=PS_BIG[:, pr * 512:(pr + 1) * 512],
                           s0=(ch == 0), s1=(ch == 7):
                           e.matmul(out=o, lhsT=i, rhs=r, start=s0, stop=s1))
                    wst_rd[buf] = now("t")
                for pr in range(4):
                    wt("scalar", "t", now("t"))
                    wt("scalar", "g", sa_wb[pr])
                    te = op("scalar", lambda e, i=PS_BIG[:, pr * 512:(pr + 1) * 512],
                            o=SA[:, pr * 512:(pr + 1) * 512],
                            sc=(1.0 if kq == 0 else 0.125):
                            e.activation(out=o, in_=i, func=AF.Identity, scale=sc))
                    ps_ev[pr] = te
                    wt("gpsimd", "s", te)
                    sa_wb[pr] = dma("gpsimd", dd[pr][:, sl * 512:(sl + 1) * 512],
                                    SA[:, pr * 512:(pr + 1) * 512])
        return now("g")

    def emit_trig_a(t_hready):
        wt("sync", "g", t_hready)
        n = 1024
        for ch in range(4):
            for blk in range(4):
                guard_dma("sync")
                th = dma("sync", SD[:, 0:n], h_full[ch, :, blk * n:(blk + 1) * n], sname="in2")
                for side in range(2):
                    iw, ib = (0, 4) if side == 0 else (8, 12)
                    wt("scalar", "in2", th)
                    t1 = op("scalar", lambda e, i=SD[:, 0:n], o=SC[:, 0:n],
                            w=prw[:, iw + ch:iw + ch + 1], b=prw[:, ib + ch:ib + ch + 1]:
                            e.activation(out=o, in_=i, func=AF.Identity, scale=w, bias=b))
                    wt("vector", "s", t1)
                    opn("vector", lambda e, i=SC[:, 0:n], o=SI[:, 0:n]: e.tensor_copy(out=o, in_=i))
                    dv()
                    kf = SC[:, 1024:1024 + n]
                    opn("vector", lambda e, i=SI[:, 0:n], o=kf: e.tensor_copy(out=o, in_=i))
                    dv()
                    gp = SC[:, 2048:2048 + n]
                    opn("vector", lambda e, i=kf, o=gp:
                        e.tensor_scalar(out=o, in0=i, scalar1=2048.0, scalar2=None, op0=ALU.is_gt))
                    dv()
                    opn("vector", lambda e, a=gp, b=kf, o=kf:
                        e.scalar_tensor_tensor(out=o, in0=a, scalar=-4096.0, in1=b,
                                               op0=ALU.mult, op1=ALU.add))
                    dv()
                    opn("vector", lambda e, i=kf, o=gp:
                        e.tensor_scalar(out=o, in0=i, scalar1=-2048.0, scalar2=None, op0=ALU.is_lt))
                    dv()
                    opn("vector", lambda e, a=gp, b=kf, o=kf:
                        e.scalar_tensor_tensor(out=o, in0=a, scalar=4096.0, in1=b,
                                               op0=ALU.mult, op1=ALU.add))
                    dv()
                    k2 = SC[:, 0:n]
                    opn("vector", lambda e, i=kf, o=gp:
                        e.tensor_scalar(out=o, in0=i, scalar1=1024.0, scalar2=None, op0=ALU.is_gt))
                    dv()
                    tv = op("vector", lambda e, a=gp, b=kf, o=k2:
                            e.scalar_tensor_tensor(out=o, in0=a, scalar=-4096.0, in1=b,
                                                   op0=ALU.mult, op1=ALU.add))
                    wt("scalar", "v", tv)
                    sn = SD[:, 1024:1024 + n]
                    cs = SD[:, 2048:2048 + n]
                    op("scalar", lambda e, i=kf, o=sn:
                       e.activation(out=o, in_=i, func=AF.Sin, scale=DELTA))
                    t3 = op("scalar", lambda e, i=k2, o=cs:
                            e.activation(out=o, in_=i, func=AF.Sin, scale=DELTA, bias=BIAS_COS))
                    wt("gpsimd", "s", t3)
                    dd = qa_d if side == 0 else ka_d
                    dma("gpsimd", dd[2 * ch][0:64, blk * n:(blk + 1) * n], cs[0:64, :])
                    dma("gpsimd", dd[2 * ch][64:128, blk * n:(blk + 1) * n], sn[0:64, :])
                    dma("gpsimd", dd[2 * ch + 1][0:64, blk * n:(blk + 1) * n], cs[64:128, :])
                    tg = dma("gpsimd", dd[2 * ch + 1][64:128, blk * n:(blk + 1) * n], sn[64:128, :])
                    wt("scalar", "g", tg)
                    wt("vector", "g", tg)
        return now("g")

    def emit_o_exchange():
        wt("gpsimd", "v", now("v"))
        wt("gpsimd", "g", now("g"))
        om4 = OM.rearrange("p (c t) -> p c t", c=4)
        sw_dma(lambda e, p: e.dma_start(
            out=o_sh[4 * p:4 * (p + 1)].rearrange("c p t -> p c t"),
            in_=om4).then_inc(sem["g"], 16))
        bv = barrier()
        wt("gpsimd", "bar", bv)
        def _oread(e, p):
            for c in range(4):
                e.dma_start(out=SA[:, c * 2048:(c + 1) * 2048],
                            in_=o_sh[4 * p + c][:, p * SH:(p + 1) * SH]).then_inc(sem["g"], 16)
                e.dma_start(out=SA[:, 8192 + c * 2048:8192 + (c + 1) * 2048],
                            in_=o_sh[4 * (1 - p) + c][:, p * SH:(p + 1) * SH]).then_inc(sem["g"], 16)
        cnt["g"] += 16 * 8
        def _run_oread(e):
            for arm in e.switch_core_id(n=128):
                _oread(e, (int(arm) % 128) % 2)
        em("gpsimd", _run_oread)
        tg = cnt["g"]
        return tg

    def emit_gemm_residual(wname, rhs_fn, bias, t_rhs_ready):
        wt("tensor", "g", t_rhs_ready)
        for do in range(8):
            guard_dma("sync")
            tw = 0
            for ch in range(8):
                tw = dma("sync", wst[:, ch * 128:(ch + 1) * 128],
                         w_bf[wname][ch][:, do * 128:(do + 1) * 128], sname="in2")
            wt("tensor", "in2", tw)
            for ts4 in range(4):
                wt("tensor", "s", now("s"))
                for ch in range(8):
                    op("tensor", lambda e, i=wst[:, ch * 128:(ch + 1) * 128],
                       r=rhs_fn(ch)[:, ts4 * 512:(ts4 + 1) * 512], o=PS_C[:, :],
                       s0=(ch == 0), s1=(ch == 7):
                       e.matmul(out=o, lhsT=i, rhs=r, start=s0, stop=s1))
                wt("scalar", "t", now("t"))
                if bias is not None:
                    te = op("scalar", lambda e, i=PS_C[:, :], o=smb[:, :], b=bias[:, do:do + 1]:
                            e.activation(out=o, in_=i, func=AF.Identity, bias=b))
                else:
                    te = op("scalar", lambda e, i=PS_C[:, :], o=smb[:, :]:
                            e.activation(out=o, in_=i, func=AF.Identity))
                wt("tensor", "s", te)
                wt("tensor", "v", now("v"))
                for q in range(4):
                    op("tensor", lambda e, i=smb[:, q * 128:(q + 1) * 128],
                       o=PS_T[:, q * 128:(q + 1) * 128]:
                       e.transpose(out=o, in_=i, identity=idn[:, :]))
                wt("vector", "t", now("t"))
                for q in range(4):
                    tt_ = ts4 * 4 + q
                    op("vector", lambda e,
                       a=x_res[:, tt_ * D + do * 128:tt_ * D + (do + 1) * 128],
                       b=PS_T[:, q * 128:(q + 1) * 128],
                       o=x_res[:, tt_ * D + do * 128:tt_ * D + (do + 1) * 128]:
                       e.tensor_tensor(out=o, in0=a, in1=b, op=ALU.add))

    def emit_attn_a():
        t_h = emit_h_production()
        t_v = emit_vproj("vw_a", va_d, t_h)
        t_t = emit_trig_a(t_h)
        half_s = [0, 0]
        half_v = [0, 0]
        pv_last = [0, 0, 0, 0]
        t_sd_free = 0
        for h in range(8):
            wt("sync", "g", t_t)
            wt("sync", "t", t_sd_free)
            wt("gpsimd", "t", t_sd_free)
            tk_ch = []
            tq_ch = []
            for c in range(8):
                tk_ch.append(dma("sync", SD[:, c * 512:(c + 1) * 512],
                                 ka_d[h][:, c * 512:(c + 1) * 512], sname="in2"))
                tq_ch.append(dma("gpsimd", SD[:, 4096 + c * 512:4096 + (c + 1) * 512],
                                 qa_d[h][:, c * 512:(c + 1) * 512], sname="g"))
            for qc in range(8):
                nkt = 4 * qc + 4
                wt("tensor", "in2", tk_ch[qc])
                wt("tensor", "g", tq_ch[qc])
                for g in range((nkt + 1) // 2):
                    half = g % 2
                    base = half * 1024
                    kts = [2 * g + j for j in range(2) if 2 * g + j < nkt]
                    wt("tensor", "v", half_v[half])
                    wt("tensor", "s", half_s[half])
                    for j, kt in enumerate(kts):
                        op("tensor", lambda e, i=SD[:, kt * 128:(kt + 1) * 128],
                           r=SD[:, 4096 + qc * 512:4096 + (qc + 1) * 512],
                           o=PS_BIG[:, base + j * 512:base + (j + 1) * 512]:
                           e.matmul(out=o, lhsT=i, rhs=r, start=True, stop=True))
                    wt("scalar", "t", now("t"))
                    ts = op("scalar", lambda e, i=PS_BIG[:, base:base + len(kts) * 512],
                            o=SA[:, kts[0] * 512:(kts[0] + len(kts)) * 512]:
                            e.activation(out=o, in_=i, func=AF.Exp, scale=SCALE_A))
                    half_s[half] = ts
                    half_v[half] = now("v")
                    for kt in kts:
                        if kt >= 4 * qc:
                            r = kt - 4 * qc
                            wt("vector", "s", ts)
                            half_v[half] = op("vector",
                               lambda e, a=SA[:, kt * 512:(kt + 1) * 512],
                               m=msk[:, r * 512:(r + 1) * 512],
                               o=SA[:, kt * 512:(kt + 1) * 512]:
                               e.tensor_tensor(out=o, in0=a, in1=m, op=ALU.mult))
                if qc == 7:
                    t_sd_free = now("t")
                wt("tensor", "v", now("v"))
                wt("tensor", "s", now("s"))
                for kt in range(nkt):
                    buf = kt % 4
                    wt("sync", "t", pv_last[buf])
                    tvd = dma("sync", pvs[:, buf * 65:buf * 65 + 64],
                              va_d[kt][:, h * 64:(h + 1) * 64], sname="in2")
                    wt("tensor", "in2", tvd)
                    pv_last[buf] = op("tensor", lambda e, i=pvs[:, buf * 65:(buf + 1) * 65],
                       r=SA[:, kt * 512:(kt + 1) * 512], o=PS_A[0:65, :],
                       s0=(kt == 0), s1=(kt == nkt - 1):
                       e.matmul(out=o, lhsT=i, rhs=r, start=s0, stop=s1))
                wt("vector", "t", now("t"))
                opn("vector", lambda e, i=PS_A[64:65, :], o=smf[0:1, :]:
                    e.reciprocal(out=o, in_=i))
                dv()
                tvr = op("vector", lambda e, i=PS_A[0:64, :], o=smb[0:64, :]:
                         e.tensor_copy(out=o, in_=i))
                wt("tensor", "v", tvr)
                tb = op("tensor", lambda e, i=on1f[:, 0:64], r=smf[0:1, :], o=PS_B[0:64, :]:
                        e.matmul(out=o, lhsT=i, rhs=r, start=True, stop=True))
                wt("vector", "t", tb)
                if h % 2 == 0:
                    dst = OM[0:64, (h // 2) * S + qc * 512:(h // 2) * S + (qc + 1) * 512]
                    op("vector", lambda e, a=smb[0:64, :], b=PS_B[0:64, :], o=dst:
                       e.tensor_tensor(out=o, in0=a, in1=b, op=ALU.mult))
                else:
                    tvo = op("vector", lambda e, a=smb[0:64, :], b=PS_B[0:64, :], o=smf[0:64, :]:
                             e.tensor_tensor(out=o, in0=a, in1=b, op=ALU.mult))
                    tvo2 = op("vector", lambda e, i=smf[0:64, :], o=smb[0:64, :]:
                              e.tensor_copy(out=o, in_=i))
                    wt("gpsimd", "v", tvo2)
                    tg = dma("gpsimd", OM[64:128, (h // 2) * S + qc * 512:
                                          (h // 2) * S + (qc + 1) * 512], smb[0:64, :])
                    wt("vector", "g", tg)
                wt("tensor", "v", now("v"))
        t_o = emit_o_exchange()
        emit_gemm_residual("wo_a", lambda ch: SA[:, ch * 2048:(ch + 1) * 2048], None, t_o)

    def emit_ffn():
        for tt in range(NT):
            wt("vector", "t", now("t"))
            wt("vector", "s", now("s"))
            tln = ln_tile(tt, SC[:, 0:D])
            emit_quant(SC[:, 0:D], SD[:, 0:D], D)
            wt("tensor", "v", now("v"))
            for ch in range(8):
                op("tensor", lambda e, i=SD[:, ch * 128:(ch + 1) * 128],
                   o=PS_T[:, ch * 128:(ch + 1) * 128]:
                   e.transpose(out=o, in_=i, identity=idn[:, :]))
            wt("vector", "t", now("t"))
            for ch in range(8):
                op("vector", lambda e, i=PS_T[:, ch * 128:(ch + 1) * 128],
                   o=SA[:, ch * 2048 + tt * 128:ch * 2048 + (tt + 1) * 128]:
                   e.tensor_copy(out=o, in_=i))
        emit_gemm_residual("wffT", lambda ch: SA[:, ch * 2048:(ch + 1) * 2048], bfc, now("g"))

    def emit_attn_b():
        t_h = emit_h_production()
        emit_vproj("wv_b", vb_d, t_h)
        t_kq = emit_kqproj_b(t_h)
        pvb_last = [0, 0, 0, 0]
        ps_rd_s = [now("s"), now("s")]
        ps_rd_v = [now("v"), now("v")]
        sa_rd_v = [0, 0]
        pay_rd = [0, 0]
        pv_ev = 0
        t_sd_free = 0
        t_kq = now("g")
        for pr in range(4):
            wt("sync", "g", t_kq)
            wt("sync", "t", t_sd_free)
            wt("gpsimd", "t", t_sd_free)
            tk_ch = 0
            tq_ch = []
            for c in range(8):
                tk_ch = dma("sync", SD[:, c * 512:(c + 1) * 512],
                            kb_d[pr][:, c * 512:(c + 1) * 512], sname="in2")
                tq_ch.append(dma("gpsimd", SD[:, 4096 + c * 512:4096 + (c + 1) * 512],
                                 qb_d[pr][:, c * 512:(c + 1) * 512], sname="g"))
            wt("tensor", "in2", tk_ch)
            for h01 in range(2):
                for qc in range(8):
                    q0 = 4096 + qc * 512
                    wt("tensor", "g", tq_ch[qc])
                    if bstage < 1:
                        continue
                    tile = 0
                    for qt2 in range(4):
                        qt = 4 * qc + qt2
                        for hf in range(2):
                            for kh in range(2):
                                pp = tile % 2
                                tile += 1
                                wt("tensor", "v", ps_rd_v[pp])
                                wt("tensor", "s", ps_rd_s[pp])
                                for ksl in range(2):
                                    op("tensor", lambda e,
                                       i=SD[:, 4096 + qt * 128:4096 + (qt + 1) * 128][h01 * 64:(h01 + 1) * 64, :],
                                       r=SD[:, hf * 2048 + kh * 1024 + ksl * 512:hf * 2048 + kh * 1024 + (ksl + 1) * 512][h01 * 64:(h01 + 1) * 64, :],
                                       o=PS_BIG[:, pp * 1024 + ksl * 512:pp * 1024 + (ksl + 1) * 512]:
                                       e.matmul(out=o, lhsT=i, rhs=r, start=True, stop=True))
                                t_mm = now("t")
                                wt("vector", "t", t_mm)
                                ps_rd_v[pp] = op("vector", lambda e,
                                   i=PS_BIG[:, pp * 1024:(pp + 1) * 1024].rearrange("p (a b) -> p a b", b=64),
                                   o=SEb[:, qt2 * 64 + hf * 32 + kh * 16:qt2 * 64 + hf * 32 + kh * 16 + 16]:
                                   e.reduce_max(out=o, in_=i, axis=AX.X))
                                wt("scalar", "t", t_mm)
                                wt("scalar", "v", ps_rd_v[pp])
                                tex = op("scalar", lambda e, i=PS_BIG[:, pp * 1024:(pp + 1) * 1024],
                                         o=SA[:, pp * 1024:(pp + 1) * 1024]:
                                         e.activation(out=o, in_=i, func=AF.Exp))
                                ps_rd_s[pp] = tex
                                wt("vector", "s", tex)
                                sa_rd_v[pp] = op("vector", lambda e,
                                   i=SA[:, pp * 1024:(pp + 1) * 1024].rearrange("p (a b) -> p a b", b=64),
                                   o=SEf[:, 768 + qt2 * 64 + hf * 32 + kh * 16:768 + qt2 * 64 + hf * 32 + kh * 16 + 16]:
                                   e.reduce_sum(out=o, in_=i, axis=AX.X))
                    if bstage < 4:
                        continue
                    LC = SEf[:, 0:256]
                    MR = SEf[:, 256:512]
                    LNL = SEf[:, 512:768]
                    D4 = SEf[:, 768:1024]
                    dv()
                    t_lc = 0
                    for qt2 in range(4):
                        t_lc = op("vector", lambda e, i=D4[:, qt2 * 64:(qt2 + 1) * 64],
                                  o=LC[:, qt2 * 64:(qt2 + 1) * 64]:
                                  e.tensor_tensor_scan(out=o, data0=i, data1=i, initial=0.0,
                                                       op0=ALU.add, op1=ALU.bypass))
                        dv()
                    for qt2 in range(4):
                        opn("vector", lambda e, i=SEb[:, qt2 * 64:(qt2 + 1) * 64],
                            o=MR[:, qt2 * 64:(qt2 + 1) * 64]:
                            e.tensor_tensor_scan(out=o, data0=i, data1=i, initial=-1e30,
                                                 op0=ALU.max, op1=ALU.bypass))
                        dv()
                    if bstage == 41:
                        continue
                    wt("scalar", "v", t_lc)
                    t_ln = op("scalar", lambda e, i=LC, o=LNL:
                              e.activation(out=o, in_=i, func=AF.Ln))
                    wt("vector", "s", t_ln)
                    dv()
                    opn("vector", lambda e, a=MR, b=LNL, o=D4:
                        e.tensor_tensor(out=o, in0=a, in1=b, op=ALU.subtract))
                    dv()
                    if bstage == 42:
                        continue
                    for qt2 in range(4):
                        opn("vector", lambda e, i=D4[:, qt2 * 64:(qt2 + 1) * 64],
                            o=LC[:, qt2 * 64:(qt2 + 1) * 64]:
                            e.tensor_tensor_scan(out=o, data0=i, data1=i, initial=0.0,
                                                 op0=ALU.add, op1=ALU.bypass))
                        dv()
                    opn("vector", lambda e, a=LC, b=D4, o=MR:
                        e.tensor_tensor(out=o, in0=a, in1=b, op=ALU.subtract))
                    dv()
                    if bstage == 43:
                        continue
                    for qt2 in range(4):
                        opn("vector", lambda e, a=MR[:, qt2 * 64:(qt2 + 1) * 64],
                            s_=LC[:, qt2 * 64 + 63:qt2 * 64 + 64].broadcast_to((128, 64)),
                            o=MR[:, qt2 * 64:(qt2 + 1) * 64]:
                            e.scalar_tensor_tensor(out=o, in0=a, scalar=-1.0, in1=s_,
                                                   op0=ALU.mult, op1=ALU.add))
                        dv()
                    for qt2 in range(4):
                        opn("vector", lambda e, i=LNL[:, qt2 * 64 + 63:qt2 * 64 + 64],
                            o=SEf[:, 4 + qt2:5 + qt2]:
                            e.tensor_scalar(out=o, in0=i, scalar1=-1.0, scalar2=None, op0=ALU.mult))
                    for qt2 in range(4):
                        opn("vector", lambda e, i=D4[:, qt2 * 64 + 63:qt2 * 64 + 64],
                            o=SEf[:, qt2:qt2 + 1]: e.tensor_copy(out=o, in_=i))
                    dv()
                    if bstage == 44:
                        continue
                    for qt2 in range(4):
                        wt("vector", "t", pay_rd[0])
                        if bstage == 448:
                            opn("vector", lambda e, i=MR[:, qt2 * 64:(qt2 + 1) * 64],
                                o=smf[:, 0:64]: e.tensor_copy(out=o, in_=i))
                            continue
                        if bstage == 449:
                            opn("vector", lambda e, i=MR[:, qt2 * 64:(qt2 + 1) * 64],
                                o=smf[:, 0:64]:
                                e.tensor_scalar(out=o, in0=i, scalar1=-1.0, scalar2=None,
                                                op0=ALU.subtract))
                            continue
                        opn("vector", lambda e, i=MR[:, qt2 * 64:(qt2 + 1) * 64],
                            c=SEf[:, qt2:qt2 + 1],
                            o=(smf[:, 0:64] if bstage == 450 else pay[:, 0:64]):
                            e.tensor_scalar(out=o, in0=i, scalar1=c, scalar2=None,
                                            op0=ALU.subtract))
                        if bstage in (450, 451):
                            continue
                        opn("vector", lambda e, i=SEf[:, 4 + qt2:5 + qt2],
                            o=pay[:, 64:65]: e.tensor_copy(out=o, in_=i))
                        if bstage == 452:
                            continue
                        dv()
                        tv3 = op("vector", lambda e, a=SEf[:, 4 + qt2:5 + qt2],
                                 b=pay[:, 64:65], o=pay[:, 65:66]:
                                 e.tensor_tensor(out=o, in0=a, in1=b, op=ALU.subtract))
                        if bstage == 45:
                            continue
                        wt("tensor", "v", tv3)
                        tt4 = op("tensor", lambda e, i=pay[:, 0:128],
                                 o=PS_T[:, qt2 * 128:(qt2 + 1) * 128]:
                                 e.transpose(out=o, in_=i, identity=idn[:, :]))
                        pay_rd[0] = tt4
                        if bstage == 46:
                            continue
                        wt("vector", "t", tt4)
                        op("vector", lambda e, i=PS_T[:, qt2 * 128:(qt2 + 1) * 128][0:66, :],
                           o=smb[0:66, qt2 * 128:(qt2 + 1) * 128]:
                           e.tensor_copy(out=o, in_=i))
                    if bstage < 5:
                        continue
                    dh_s = [now("s"), now("s")]
                    wt("tensor", "v", now("v"))
                    for g in range(16):
                        half = g % 2
                        base = half * 1024
                        wt("tensor", "s", dh_s[half])
                        for j in range(2):
                            kt = 2 * g + j
                            op("tensor", lambda e,
                               i=SD[:, kt * 128:(kt + 1) * 128][h01 * 64:(h01 + 1) * 64, :],
                               r=SD[:, q0:q0 + 512][h01 * 64:(h01 + 1) * 64, :],
                               o=PS_BIG[:, base + j * 512:base + (j + 1) * 512]:
                               e.matmul(out=o, lhsT=i, rhs=r, start=True, stop=False))
                        for j in range(2):
                            kt = 2 * g + j
                            op("tensor", lambda e, i=selb[0:66, kt * 128:(kt + 1) * 128],
                               r=smb[0:66, 0:512],
                               o=PS_BIG[:, base + j * 512:base + (j + 1) * 512]:
                               e.matmul(out=o, lhsT=i, rhs=r, start=False, stop=True))
                        wt("scalar", "t", now("t"))
                        tsx = op("scalar", lambda e, i=PS_BIG[:, base:base + 1024],
                                 o=SA[:, g * 1024:(g + 1) * 1024]:
                                 e.activation(out=o, in_=i, func=AF.Exp))
                        dh_s[half] = tsx
                    ps_rd_s[0] = dh_s[0]
                    ps_rd_s[1] = dh_s[1]
                    if bstage < 60:
                        continue
                    wt("tensor", "s", now("s"))
                    wt("tensor", "v", pv_ev)
                    for kt in range(32):
                        buf = kt % 4
                        wt("sync", "t", pvb_last[buf])
                        tvd = dma("sync", vst[:, buf * 64:(buf + 1) * 64],
                                  vb_d[kt][:, (pr * 2 + h01) * 64:(pr * 2 + h01 + 1) * 64],
                                  sname="in2")
                        wt("tensor", "in2", tvd)
                        pvb_last[buf] = op("tensor", lambda e, i=vst[:, buf * 64:(buf + 1) * 64],
                           r=SA[:, kt * 512:(kt + 1) * 512], o=PS_A[0:64, :],
                           s0=(kt == 0), s1=(kt == 31):
                           e.matmul(out=o, lhsT=i, rhs=r, start=s0, stop=s1))
                    wt("vector", "t", now("t"))
                    pv_ev = op("vector", lambda e, i=PS_A[0:64, :],
                               o=OM[h01 * 64:(h01 + 1) * 64, pr * S + qc * 512:pr * S + (qc + 1) * 512]:
                               e.tensor_copy(out=o, in_=i))
                t_sd_free = now("t")
        t_o = emit_o_exchange()
        emit_gemm_residual("wo_b", lambda ch: SA[:, ch * 2048:(ch + 1) * 2048], None, t_o)

    if n_sub == -3:
        emit_attn_b()
    if n_sub >= 1:
        emit_attn_a()
    if n_sub >= 2:
        emit_ffn()
    if n_sub >= 3:
        emit_attn_b()
    if n_sub >= 4:
        emit_ffn()

    wt("gpsimd", "v", now("v"))
    for tt in range(NT):
        dma("gpsimd", y_out.rearrange("(n p) d -> n p d", p=128)[tt],
            x_res[:, tt * D:(tt + 1) * D])
    wt("gpsimd", "g", now("g"))

    blk = st.enter_context(nc.Block())

    @blk.sync
    def _(e):
        for fn in EM["sync"]:
            fn(e)

    @blk.vector
    def _(e):
        for fn in EM["vector"]:
            fn(e)

    @blk.scalar
    def _(e):
        for fn in EM["scalar"]:
            fn(e)

    @blk.tensor
    def _(e):
        for fn in EM["tensor"]:
            fn(e)

    @blk.gpsimd
    def _(e):
        for fn in EM["gpsimd"]:
            fn(e)

    st.close()
    global _LAST_CNT
    _LAST_CNT = dict(cnt)
    _LAST_CNT["_em"] = {k: len(v) for k, v in EM.items()}
    try:
        _LAST_CNT["_insts"] = len(list(nc.all_instructions()))
    except Exception:
        pass
    return nc


def _host_prep(inputs, core):
    p = core % 2
    b = core // 2
    perm = np.arange(D)
    if p == 1:
        perm = np.concatenate([perm[512:], perm[:512]])
    hsl = slice(p * 8, p * 8 + 8)

    x = inputs["L0_attn_a_x"]
    xc = np.ascontiguousarray(x[b, p * SH:(p + 1) * SH, :][:, perm], np.float32)

    def chunkT(w):
        return np.ascontiguousarray(w.reshape(8, 128, -1), np.float32)

    wq = inputs["L0_attn_a_w_q"][hsl].reshape(512)
    bq = inputs["L0_attn_a_b_q"][hsl].reshape(512)
    wk = inputs["L0_attn_a_w_k"][hsl].reshape(512)
    bk = inputs["L0_attn_a_b_k"][hsl].reshape(512)
    tprm = np.stack([wq.reshape(4, 128), bq.reshape(4, 128),
                     wk.reshape(4, 128), bk.reshape(4, 128)], 0)
    tprm = np.ascontiguousarray(tprm.reshape(16, 128).T, np.float32)

    vw = inputs["L0_attn_a_v_w"][p * 512:(p + 1) * 512, :][:, perm]
    vw_a = chunkT(np.ascontiguousarray(vw.T))
    ow = inputs["L0_attn_a_out_w"]
    wo_a = chunkT(np.ascontiguousarray(ow.T[perm][:, perm]))

    wqb = inputs["L0_attn_b_Wq"][perm][:, p * 512:(p + 1) * 512]
    wkb = inputs["L0_attn_b_Wk"][perm][:, p * 512:(p + 1) * 512]
    wvb = inputs["L0_attn_b_Wv"][perm][:, p * 512:(p + 1) * 512]
    wob = inputs["L0_attn_b_Wo"][perm][:, perm]
    wff = inputs["L0_ffn_weight"][perm][:, perm]
    bff = inputs["L0_ffn_bias"][perm]

    masks = np.zeros((4, 128, 512), ml_dtypes.bfloat16)
    for r in range(4):
        i_ = np.arange(128)[:, None] + 128 * r
        j_ = np.arange(512)[None, :]
        masks[r] = (i_ <= j_).astype(ml_dtypes.bfloat16)
    ident = np.eye(128, dtype=ml_dtypes.bfloat16)
    ind2 = np.zeros((2, 128), ml_dtypes.bfloat16)
    ind2[0, 0:64] = 1
    ind2[1, 64:128] = 1
    ones2 = np.ones((2, 128), ml_dtypes.bfloat16)
    ones1 = np.ones((1, 128), ml_dtypes.bfloat16)
    bigm = np.zeros((128, 192), ml_dtypes.bfloat16)
    bigm[0:64, 126] = 1
    bigm[64:128, 127] = 1
    selb = np.zeros((128, 4096), ml_dtypes.bfloat16)
    for kt in range(32):
        for c in range(128):
            selb[2 * kt + (c >= 64), kt * 128 + c] = 1
        selb[64, kt * 128:(kt + 1) * 128] = 1
        selb[65, kt * 128:(kt + 1) * 128] = 1

    return {
        "x_in": xc, "tprm": tprm,
        "bffi": np.ascontiguousarray(bff.reshape(8, 128).T, np.float32),
        "vw_a": vw_a, "wo_a": wo_a,
        "wq_b": chunkT(wqb), "wk_b": chunkT(wkb), "wv_b": chunkT(wvb),
        "wo_b": chunkT(wob), "wff": chunkT(wff),
        "masks_i": masks, "ident_i": ident, "ind2_i": ind2, "ones2_i": ones2,
        "ones1_i": ones1, "ones1f_i": np.ones((1, 128), np.float32), "bigm_i": bigm,
        "selb_i": selb,
        "cvals_i": np.ascontiguousarray(
            np.broadcast_to(np.array([1e-5, BIAS_COS, 0.0, 0.0], np.float32), (128, 4))),
    }, perm, b, p


def kernel(**inputs):
    key = ("nc", SUBLAYERS, ATTNB_STAGE)
    if key not in _CACHE:
        _CACHE[key] = build_nc(SUBLAYERS, ATTNB_STAGE)
    nc = _CACHE[key]
    in_maps = []
    metas = []
    for c in range(8):
        m, perm, b, p = _host_prep(inputs, c)
        in_maps.append(m)
        metas.append((perm, b, p))
    res = run_bass_kernel_spmd(nc, in_maps, core_ids=list(range(8)))
    out = np.empty((B, S, D), np.float32)
    for c in range(8):
        perm, b, p = metas[c]
        inv = np.argsort(perm)
        out[b, p * SH:(p + 1) * SH, :] = res.results[c]["y_out"][:, inv]
    return out


if __name__ == "__main__":
    nc = build_nc()
    print("built ok")


# version 05
# speedup vs baseline: 2.1230x; 1.1012x over previous
import contextlib
import numpy as np
import ml_dtypes
import concourse.bass as bass
import concourse.mybir as mybir
from concourse.bass_utils import run_bass_kernel_spmd

F32 = mybir.dt.float32
BF16 = mybir.dt.bfloat16
I32 = mybir.dt.int32
AF = mybir.ActivationFunctionType
ALU = mybir.AluOpType
AX = mybir.AxisListType

B, S, D, H, DH = 4, 4096, 1024, 16, 64
SH = S // 2
NT = SH // 128
NTF = S // 128
NKT = S // 128
C_LUT = float(np.float32(4096.0 / (2.0 * np.pi)))
DELTA = float(np.float32(np.float32(2.0 * np.pi) / 4096.0))
BIAS_COS = float(np.float32(1024.0) * np.float32(DELTA))
SCALE_A = float(np.float32(1.0 / np.sqrt(np.float32(128.0))))
SUBLAYERS = 4
ATTNB_STAGE = 60
_LAST_CNT = None
_CACHE = {}


def build_nc(n_sub=4, bstage=None):
    if bstage is None:
        bstage = ATTNB_STAGE
    nc = bass.Bass("TRN2", target_bir_lowering=False)

    ein = lambda n, sh, dt=F32: nc.dram_tensor(n, sh, dt, kind="ExternalInput").ap()
    x_in = ein("x_in", [SH, D])
    tprm = ein("tprm", [128, 16])
    bffi = ein("bffi", [128, 8])
    vw_a = ein("vw_a", [8, 128, 512])
    wo_a = ein("wo_a", [8, 128, 1024])
    wq_b = ein("wq_b", [8, 128, 512])
    wk_b = ein("wk_b", [8, 128, 512])
    wv_b = ein("wv_b", [8, 128, 512])
    wo_b = ein("wo_b", [8, 128, 1024])
    wff = ein("wff", [8, 128, 1024])
    masks_i = ein("masks_i", [4, 128, 512], BF16)
    ident_i = ein("ident_i", [128, 128], BF16)
    ind2_i = ein("ind2_i", [2, 128], BF16)
    ones2_i = ein("ones2_i", [2, 128], BF16)
    ones1_i = ein("ones1_i", [1, 128], BF16)
    ones1f_i = ein("ones1f_i", [1, 128])
    bigm_i = ein("bigm_i", [128, 192], BF16)
    selb_i = ein("selb_i", [128, 4096], BF16)
    cvals_i = ein("cvals_i", [128, 4])
    y_out = nc.dram_tensor("y_out", [SH, D], F32, kind="ExternalOutput").ap()

    din = lambda n, sh, dt=BF16, **kw: nc.dram_tensor(n, sh, dt, kind="Internal", **kw).ap()
    h_sh = din("h_sh", [2, 8, 128, SH], BF16, addr_space="Shared")
    o_sh = din("o_sh", [8, 128, S], BF16, addr_space="Shared")
    h_full = din("h_full", [8, 128, S])
    qa_d = din("qa_d", [8, 128, S])
    ka_d = din("ka_d", [8, 128, S])
    va_d = din("va_d", [NKT, 128, 512])
    qb_d = din("qb_d", [4, 128, S])
    kb_d = din("kb_d", [4, 128, S])
    vb_d = din("vb_d", [NKT, 128, 512])
    WCOLS = {"vw_a": 512, "wo_a": 1024, "wq_b": 512, "wk_b": 512,
             "wv_b": 512, "wo_b": 1024, "wffT": 1024}
    w_bf = {n: din("bf_" + n, [8, 128, c]) for n, c in WCOLS.items()}
    bar_i = din("bar_i", [1, 64], F32)
    bar_o = din("bar_o", [2, 64], F32)

    st = contextlib.ExitStack()
    sb = lambda n, sh, dt: st.enter_context(nc.sbuf_tensor(n, sh, dt))
    pst = lambda n, sh, dt=F32: st.enter_context(nc.psum_tensor(n, sh, dt))

    x_res = sb("x_res", [128, NT * D], F32)
    SA = sb("SA", [128, 16384], BF16)
    OM = sb("OM", [128, 4 * S], BF16)
    SC = sb("SC", [128, 3 * 1024], F32)
    SD = sb("SD", [128, 8192], BF16)
    SEb = sb("SEb", [128, 1024], BF16)
    SEf = sb("SEf", [128, 1056], F32)
    SI = sb("SI", [128, 1024], I32)
    msk = sb("msk", [128, 4 * 512], BF16)
    idn = sb("idn", [128, 128], BF16)
    i2 = sb("i2", [2, 128], BF16)
    o2 = sb("o2", [2, 128], BF16)
    on1 = sb("on1", [1, 128], BF16)
    on1f = sb("on1f", [1, 128], F32)
    bgm = sb("bgm", [128, 192], BF16)
    selb = sb("selb", [128, 4096], BF16)
    prm = sb("prm", [128, 16], F32)
    prw = sb("prw", [128, 16], F32)
    bfc = sb("bfc", [128, 8], F32)
    pvs = sb("pvs", [128, 4 * 65], BF16)
    vst = sb("vst", [128, 4 * 64], BF16)
    wst = sb("wst", [128, 2 * 1024], BF16)
    smf = sb("smf", [128, 512], F32)
    smb = sb("smb", [128, 512], BF16)
    pay = sb("pay", [128, 132], BF16)
    cva = sb("cva", [128, 4], F32)

    PS_BIG = pst("PS_BIG", [128, 2048], F32)
    PS_A = pst("PS_A", [128, 512], F32)
    PS_B = pst("PS_B", [128, 512], F32)
    PS_T = pst("PS_T", [128, 1024], BF16)
    PS_C = pst("PS_C", [128, 512], F32)

    sem = {n: st.enter_context(nc.semaphore(name="sem_" + n)) for n in
           ["in", "in2", "v", "s", "t", "g", "bar"]}
    cnt = {n: 0 for n in sem}
    EM = {k: [] for k in ["sync", "vector", "scalar", "tensor", "gpsimd"]}
    ES = {"sync": "in", "vector": "v", "scalar": "s", "tensor": "t", "gpsimd": "g"}

    def em(eng, fn):
        EM[eng].append(fn)

    def dma(eng, out, in_, sname=None):
        sname = sname or ES[eng]
        cnt[sname] += 16
        em(eng, lambda e, o=out, i=in_, s=sem[sname]: e.dma_start(out=o, in_=i).then_inc(s, 16))
        return cnt[sname]

    def wt(eng, sname, val):
        if val > 0:
            em(eng, lambda e, s=sem[sname], v=val: e.wait_ge(s, v))

    def op(eng, fn):
        sname = ES[eng]
        cnt[sname] += 1
        em(eng, lambda e, f=fn, s=sem[sname]: f(e).then_inc(s, 1))
        return cnt[sname]

    def opn(eng, fn):
        em(eng, fn)

    def dv():
        em("vector", lambda e: e.drain())

    def now(s):
        return cnt[s]

    def guard_dma(eng):
        for s in ("v", "s", "t", "g"):
            wt(eng, s, cnt[s])

    def sw_dma(fn_of_p):
        cnt["g"] += 16
        def run(e, f=fn_of_p):
            for arm in e.switch_core_id(n=128):
                f(e, (int(arm) % 128) % 2)
        em("gpsimd", run)
        return cnt["g"]

    def barrier():
        cnt["bar"] += 1
        em("gpsimd", lambda e, s=sem["bar"]: e.collective_compute(
            "AllGather", ALU.bypass, replica_groups=[[0, 1], [2, 3], [4, 5], [6, 7]],
            ins=[bar_i], outs=[bar_o]).then_inc(s, 1))
        return cnt["bar"]

    for r in range(4):
        dma("sync", msk[:, r * 512:(r + 1) * 512], masks_i[r])
    dma("sync", idn[:, :], ident_i)
    dma("sync", i2[:, :], ind2_i)
    dma("sync", o2[:, :], ones2_i)
    dma("sync", on1[:, :], ones1_i)
    dma("sync", on1f[:, :], ones1f_i)
    dma("sync", bgm[:, :], bigm_i)
    dma("sync", selb[:, :], selb_i)
    dma("sync", prm[:, :], tprm)
    dma("sync", cva[:, :], cvals_i)
    nc.const_aps.register(F32, 1e-5, cva[:, 0:1])
    nc.const_aps.register(F32, BIAS_COS, cva[:, 1:2])
    t_c = dma("sync", bfc[:, :], bffi)
    for tt in range(NT):
        dma("sync", x_res[:, tt * D:(tt + 1) * D],
            x_in.rearrange("(n p) d -> n p d", p=128)[tt])
    t_x = now("in")

    opn("vector", lambda e: e.memset(pvs[:, :], 0.0))
    dv()
    for b_ in range(3):
        opn("vector", lambda e, c=b_ * 65 + 64: e.memset(pvs[:, c:c + 1], 1.0))
    op("vector", lambda e: e.memset(pvs[:, 3 * 65 + 64:3 * 65 + 65], 1.0))

    wt("scalar", "in", t_c)
    wt("vector", "in", t_c)
    for dst, s_w, s_b in [(0, 0, 4), (8, 8, 12)]:
        t1 = op("scalar", lambda e, i=prm[:, s_w:s_w + 4], o=SC[:, 0:4]:
                e.activation(out=o, in_=i, func=AF.Abs, bias=1.0))
        wt("vector", "s", t1)
        opn("vector", lambda e, i=SC[:, 0:4], o=SC[:, 4:8]: e.reciprocal(out=o, in_=i))
        dv()
        opn("vector", lambda e, i=SC[:, 4:8], o=prw[:, dst:dst + 4]:
            e.tensor_scalar(out=o, in0=i, scalar1=C_LUT, scalar2=None, op0=ALU.mult))
        op("vector", lambda e, i=prm[:, s_b:s_b + 4], o=prw[:, dst + 4:dst + 8]:
           e.tensor_scalar(out=o, in0=i, scalar1=C_LUT, scalar2=None, op0=ALU.mult))

    SRC_F32 = {"vw_a": vw_a, "wo_a": wo_a, "wq_b": wq_b, "wk_b": wk_b,
               "wv_b": wv_b, "wo_b": wo_b}
    for name, src in SRC_F32.items():
        ncol = WCOLS[name]
        for ch in range(8):
            guard_dma("sync")
            t1 = dma("sync", SC[:, 0:ncol], src[ch])
            wt("vector", "in", t1)
            t2 = op("vector", lambda e, i=SC[:, 0:ncol], o=wst[:, 0:ncol]:
                    e.tensor_copy(out=o, in_=i))
            wt("gpsimd", "v", t2)
            t3 = dma("gpsimd", w_bf[name][ch], wst[:, 0:ncol])
            wt("sync", "g", t3)
            wt("vector", "g", t3)

    def emit_quant(src, dst, n):
        nb = n // 32
        amx = SEf[:, 0:nb]
        opn("vector", lambda e, i=src.rearrange("p (a b) -> p a b", b=32), o=amx:
            e.reduce_max(out=o, in_=i, axis=AX.X, apply_absolute_value=True))
        scl = SEf[:, nb:2 * nb]
        rsc = SEf[:, 2 * nb:3 * nb]
        dv()
        opn("vector", lambda e, i=amx, o=scl:
            e.tensor_scalar(out=o, in0=i, scalar1=float(np.float32(1.0) / np.float32(7.5)),
                            scalar2=1e-30, op0=ALU.mult, op1=ALU.max))
        dv()
        opn("vector", lambda e, i=scl, o=rsc: e.reciprocal(out=o, in_=i))
        dv()
        yv = SC[:, 1024:1024 + n]
        tv = op("vector", lambda e, x=src.rearrange("p (a b) -> p a b", b=32),
                r=rsc.unsqueeze(2).broadcast_to((128, nb, 32)),
                o=yv.rearrange("p (a b) -> p a b", b=32):
                e.tensor_tensor(out=o, in0=x, in1=r, op=ALU.mult))
        ay = SC[:, 2048:2048 + n]
        wt("scalar", "v", tv)
        ts = op("scalar", lambda e, i=yv, o=ay: e.activation(out=o, in_=i, func=AF.Abs))
        wt("vector", "s", ts)
        f1 = SD[:, 6144:6144 + n]
        f2 = SD[:, 7168:7168 + n]
        opn("vector", lambda e, i=ay, o=f1:
            e.tensor_scalar(out=o, in0=i, scalar1=1.9375, scalar2=None, op0=ALU.is_gt))
        opn("vector", lambda e, i=ay, o=f2:
            e.tensor_scalar(out=o, in0=i, scalar1=3.875, scalar2=None, op0=ALU.is_gt))
        t3v = ay
        dv()
        opn("vector", lambda e, i=f1, o=t3v:
            e.tensor_scalar(out=o, in0=i, scalar1=-4.0, scalar2=8.0, op0=ALU.mult, op1=ALU.add))
        inv = ay
        dv()
        opn("vector", lambda e, a=f2, b=t3v, o=inv:
            e.scalar_tensor_tensor(out=o, in0=a, scalar=-2.0, in1=b, op0=ALU.mult, op1=ALU.add))
        opn("vector", lambda e, a=yv, b=inv, o=yv: e.tensor_tensor(out=o, in0=a, in1=b, op=ALU.mult))
        opn("vector", lambda e, i=yv, o=SI[:, 0:n]: e.tensor_copy(out=o, in_=i))
        opn("vector", lambda e, i=SI[:, 0:n], o=yv: e.tensor_copy(out=o, in_=i))
        dv()
        opn("vector", lambda e, i=f1, o=f1:
            e.tensor_scalar(out=o, in0=i, scalar1=0.125, scalar2=0.125, op0=ALU.mult, op1=ALU.add))
        stp = f2
        dv()
        opn("vector", lambda e, a=f2, b=f1, o=stp:
            e.scalar_tensor_tensor(out=o, in0=a, scalar=0.25, in1=b, op0=ALU.mult, op1=ALU.add))
        dv()
        opn("vector", lambda e, a=yv, b=stp, o=yv: e.tensor_tensor(out=o, in0=a, in1=b, op=ALU.mult))
        dv()
        return op("vector", lambda e, a=yv.rearrange("p (a b) -> p a b", b=32),
                  s_=scl.unsqueeze(2).broadcast_to((128, nb, 32)),
                  o=dst.rearrange("p (a b) -> p a b", b=32):
                  e.tensor_tensor(out=o, in0=a, in1=s_, op=ALU.mult))

    for ch in range(8):
        guard_dma("sync")
        t1 = dma("sync", SC[:, 0:D], wff[ch])
        wt("vector", "in", t1)
        emit_quant(SC[:, 0:D], SA[:, ch * 1024:(ch + 1) * 1024], D)
    wt("tensor", "v", now("v"))
    wt("tensor", "in", t_c)
    for fi in range(8):
        wt("tensor", "g", now("g"))
        for fo in range(8):
            op("tensor", lambda e, i=SA[:, fo * 1024 + fi * 128:fo * 1024 + (fi + 1) * 128],
               o=PS_T[:, fo * 128:(fo + 1) * 128]:
               e.transpose(out=o, in_=i, identity=idn[:, :]))
        wt("vector", "t", now("t"))
        tv_ = op("vector", lambda e, i=PS_T[:, :], o=wst[:, 0:1024]: e.tensor_copy(out=o, in_=i))
        wt("gpsimd", "v", tv_)
        dma("gpsimd", w_bf["wffT"][fi], wst[:, 0:1024])

    wt("vector", "in", t_x)

    def ln_tile(tt, out_ap):
        xt = x_res[:, tt * D:(tt + 1) * D]
        opn("vector", lambda e, i=xt, o=SEf[:, 0:1]: e.reduce_sum(out=o, in_=i, axis=AX.X))
        dv()
        opn("vector", lambda e, i=SEf[:, 0:1], o=SEf[:, 1:2]:
            e.tensor_scalar(out=o, in0=i, scalar1=1.0 / 1024.0, scalar2=None, op0=ALU.mult))
        dv()
        xm = SC[:, 0:D]
        tv = op("vector", lambda e, i=xt, m=SEf[:, 1:2], o=xm:
                e.tensor_scalar(out=o, in0=i, scalar1=m, scalar2=None, op0=ALU.subtract))
        wt("scalar", "v", tv)
        tsq = op("scalar", lambda e, i=xm, o=SC[:, 1024:2048]:
                 e.activation(out=o, in_=i, func=AF.Square))
        wt("vector", "s", tsq)
        tss = op("vector", lambda e, i=SC[:, 1024:2048], o=SEf[:, 2:3]:
                 e.reduce_sum(out=o, in_=i, axis=AX.X))
        wt("scalar", "v", tss)
        t2 = op("scalar", lambda e, i=SEf[:, 2:3], o=SEf[:, 3:4]:
                e.activation(out=o, in_=i, func=AF.Sqrt, scale=1.0 / 1024.0, bias=1e-5))
        wt("vector", "s", t2)
        opn("vector", lambda e, i=SEf[:, 3:4], o=SEf[:, 4:5]: e.reciprocal(out=o, in_=i))
        dv()
        rr = op("vector", lambda e, i=xm, r=SEf[:, 4:5], o=out_ap:
                e.tensor_scalar(out=o, in0=i, scalar1=r, scalar2=None, op0=ALU.mult))
        return rr

    def emit_h_production():
        for tt in range(NT):
            wt("vector", "t", now("t"))
            tln = ln_tile(tt, SD[:, 0:D])
            wt("tensor", "v", tln)
            for ch in range(8):
                op("tensor", lambda e, i=SD[:, ch * 128:(ch + 1) * 128],
                   o=PS_T[:, ch * 128:(ch + 1) * 128]:
                   e.transpose(out=o, in_=i, identity=idn[:, :]))
            wt("vector", "t", now("t"))
            for ch in range(8):
                op("vector", lambda e, i=PS_T[:, ch * 128:(ch + 1) * 128],
                   o=SA[:, ch * 2048 + tt * 128:ch * 2048 + (tt + 1) * 128]:
                   e.tensor_copy(out=o, in_=i))
        wt("gpsimd", "v", now("v"))
        h_own = SA[:, 0:16384].rearrange("p (c t) -> p c t", c=8)
        sw_dma(lambda e, p: e.dma_start(out=h_sh[p].rearrange("c p t -> p c t"),
                                        in_=h_own).then_inc(sem["g"], 16))
        sw_dma(lambda e, p: e.dma_start(
            out=h_full[:, :, p * SH:(p + 1) * SH].rearrange("c p t -> p c t"),
            in_=h_own).then_inc(sem["g"], 16))
        wt("gpsimd", "g", now("g"))
        bv = barrier()
        wt("gpsimd", "bar", bv)
        sw_dma(lambda e, p: e.dma_start(out=h_full[0:4, :, (1 - p) * SH:(2 - p) * SH],
                                        in_=h_sh[1 - p][4:8]).then_inc(sem["g"], 16))
        tg = sw_dma(lambda e, p: e.dma_start(out=h_full[4:8, :, (1 - p) * SH:(2 - p) * SH],
                                             in_=h_sh[1 - p][0:4]).then_inc(sem["g"], 16))
        return tg

    def emit_vproj(wname, dst, t_hready):
        for ch in range(8):
            guard_dma("sync")
            dma("sync", SD[:, ch * 512:(ch + 1) * 512], w_bf[wname][ch], sname="in2")
        wt("sync", "g", t_hready)
        wst_rd = [now("t"), now("t")]
        smb_wb = 0
        for tt in range(NTF):
            buf = tt % 2
            wt("sync", "t", wst_rd[buf])
            th = dma("sync", wst[:, buf * 1024:(buf + 1) * 1024].rearrange(
                "p (c q) -> p c q", c=8),
                h_full[:, :, tt * 128:(tt + 1) * 128].rearrange("c p q -> p c q"),
                sname="in2")
            wt("tensor", "in2", th)
            for ch in range(8):
                op("tensor", lambda e, i=wst[:, buf * 1024 + ch * 128:buf * 1024 + (ch + 1) * 128],
                   r=SD[:, ch * 512:(ch + 1) * 512], o=PS_C[:, :], s0=(ch == 0), s1=(ch == 7):
                   e.matmul(out=o, lhsT=i, rhs=r, start=s0, stop=s1))
            wst_rd[buf] = now("t")
            wt("scalar", "t", now("t"))
            wt("scalar", "g", smb_wb)
            te = op("scalar", lambda e, i=PS_C[:, :], o=smb[:, :]:
                    e.activation(out=o, in_=i, func=AF.Identity))
            wt("gpsimd", "s", te)
            smb_wb = dma("gpsimd", dst[tt], smb[:, :])
            wt("tensor", "s", te)
        return now("g")

    def emit_kqproj_b(t_hready):
        for ch in range(8):
            guard_dma("sync")
            dma("sync", SD[:, ch * 512:(ch + 1) * 512], w_bf["wk_b"][ch], sname="in2")
            dma("sync", SD[:, 4096 + ch * 512:4096 + (ch + 1) * 512], w_bf["wv_b"][ch] if False else w_bf["wq_b"][ch], sname="in2")
        wt("sync", "g", t_hready)
        wst_rd = [now("t")] * 4
        sa_wb = [t_hready] * 4
        ps_ev = [0] * 4
        for kq in range(2):
            base = 0 if kq == 0 else 4096
            dd = kb_d if kq == 0 else qb_d
            for sl in range(8):
                for ch in range(8):
                    buf = ch % 4
                    eng, sn = ("sync", "in2") if ch % 2 == 0 else ("gpsimd", "g")
                    wt(eng, "t", wst_rd[buf])
                    th = dma(eng, wst[:, buf * 512:(buf + 1) * 512],
                             h_full[ch, :, sl * 512:(sl + 1) * 512], sname=sn)
                    wt("tensor", sn, th)
                    if ch == 0:
                        for pr in range(4):
                            wt("tensor", "s", ps_ev[pr])
                    for pr in range(4):
                        op("tensor", lambda e,
                           i=SD[:, base + ch * 512 + pr * 128:base + ch * 512 + (pr + 1) * 128],
                           r=wst[:, buf * 512:(buf + 1) * 512],
                           o=PS_BIG[:, pr * 512:(pr + 1) * 512],
                           s0=(ch == 0), s1=(ch == 7):
                           e.matmul(out=o, lhsT=i, rhs=r, start=s0, stop=s1))
                    wst_rd[buf] = now("t")
                for pr in range(4):
                    wt("scalar", "t", now("t"))
                    wt("scalar", "g", sa_wb[pr])
                    te = op("scalar", lambda e, i=PS_BIG[:, pr * 512:(pr + 1) * 512],
                            o=SA[:, pr * 512:(pr + 1) * 512],
                            sc=(1.0 if kq == 0 else 0.125):
                            e.activation(out=o, in_=i, func=AF.Identity, scale=sc))
                    ps_ev[pr] = te
                    wt("gpsimd", "s", te)
                    sa_wb[pr] = dma("gpsimd", dd[pr][:, sl * 512:(sl + 1) * 512],
                                    SA[:, pr * 512:(pr + 1) * 512])
        return now("g")

    def emit_trig_a(t_hready):
        wt("sync", "g", t_hready)
        n = 1024
        for ch in range(4):
            for blk in range(4):
                guard_dma("sync")
                th = dma("sync", SD[:, 0:n], h_full[ch, :, blk * n:(blk + 1) * n], sname="in2")
                for side in range(2):
                    iw, ib = (0, 4) if side == 0 else (8, 12)
                    wt("scalar", "in2", th)
                    t1 = op("scalar", lambda e, i=SD[:, 0:n], o=SC[:, 0:n],
                            w=prw[:, iw + ch:iw + ch + 1], b=prw[:, ib + ch:ib + ch + 1]:
                            e.activation(out=o, in_=i, func=AF.Identity, scale=w, bias=b))
                    wt("vector", "s", t1)
                    opn("vector", lambda e, i=SC[:, 0:n], o=SI[:, 0:n]: e.tensor_copy(out=o, in_=i))
                    dv()
                    kf = SC[:, 1024:1024 + n]
                    opn("vector", lambda e, i=SI[:, 0:n], o=kf: e.tensor_copy(out=o, in_=i))
                    dv()
                    gp = SC[:, 2048:2048 + n]
                    opn("vector", lambda e, i=kf, o=gp:
                        e.tensor_scalar(out=o, in0=i, scalar1=2048.0, scalar2=None, op0=ALU.is_gt))
                    dv()
                    opn("vector", lambda e, a=gp, b=kf, o=kf:
                        e.scalar_tensor_tensor(out=o, in0=a, scalar=-4096.0, in1=b,
                                               op0=ALU.mult, op1=ALU.add))
                    dv()
                    opn("vector", lambda e, i=kf, o=gp:
                        e.tensor_scalar(out=o, in0=i, scalar1=-2048.0, scalar2=None, op0=ALU.is_lt))
                    dv()
                    opn("vector", lambda e, a=gp, b=kf, o=kf:
                        e.scalar_tensor_tensor(out=o, in0=a, scalar=4096.0, in1=b,
                                               op0=ALU.mult, op1=ALU.add))
                    dv()
                    k2 = SC[:, 0:n]
                    opn("vector", lambda e, i=kf, o=gp:
                        e.tensor_scalar(out=o, in0=i, scalar1=1024.0, scalar2=None, op0=ALU.is_gt))
                    dv()
                    tv = op("vector", lambda e, a=gp, b=kf, o=k2:
                            e.scalar_tensor_tensor(out=o, in0=a, scalar=-4096.0, in1=b,
                                                   op0=ALU.mult, op1=ALU.add))
                    wt("scalar", "v", tv)
                    sn = SD[:, 1024:1024 + n]
                    cs = SD[:, 2048:2048 + n]
                    op("scalar", lambda e, i=kf, o=sn:
                       e.activation(out=o, in_=i, func=AF.Sin, scale=DELTA))
                    t3 = op("scalar", lambda e, i=k2, o=cs:
                            e.activation(out=o, in_=i, func=AF.Sin, scale=DELTA, bias=BIAS_COS))
                    wt("gpsimd", "s", t3)
                    dd = qa_d if side == 0 else ka_d
                    dma("gpsimd", dd[2 * ch][0:64, blk * n:(blk + 1) * n], cs[0:64, :])
                    dma("gpsimd", dd[2 * ch][64:128, blk * n:(blk + 1) * n], sn[0:64, :])
                    dma("gpsimd", dd[2 * ch + 1][0:64, blk * n:(blk + 1) * n], cs[64:128, :])
                    tg = dma("gpsimd", dd[2 * ch + 1][64:128, blk * n:(blk + 1) * n], sn[64:128, :])
                    wt("scalar", "g", tg)
                    wt("vector", "g", tg)
        return now("g")

    def emit_o_exchange():
        wt("gpsimd", "v", now("v"))
        wt("gpsimd", "g", now("g"))
        om4 = OM.rearrange("p (c t) -> p c t", c=4)
        sw_dma(lambda e, p: e.dma_start(
            out=o_sh[4 * p:4 * (p + 1)].rearrange("c p t -> p c t"),
            in_=om4).then_inc(sem["g"], 16))
        bv = barrier()
        wt("gpsimd", "bar", bv)
        def _oread(e, p):
            for c in range(4):
                e.dma_start(out=SA[:, c * 2048:(c + 1) * 2048],
                            in_=o_sh[4 * p + c][:, p * SH:(p + 1) * SH]).then_inc(sem["g"], 16)
                e.dma_start(out=SA[:, 8192 + c * 2048:8192 + (c + 1) * 2048],
                            in_=o_sh[4 * (1 - p) + c][:, p * SH:(p + 1) * SH]).then_inc(sem["g"], 16)
        cnt["g"] += 16 * 8
        def _run_oread(e):
            for arm in e.switch_core_id(n=128):
                _oread(e, (int(arm) % 128) % 2)
        em("gpsimd", _run_oread)
        tg = cnt["g"]
        return tg

    def emit_gemm_residual(wname, rhs_fn, bias, t_rhs_ready):
        wt("tensor", "g", t_rhs_ready)
        for do in range(8):
            guard_dma("sync")
            tw = 0
            for ch in range(8):
                tw = dma("sync", wst[:, ch * 128:(ch + 1) * 128],
                         w_bf[wname][ch][:, do * 128:(do + 1) * 128], sname="in2")
            wt("tensor", "in2", tw)
            for ts4 in range(4):
                wt("tensor", "s", now("s"))
                for ch in range(8):
                    op("tensor", lambda e, i=wst[:, ch * 128:(ch + 1) * 128],
                       r=rhs_fn(ch)[:, ts4 * 512:(ts4 + 1) * 512], o=PS_C[:, :],
                       s0=(ch == 0), s1=(ch == 7):
                       e.matmul(out=o, lhsT=i, rhs=r, start=s0, stop=s1))
                wt("scalar", "t", now("t"))
                if bias is not None:
                    te = op("scalar", lambda e, i=PS_C[:, :], o=smb[:, :], b=bias[:, do:do + 1]:
                            e.activation(out=o, in_=i, func=AF.Identity, bias=b))
                else:
                    te = op("scalar", lambda e, i=PS_C[:, :], o=smb[:, :]:
                            e.activation(out=o, in_=i, func=AF.Identity))
                wt("tensor", "s", te)
                wt("tensor", "v", now("v"))
                for q in range(4):
                    op("tensor", lambda e, i=smb[:, q * 128:(q + 1) * 128],
                       o=PS_T[:, q * 128:(q + 1) * 128]:
                       e.transpose(out=o, in_=i, identity=idn[:, :]))
                wt("vector", "t", now("t"))
                for q in range(4):
                    tt_ = ts4 * 4 + q
                    op("vector", lambda e,
                       a=x_res[:, tt_ * D + do * 128:tt_ * D + (do + 1) * 128],
                       b=PS_T[:, q * 128:(q + 1) * 128],
                       o=x_res[:, tt_ * D + do * 128:tt_ * D + (do + 1) * 128]:
                       e.tensor_tensor(out=o, in0=a, in1=b, op=ALU.add))

    def emit_attn_a():
        t_h = emit_h_production()
        t_v = emit_vproj("vw_a", va_d, t_h)
        t_t = emit_trig_a(t_h)
        half_s = [0, 0]
        half_v = [0, 0]
        pv_last = [0, 0, 0, 0]
        t_sd_free = 0
        for h in range(8):
            wt("sync", "g", t_t)
            wt("sync", "t", t_sd_free)
            wt("gpsimd", "t", t_sd_free)
            tk_ch = []
            tq_ch = []
            for c in range(8):
                tk_ch.append(dma("sync", SD[:, c * 512:(c + 1) * 512],
                                 ka_d[h][:, c * 512:(c + 1) * 512], sname="in2"))
                tq_ch.append(dma("gpsimd", SD[:, 4096 + c * 512:4096 + (c + 1) * 512],
                                 qa_d[h][:, c * 512:(c + 1) * 512], sname="g"))
            for qc in range(8):
                nkt = 4 * qc + 4
                wt("tensor", "in2", tk_ch[qc])
                wt("tensor", "g", tq_ch[qc])
                for g in range((nkt + 1) // 2):
                    half = g % 2
                    base = half * 1024
                    kts = [2 * g + j for j in range(2) if 2 * g + j < nkt]
                    wt("tensor", "v", half_v[half])
                    wt("tensor", "s", half_s[half])
                    for j, kt in enumerate(kts):
                        op("tensor", lambda e, i=SD[:, kt * 128:(kt + 1) * 128],
                           r=SD[:, 4096 + qc * 512:4096 + (qc + 1) * 512],
                           o=PS_BIG[:, base + j * 512:base + (j + 1) * 512]:
                           e.matmul(out=o, lhsT=i, rhs=r, start=True, stop=True))
                    wt("scalar", "t", now("t"))
                    ts = op("scalar", lambda e, i=PS_BIG[:, base:base + len(kts) * 512],
                            o=SA[:, kts[0] * 512:(kts[0] + len(kts)) * 512]:
                            e.activation(out=o, in_=i, func=AF.Exp, scale=SCALE_A))
                    half_s[half] = ts
                    half_v[half] = now("v")
                    for kt in kts:
                        if kt >= 4 * qc:
                            r = kt - 4 * qc
                            wt("vector", "s", ts)
                            half_v[half] = op("vector",
                               lambda e, a=SA[:, kt * 512:(kt + 1) * 512],
                               m=msk[:, r * 512:(r + 1) * 512],
                               o=SA[:, kt * 512:(kt + 1) * 512]:
                               e.tensor_tensor(out=o, in0=a, in1=m, op=ALU.mult))
                if qc == 7:
                    t_sd_free = now("t")
                wt("tensor", "v", now("v"))
                wt("tensor", "s", now("s"))
                for kt in range(nkt):
                    buf = kt % 4
                    wt("sync", "t", pv_last[buf])
                    tvd = dma("sync", pvs[:, buf * 65:buf * 65 + 64],
                              va_d[kt][:, h * 64:(h + 1) * 64], sname="in2")
                    wt("tensor", "in2", tvd)
                    pv_last[buf] = op("tensor", lambda e, i=pvs[:, buf * 65:(buf + 1) * 65],
                       r=SA[:, kt * 512:(kt + 1) * 512], o=PS_A[0:65, :],
                       s0=(kt == 0), s1=(kt == nkt - 1):
                       e.matmul(out=o, lhsT=i, rhs=r, start=s0, stop=s1))
                wt("vector", "t", now("t"))
                opn("vector", lambda e, i=PS_A[64:65, :], o=smf[0:1, :]:
                    e.reciprocal(out=o, in_=i))
                dv()
                tvr = op("vector", lambda e, i=PS_A[0:64, :], o=smb[0:64, :]:
                         e.tensor_copy(out=o, in_=i))
                wt("tensor", "v", tvr)
                tb = op("tensor", lambda e, i=on1f[:, 0:64], r=smf[0:1, :], o=PS_B[0:64, :]:
                        e.matmul(out=o, lhsT=i, rhs=r, start=True, stop=True))
                wt("vector", "t", tb)
                if h % 2 == 0:
                    dst = OM[0:64, (h // 2) * S + qc * 512:(h // 2) * S + (qc + 1) * 512]
                    op("vector", lambda e, a=smb[0:64, :], b=PS_B[0:64, :], o=dst:
                       e.tensor_tensor(out=o, in0=a, in1=b, op=ALU.mult))
                else:
                    tvo = op("vector", lambda e, a=smb[0:64, :], b=PS_B[0:64, :], o=smf[0:64, :]:
                             e.tensor_tensor(out=o, in0=a, in1=b, op=ALU.mult))
                    tvo2 = op("vector", lambda e, i=smf[0:64, :], o=smb[0:64, :]:
                              e.tensor_copy(out=o, in_=i))
                    wt("gpsimd", "v", tvo2)
                    tg = dma("gpsimd", OM[64:128, (h // 2) * S + qc * 512:
                                          (h // 2) * S + (qc + 1) * 512], smb[0:64, :])
                    wt("vector", "g", tg)
                wt("tensor", "v", now("v"))
        t_o = emit_o_exchange()
        emit_gemm_residual("wo_a", lambda ch: SA[:, ch * 2048:(ch + 1) * 2048], None, t_o)

    def emit_ffn():
        for tt in range(NT):
            wt("vector", "t", now("t"))
            wt("vector", "s", now("s"))
            tln = ln_tile(tt, SC[:, 0:D])
            emit_quant(SC[:, 0:D], SD[:, 0:D], D)
            wt("tensor", "v", now("v"))
            for ch in range(8):
                op("tensor", lambda e, i=SD[:, ch * 128:(ch + 1) * 128],
                   o=PS_T[:, ch * 128:(ch + 1) * 128]:
                   e.transpose(out=o, in_=i, identity=idn[:, :]))
            wt("vector", "t", now("t"))
            for ch in range(8):
                op("vector", lambda e, i=PS_T[:, ch * 128:(ch + 1) * 128],
                   o=SA[:, ch * 2048 + tt * 128:ch * 2048 + (tt + 1) * 128]:
                   e.tensor_copy(out=o, in_=i))
        emit_gemm_residual("wffT", lambda ch: SA[:, ch * 2048:(ch + 1) * 2048], bfc, now("g"))

    def emit_attn_b():
        t_h = emit_h_production()
        emit_vproj("wv_b", vb_d, t_h)
        t_kq = emit_kqproj_b(t_h)
        pvb_last = [0, 0, 0, 0]
        ps_rd_s = [now("s"), now("s")]
        ps_rd_v = [now("v"), now("v")]
        sa_rd_v = [0, 0]
        pay_rd = [0, 0]
        pv_ev = 0
        t_sd_free = 0
        t_kq = now("g")
        for pr in range(4):
            wt("sync", "g", t_kq)
            wt("sync", "t", t_sd_free)
            wt("gpsimd", "t", t_sd_free)
            tk_ch = 0
            tq_ch = []
            for c in range(8):
                tk_ch = dma("sync", SD[:, c * 512:(c + 1) * 512],
                            kb_d[pr][:, c * 512:(c + 1) * 512], sname="in2")
                tq_ch.append(dma("gpsimd", SD[:, 4096 + c * 512:4096 + (c + 1) * 512],
                                 qb_d[pr][:, c * 512:(c + 1) * 512], sname="g"))
            wt("tensor", "in2", tk_ch)
            for h01 in range(2):
                for qc in range(8):
                    q0 = 4096 + qc * 512
                    wt("tensor", "g", tq_ch[qc])
                    if bstage < 1:
                        continue
                    tile = 0
                    for qt2 in range(4):
                        qt = 4 * qc + qt2
                        for hf in range(2):
                            for kh in range(2):
                                pp = tile % 2
                                tile += 1
                                wt("tensor", "v", ps_rd_v[pp])
                                wt("tensor", "s", ps_rd_s[pp])
                                for ksl in range(2):
                                    op("tensor", lambda e,
                                       i=SD[:, 4096 + qt * 128:4096 + (qt + 1) * 128][h01 * 64:(h01 + 1) * 64, :],
                                       r=SD[:, hf * 2048 + kh * 1024 + ksl * 512:hf * 2048 + kh * 1024 + (ksl + 1) * 512][h01 * 64:(h01 + 1) * 64, :],
                                       o=PS_BIG[:, pp * 1024 + ksl * 512:pp * 1024 + (ksl + 1) * 512]:
                                       e.matmul(out=o, lhsT=i, rhs=r, start=True, stop=True))
                                t_mm = now("t")
                                wt("scalar", "t", t_mm)
                                wt("scalar", "v", sa_rd_v[pp])
                                tex = op("scalar", lambda e, i=PS_BIG[:, pp * 1024:(pp + 1) * 1024],
                                         o=SA[:, pp * 1024:(pp + 1) * 1024]:
                                         e.activation(out=o, in_=i, func=AF.Exp))
                                ps_rd_s[pp] = tex
                                wt("vector", "s", tex)
                                opn("vector", lambda e,
                                   i=SA[:, pp * 1024:(pp + 1) * 1024].rearrange("p (a b) -> p a b", b=64),
                                   o=SEb[:, qt2 * 64 + hf * 32 + kh * 16:qt2 * 64 + hf * 32 + kh * 16 + 16]:
                                   e.reduce_max(out=o, in_=i, axis=AX.X))
                                sa_rd_v[pp] = op("vector", lambda e,
                                   i=SA[:, pp * 1024:(pp + 1) * 1024].rearrange("p (a b) -> p a b", b=64),
                                   o=SEf[:, 768 + qt2 * 64 + hf * 32 + kh * 16:768 + qt2 * 64 + hf * 32 + kh * 16 + 16]:
                                   e.reduce_sum(out=o, in_=i, axis=AX.X))
                    if bstage < 4:
                        continue
                    LC = SEf[:, 0:256]
                    MR = SEf[:, 256:512]
                    LNL = SEf[:, 512:768]
                    D4 = SEf[:, 768:1024]
                    dv()
                    t_lc = 0
                    for qt2 in range(4):
                        t_lc = op("vector", lambda e, i=D4[:, qt2 * 64:(qt2 + 1) * 64],
                                  o=LC[:, qt2 * 64:(qt2 + 1) * 64]:
                                  e.tensor_tensor_scan(out=o, data0=i, data1=i, initial=0.0,
                                                       op0=ALU.add, op1=ALU.bypass))
                        dv()
                    t_mr = 0
                    for qt2 in range(4):
                        t_mr = op("vector", lambda e, i=SEb[:, qt2 * 64:(qt2 + 1) * 64],
                                  o=MR[:, qt2 * 64:(qt2 + 1) * 64]:
                                  e.tensor_tensor_scan(out=o, data0=i, data1=i, initial=0.0,
                                                       op0=ALU.max, op1=ALU.bypass))
                        dv()
                    if bstage == 41:
                        continue
                    wt("scalar", "v", t_mr)
                    t_ln = op("scalar", lambda e, i=LC, o=LNL:
                              e.activation(out=o, in_=i, func=AF.Ln))
                    t_ln2 = op("scalar", lambda e, i=MR, o=D4:
                               e.activation(out=o, in_=i, func=AF.Ln))
                    wt("vector", "s", t_ln2)
                    dv()
                    opn("vector", lambda e, a=D4, b=LNL, o=D4:
                        e.tensor_tensor(out=o, in0=a, in1=b, op=ALU.subtract))
                    dv()
                    if bstage == 42:
                        continue
                    for qt2 in range(4):
                        opn("vector", lambda e, i=D4[:, qt2 * 64:(qt2 + 1) * 64],
                            o=LC[:, qt2 * 64:(qt2 + 1) * 64]:
                            e.tensor_tensor_scan(out=o, data0=i, data1=i, initial=0.0,
                                                 op0=ALU.add, op1=ALU.bypass))
                        dv()
                    opn("vector", lambda e, a=LC, b=D4, o=MR:
                        e.tensor_tensor(out=o, in0=a, in1=b, op=ALU.subtract))
                    dv()
                    if bstage == 43:
                        continue
                    for qt2 in range(4):
                        opn("vector", lambda e, a=MR[:, qt2 * 64:(qt2 + 1) * 64],
                            s_=LC[:, qt2 * 64 + 63:qt2 * 64 + 64].broadcast_to((128, 64)),
                            o=MR[:, qt2 * 64:(qt2 + 1) * 64]:
                            e.scalar_tensor_tensor(out=o, in0=a, scalar=-1.0, in1=s_,
                                                   op0=ALU.mult, op1=ALU.add))
                        dv()
                    for qt2 in range(4):
                        opn("vector", lambda e, i=LNL[:, qt2 * 64 + 63:qt2 * 64 + 64],
                            o=SEf[:, 4 + qt2:5 + qt2]:
                            e.tensor_scalar(out=o, in0=i, scalar1=-1.0, scalar2=None, op0=ALU.mult))
                    for qt2 in range(4):
                        opn("vector", lambda e, i=D4[:, qt2 * 64 + 63:qt2 * 64 + 64],
                            o=SEf[:, qt2:qt2 + 1]: e.tensor_copy(out=o, in_=i))
                    dv()
                    if bstage == 44:
                        continue
                    for qt2 in range(4):
                        wt("vector", "t", pay_rd[0])
                        if bstage == 448:
                            opn("vector", lambda e, i=MR[:, qt2 * 64:(qt2 + 1) * 64],
                                o=smf[:, 0:64]: e.tensor_copy(out=o, in_=i))
                            continue
                        if bstage == 449:
                            opn("vector", lambda e, i=MR[:, qt2 * 64:(qt2 + 1) * 64],
                                o=smf[:, 0:64]:
                                e.tensor_scalar(out=o, in0=i, scalar1=-1.0, scalar2=None,
                                                op0=ALU.subtract))
                            continue
                        opn("vector", lambda e, i=MR[:, qt2 * 64:(qt2 + 1) * 64],
                            c=SEf[:, qt2:qt2 + 1],
                            o=(smf[:, 0:64] if bstage == 450 else pay[:, 0:64]):
                            e.tensor_scalar(out=o, in0=i, scalar1=c, scalar2=None,
                                            op0=ALU.subtract))
                        if bstage in (450, 451):
                            continue
                        opn("vector", lambda e, i=SEf[:, 4 + qt2:5 + qt2],
                            o=pay[:, 64:65]: e.tensor_copy(out=o, in_=i))
                        if bstage == 452:
                            continue
                        dv()
                        tv3 = op("vector", lambda e, a=SEf[:, 4 + qt2:5 + qt2],
                                 b=pay[:, 64:65], o=pay[:, 65:66]:
                                 e.tensor_tensor(out=o, in0=a, in1=b, op=ALU.subtract))
                        if bstage == 45:
                            continue
                        wt("tensor", "v", tv3)
                        tt4 = op("tensor", lambda e, i=pay[:, 0:128],
                                 o=PS_T[:, qt2 * 128:(qt2 + 1) * 128]:
                                 e.transpose(out=o, in_=i, identity=idn[:, :]))
                        pay_rd[0] = tt4
                        if bstage == 46:
                            continue
                        wt("vector", "t", tt4)
                        op("vector", lambda e, i=PS_T[:, qt2 * 128:(qt2 + 1) * 128][0:66, :],
                           o=smb[0:66, qt2 * 128:(qt2 + 1) * 128]:
                           e.tensor_copy(out=o, in_=i))
                    if bstage < 5:
                        continue
                    dh_s = [now("s"), now("s")]
                    wt("tensor", "v", now("v"))
                    for g in range(16):
                        half = g % 2
                        base = half * 1024
                        wt("tensor", "s", dh_s[half])
                        for j in range(2):
                            kt = 2 * g + j
                            op("tensor", lambda e,
                               i=SD[:, kt * 128:(kt + 1) * 128][h01 * 64:(h01 + 1) * 64, :],
                               r=SD[:, q0:q0 + 512][h01 * 64:(h01 + 1) * 64, :],
                               o=PS_BIG[:, base + j * 512:base + (j + 1) * 512]:
                               e.matmul(out=o, lhsT=i, rhs=r, start=True, stop=False))
                        for j in range(2):
                            kt = 2 * g + j
                            op("tensor", lambda e, i=selb[0:66, kt * 128:(kt + 1) * 128],
                               r=smb[0:66, 0:512],
                               o=PS_BIG[:, base + j * 512:base + (j + 1) * 512]:
                               e.matmul(out=o, lhsT=i, rhs=r, start=False, stop=True))
                        wt("scalar", "t", now("t"))
                        tsx = op("scalar", lambda e, i=PS_BIG[:, base:base + 1024],
                                 o=SA[:, g * 1024:(g + 1) * 1024]:
                                 e.activation(out=o, in_=i, func=AF.Exp))
                        dh_s[half] = tsx
                    ps_rd_s[0] = dh_s[0]
                    ps_rd_s[1] = dh_s[1]
                    if bstage < 60:
                        continue
                    wt("tensor", "s", now("s"))
                    wt("tensor", "v", pv_ev)
                    for kt in range(32):
                        buf = kt % 4
                        wt("sync", "t", pvb_last[buf])
                        tvd = dma("sync", vst[:, buf * 64:(buf + 1) * 64],
                                  vb_d[kt][:, (pr * 2 + h01) * 64:(pr * 2 + h01 + 1) * 64],
                                  sname="in2")
                        wt("tensor", "in2", tvd)
                        pvb_last[buf] = op("tensor", lambda e, i=vst[:, buf * 64:(buf + 1) * 64],
                           r=SA[:, kt * 512:(kt + 1) * 512], o=PS_A[0:64, :],
                           s0=(kt == 0), s1=(kt == 31):
                           e.matmul(out=o, lhsT=i, rhs=r, start=s0, stop=s1))
                    wt("vector", "t", now("t"))
                    pv_ev = op("vector", lambda e, i=PS_A[0:64, :],
                               o=OM[h01 * 64:(h01 + 1) * 64, pr * S + qc * 512:pr * S + (qc + 1) * 512]:
                               e.tensor_copy(out=o, in_=i))
                t_sd_free = now("t")
        t_o = emit_o_exchange()
        emit_gemm_residual("wo_b", lambda ch: SA[:, ch * 2048:(ch + 1) * 2048], None, t_o)

    if n_sub == -3:
        emit_attn_b()
    if n_sub >= 1:
        emit_attn_a()
    if n_sub >= 2:
        emit_ffn()
    if n_sub >= 3:
        emit_attn_b()
    if n_sub >= 4:
        emit_ffn()

    wt("gpsimd", "v", now("v"))
    for tt in range(NT):
        dma("gpsimd", y_out.rearrange("(n p) d -> n p d", p=128)[tt],
            x_res[:, tt * D:(tt + 1) * D])
    wt("gpsimd", "g", now("g"))

    blk = st.enter_context(nc.Block())

    @blk.sync
    def _(e):
        for fn in EM["sync"]:
            fn(e)

    @blk.vector
    def _(e):
        for fn in EM["vector"]:
            fn(e)

    @blk.scalar
    def _(e):
        for fn in EM["scalar"]:
            fn(e)

    @blk.tensor
    def _(e):
        for fn in EM["tensor"]:
            fn(e)

    @blk.gpsimd
    def _(e):
        for fn in EM["gpsimd"]:
            fn(e)

    st.close()
    global _LAST_CNT
    _LAST_CNT = dict(cnt)
    _LAST_CNT["_em"] = {k: len(v) for k, v in EM.items()}
    try:
        _LAST_CNT["_insts"] = len(list(nc.all_instructions()))
    except Exception:
        pass
    return nc


def _host_prep(inputs, core):
    p = core % 2
    b = core // 2
    perm = np.arange(D)
    if p == 1:
        perm = np.concatenate([perm[512:], perm[:512]])
    hsl = slice(p * 8, p * 8 + 8)

    x = inputs["L0_attn_a_x"]
    xc = np.ascontiguousarray(x[b, p * SH:(p + 1) * SH, :][:, perm], np.float32)

    def chunkT(w):
        return np.ascontiguousarray(w.reshape(8, 128, -1), np.float32)

    wq = inputs["L0_attn_a_w_q"][hsl].reshape(512)
    bq = inputs["L0_attn_a_b_q"][hsl].reshape(512)
    wk = inputs["L0_attn_a_w_k"][hsl].reshape(512)
    bk = inputs["L0_attn_a_b_k"][hsl].reshape(512)
    tprm = np.stack([wq.reshape(4, 128), bq.reshape(4, 128),
                     wk.reshape(4, 128), bk.reshape(4, 128)], 0)
    tprm = np.ascontiguousarray(tprm.reshape(16, 128).T, np.float32)

    vw = inputs["L0_attn_a_v_w"][p * 512:(p + 1) * 512, :][:, perm]
    vw_a = chunkT(np.ascontiguousarray(vw.T))
    ow = inputs["L0_attn_a_out_w"]
    wo_a = chunkT(np.ascontiguousarray(ow.T[perm][:, perm]))

    wqb = inputs["L0_attn_b_Wq"][perm][:, p * 512:(p + 1) * 512]
    wkb = inputs["L0_attn_b_Wk"][perm][:, p * 512:(p + 1) * 512]
    wvb = inputs["L0_attn_b_Wv"][perm][:, p * 512:(p + 1) * 512]
    wob = inputs["L0_attn_b_Wo"][perm][:, perm]
    wff = inputs["L0_ffn_weight"][perm][:, perm]
    bff = inputs["L0_ffn_bias"][perm]

    masks = np.zeros((4, 128, 512), ml_dtypes.bfloat16)
    for r in range(4):
        i_ = np.arange(128)[:, None] + 128 * r
        j_ = np.arange(512)[None, :]
        masks[r] = (i_ <= j_).astype(ml_dtypes.bfloat16)
    ident = np.eye(128, dtype=ml_dtypes.bfloat16)
    ind2 = np.zeros((2, 128), ml_dtypes.bfloat16)
    ind2[0, 0:64] = 1
    ind2[1, 64:128] = 1
    ones2 = np.ones((2, 128), ml_dtypes.bfloat16)
    ones1 = np.ones((1, 128), ml_dtypes.bfloat16)
    bigm = np.zeros((128, 192), ml_dtypes.bfloat16)
    bigm[0:64, 126] = 1
    bigm[64:128, 127] = 1
    selb = np.zeros((128, 4096), ml_dtypes.bfloat16)
    for kt in range(32):
        for c in range(128):
            selb[2 * kt + (c >= 64), kt * 128 + c] = 1
        selb[64, kt * 128:(kt + 1) * 128] = 1
        selb[65, kt * 128:(kt + 1) * 128] = 1

    return {
        "x_in": xc, "tprm": tprm,
        "bffi": np.ascontiguousarray(bff.reshape(8, 128).T, np.float32),
        "vw_a": vw_a, "wo_a": wo_a,
        "wq_b": chunkT(wqb), "wk_b": chunkT(wkb), "wv_b": chunkT(wvb),
        "wo_b": chunkT(wob), "wff": chunkT(wff),
        "masks_i": masks, "ident_i": ident, "ind2_i": ind2, "ones2_i": ones2,
        "ones1_i": ones1, "ones1f_i": np.ones((1, 128), np.float32), "bigm_i": bigm,
        "selb_i": selb,
        "cvals_i": np.ascontiguousarray(
            np.broadcast_to(np.array([1e-5, BIAS_COS, 0.0, 0.0], np.float32), (128, 4))),
    }, perm, b, p


def kernel(**inputs):
    key = ("nc", SUBLAYERS, ATTNB_STAGE)
    if key not in _CACHE:
        _CACHE[key] = build_nc(SUBLAYERS, ATTNB_STAGE)
    nc = _CACHE[key]
    in_maps = []
    metas = []
    for c in range(8):
        m, perm, b, p = _host_prep(inputs, c)
        in_maps.append(m)
        metas.append((perm, b, p))
    res = run_bass_kernel_spmd(nc, in_maps, core_ids=list(range(8)))
    out = np.empty((B, S, D), np.float32)
    for c in range(8):
        perm, b, p = metas[c]
        inv = np.argsort(perm)
        out[b, p * SH:(p + 1) * SH, :] = res.results[c]["y_out"][:, inv]
    return out


if __name__ == "__main__":
    nc = build_nc()
    print("built ok")


# version 06
# speedup vs baseline: 2.2886x; 1.0780x over previous
import contextlib
import numpy as np
import ml_dtypes
import concourse.bass as bass
import concourse.mybir as mybir
from concourse.bass_utils import run_bass_kernel_spmd

F32 = mybir.dt.float32
BF16 = mybir.dt.bfloat16
I32 = mybir.dt.int32
AF = mybir.ActivationFunctionType
ALU = mybir.AluOpType
AX = mybir.AxisListType

B, S, D, H, DH = 4, 4096, 1024, 16, 64
SH = S // 2
NT = SH // 128
NTF = S // 128
NKT = S // 128
C_LUT = float(np.float32(4096.0 / (2.0 * np.pi)))
DELTA = float(np.float32(np.float32(2.0 * np.pi) / 4096.0))
BIAS_COS = float(np.float32(1024.0) * np.float32(DELTA))
SCALE_A = float(np.float32(1.0 / np.sqrt(np.float32(128.0))))
SUBLAYERS = 4
ATTNB_STAGE = 60
_LAST_CNT = None
_CACHE = {}


def build_nc(n_sub=4, bstage=None):
    if bstage is None:
        bstage = ATTNB_STAGE
    nc = bass.Bass("TRN2", target_bir_lowering=False)

    ein = lambda n, sh, dt=F32: nc.dram_tensor(n, sh, dt, kind="ExternalInput").ap()
    x_in = ein("x_in", [SH, D])
    tprm = ein("tprm", [128, 16])
    bffi = ein("bffi", [128, 8])
    vw_a = ein("vw_a", [8, 128, 512])
    wo_a = ein("wo_a", [8, 128, 1024])
    wq_b = ein("wq_b", [8, 128, 512])
    wk_b = ein("wk_b", [8, 128, 512])
    wv_b = ein("wv_b", [8, 128, 512])
    wo_b = ein("wo_b", [8, 128, 1024])
    wff = ein("wff", [8, 128, 1024])
    masks_i = ein("masks_i", [4, 128, 512], BF16)
    ident_i = ein("ident_i", [128, 128], BF16)
    ind2_i = ein("ind2_i", [2, 128], BF16)
    ones2_i = ein("ones2_i", [2, 128], BF16)
    ones1_i = ein("ones1_i", [1, 128], BF16)
    ones1f_i = ein("ones1f_i", [1, 128])
    bigm_i = ein("bigm_i", [128, 192], BF16)
    selb_i = ein("selb_i", [128, 4096], BF16)
    cvals_i = ein("cvals_i", [128, 4])
    y_out = nc.dram_tensor("y_out", [SH, D], F32, kind="ExternalOutput").ap()

    din = lambda n, sh, dt=BF16, **kw: nc.dram_tensor(n, sh, dt, kind="Internal", **kw).ap()
    h_sh = din("h_sh", [2, 8, 128, SH], BF16, addr_space="Shared")
    o_sh = din("o_sh", [8, 128, S], BF16, addr_space="Shared")
    h_full = din("h_full", [8, 128, S])
    qa_d = din("qa_d", [8, 128, S])
    ka_d = din("ka_d", [8, 128, S])
    va_d = din("va_d", [NKT, 128, 512])
    qb_d = din("qb_d", [4, 128, S])
    kb_d = din("kb_d", [4, 128, S])
    vb_d = din("vb_d", [NKT, 128, 512])
    WCOLS = {"vw_a": 512, "wo_a": 1024, "wq_b": 512, "wk_b": 512,
             "wv_b": 512, "wo_b": 1024, "wffT": 1024}
    w_bf = {n: din("bf_" + n, [8, 128, c]) for n, c in WCOLS.items()}
    bar_i = din("bar_i", [1, 64], F32)
    bar_o = din("bar_o", [2, 64], F32)

    st = contextlib.ExitStack()
    sb = lambda n, sh, dt: st.enter_context(nc.sbuf_tensor(n, sh, dt))
    pst = lambda n, sh, dt=F32: st.enter_context(nc.psum_tensor(n, sh, dt))

    x_res = sb("x_res", [128, NT * D], F32)
    SA = sb("SA", [128, 16384], BF16)
    OM = sb("OM", [128, 4 * S], BF16)
    SC = sb("SC", [128, 3 * 1024], F32)
    SD = sb("SD", [128, 8192], BF16)
    SEb = sb("SEb", [128, 1024], BF16)
    SEf = sb("SEf", [128, 1056], F32)
    SI = sb("SI", [128, 1024], I32)
    msk = sb("msk", [128, 4 * 512], BF16)
    idn = sb("idn", [128, 128], BF16)
    i2 = sb("i2", [2, 128], BF16)
    o2 = sb("o2", [2, 128], BF16)
    on1 = sb("on1", [1, 128], BF16)
    on1f = sb("on1f", [1, 128], F32)
    bgm = sb("bgm", [128, 192], BF16)
    selb = sb("selb", [128, 4096], BF16)
    prm = sb("prm", [128, 16], F32)
    prw = sb("prw", [128, 16], F32)
    bfc = sb("bfc", [128, 8], F32)
    pvs = sb("pvs", [128, 4 * 65], BF16)
    vst = sb("vst", [128, 4 * 64], BF16)
    wst = sb("wst", [128, 2 * 1024], BF16)
    smf = sb("smf", [128, 512], F32)
    smb = sb("smb", [128, 512], BF16)
    pay = sb("pay", [128, 132], BF16)
    cva = sb("cva", [128, 4], F32)

    PS_BIG = pst("PS_BIG", [128, 2048], F32)
    PS_A = pst("PS_A", [128, 512], F32)
    PS_B = pst("PS_B", [128, 512], F32)
    PS_T = pst("PS_T", [128, 1024], BF16)
    PS_C = pst("PS_C", [128, 512], F32)

    sem = {n: st.enter_context(nc.semaphore(name="sem_" + n)) for n in
           ["in", "in2", "v", "s", "t", "g", "bar"]}
    cnt = {n: 0 for n in sem}
    EM = {k: [] for k in ["sync", "vector", "scalar", "tensor", "gpsimd"]}
    ES = {"sync": "in", "vector": "v", "scalar": "s", "tensor": "t", "gpsimd": "g"}

    def em(eng, fn):
        EM[eng].append(fn)

    def dma(eng, out, in_, sname=None):
        sname = sname or ES[eng]
        cnt[sname] += 16
        em(eng, lambda e, o=out, i=in_, s=sem[sname]: e.dma_start(out=o, in_=i).then_inc(s, 16))
        return cnt[sname]

    def wt(eng, sname, val):
        if val > 0:
            em(eng, lambda e, s=sem[sname], v=val: e.wait_ge(s, v))

    def op(eng, fn):
        sname = ES[eng]
        cnt[sname] += 1
        em(eng, lambda e, f=fn, s=sem[sname]: f(e).then_inc(s, 1))
        return cnt[sname]

    def opn(eng, fn):
        em(eng, fn)

    def dv():
        em("vector", lambda e: e.drain())

    def now(s):
        return cnt[s]

    def guard_dma(eng):
        for s in ("v", "s", "t", "g"):
            wt(eng, s, cnt[s])

    def sw_dma(fn_of_p):
        cnt["g"] += 16
        def run(e, f=fn_of_p):
            for arm in e.switch_core_id(n=128):
                f(e, (int(arm) % 128) % 2)
        em("gpsimd", run)
        return cnt["g"]

    def barrier():
        cnt["bar"] += 1
        em("gpsimd", lambda e, s=sem["bar"]: e.collective_compute(
            "AllGather", ALU.bypass, replica_groups=[[0, 1], [2, 3], [4, 5], [6, 7]],
            ins=[bar_i], outs=[bar_o]).then_inc(s, 1))
        return cnt["bar"]

    for r in range(4):
        dma("sync", msk[:, r * 512:(r + 1) * 512], masks_i[r])
    dma("sync", idn[:, :], ident_i)
    dma("sync", i2[:, :], ind2_i)
    dma("sync", o2[:, :], ones2_i)
    dma("sync", on1[:, :], ones1_i)
    dma("sync", on1f[:, :], ones1f_i)
    dma("sync", bgm[:, :], bigm_i)
    dma("sync", selb[:, :], selb_i)
    dma("sync", prm[:, :], tprm)
    dma("sync", cva[:, :], cvals_i)
    nc.const_aps.register(F32, 1e-5, cva[:, 0:1])
    nc.const_aps.register(F32, BIAS_COS, cva[:, 1:2])
    t_c = dma("sync", bfc[:, :], bffi)
    for tt in range(NT):
        dma("sync", x_res[:, tt * D:(tt + 1) * D],
            x_in.rearrange("(n p) d -> n p d", p=128)[tt])
    t_x = now("in")

    opn("vector", lambda e: e.memset(pvs[:, :], 0.0))
    dv()
    for b_ in range(3):
        opn("vector", lambda e, c=b_ * 65 + 64: e.memset(pvs[:, c:c + 1], 1.0))
    op("vector", lambda e: e.memset(pvs[:, 3 * 65 + 64:3 * 65 + 65], 1.0))

    wt("scalar", "in", t_c)
    wt("vector", "in", t_c)
    for dst, s_w, s_b in [(0, 0, 4), (8, 8, 12)]:
        t1 = op("scalar", lambda e, i=prm[:, s_w:s_w + 4], o=SC[:, 0:4]:
                e.activation(out=o, in_=i, func=AF.Abs, bias=1.0))
        wt("vector", "s", t1)
        opn("vector", lambda e, i=SC[:, 0:4], o=SC[:, 4:8]: e.reciprocal(out=o, in_=i))
        dv()
        opn("vector", lambda e, i=SC[:, 4:8], o=prw[:, dst:dst + 4]:
            e.tensor_scalar(out=o, in0=i, scalar1=C_LUT, scalar2=None, op0=ALU.mult))
        op("vector", lambda e, i=prm[:, s_b:s_b + 4], o=prw[:, dst + 4:dst + 8]:
           e.tensor_scalar(out=o, in0=i, scalar1=C_LUT, scalar2=None, op0=ALU.mult))

    SRC_F32 = {"vw_a": vw_a, "wo_a": wo_a, "wq_b": wq_b, "wk_b": wk_b,
               "wv_b": wv_b, "wo_b": wo_b}
    for name, src in SRC_F32.items():
        ncol = WCOLS[name]
        for ch in range(8):
            guard_dma("sync")
            t1 = dma("sync", SC[:, 0:ncol], src[ch])
            wt("vector", "in", t1)
            t2 = op("vector", lambda e, i=SC[:, 0:ncol], o=wst[:, 0:ncol]:
                    e.tensor_copy(out=o, in_=i))
            wt("gpsimd", "v", t2)
            t3 = dma("gpsimd", w_bf[name][ch], wst[:, 0:ncol])
            wt("sync", "g", t3)
            wt("vector", "g", t3)

    def emit_quant(src, dst, n):
        nb = n // 32
        amx = SEf[:, 0:nb]
        opn("vector", lambda e, i=src.rearrange("p (a b) -> p a b", b=32), o=amx:
            e.reduce_max(out=o, in_=i, axis=AX.X, apply_absolute_value=True))
        scl = SEf[:, nb:2 * nb]
        rsc = SEf[:, 2 * nb:3 * nb]
        dv()
        opn("vector", lambda e, i=amx, o=scl:
            e.tensor_scalar(out=o, in0=i, scalar1=float(np.float32(1.0) / np.float32(7.5)),
                            scalar2=1e-30, op0=ALU.mult, op1=ALU.max))
        dv()
        opn("vector", lambda e, i=scl, o=rsc: e.reciprocal(out=o, in_=i))
        dv()
        yv = SC[:, 1024:1024 + n]
        tv = op("vector", lambda e, x=src.rearrange("p (a b) -> p a b", b=32),
                r=rsc.unsqueeze(2).broadcast_to((128, nb, 32)),
                o=yv.rearrange("p (a b) -> p a b", b=32):
                e.tensor_tensor(out=o, in0=x, in1=r, op=ALU.mult))
        ay = SC[:, 2048:2048 + n]
        wt("scalar", "v", tv)
        ts = op("scalar", lambda e, i=yv, o=ay: e.activation(out=o, in_=i, func=AF.Abs))
        wt("vector", "s", ts)
        f1 = SD[:, 6144:6144 + n]
        f2 = SD[:, 7168:7168 + n]
        opn("vector", lambda e, i=ay, o=f1:
            e.tensor_scalar(out=o, in0=i, scalar1=1.9375, scalar2=None, op0=ALU.is_gt))
        opn("vector", lambda e, i=ay, o=f2:
            e.tensor_scalar(out=o, in0=i, scalar1=3.875, scalar2=None, op0=ALU.is_gt))
        t3v = ay
        dv()
        opn("vector", lambda e, i=f1, o=t3v:
            e.tensor_scalar(out=o, in0=i, scalar1=-4.0, scalar2=8.0, op0=ALU.mult, op1=ALU.add))
        inv = ay
        dv()
        opn("vector", lambda e, a=f2, b=t3v, o=inv:
            e.scalar_tensor_tensor(out=o, in0=a, scalar=-2.0, in1=b, op0=ALU.mult, op1=ALU.add))
        opn("vector", lambda e, a=yv, b=inv, o=yv: e.tensor_tensor(out=o, in0=a, in1=b, op=ALU.mult))
        opn("vector", lambda e, i=yv, o=SI[:, 0:n]: e.tensor_copy(out=o, in_=i))
        opn("vector", lambda e, i=SI[:, 0:n], o=yv: e.tensor_copy(out=o, in_=i))
        dv()
        opn("vector", lambda e, i=f1, o=f1:
            e.tensor_scalar(out=o, in0=i, scalar1=0.125, scalar2=0.125, op0=ALU.mult, op1=ALU.add))
        stp = f2
        dv()
        opn("vector", lambda e, a=f2, b=f1, o=stp:
            e.scalar_tensor_tensor(out=o, in0=a, scalar=0.25, in1=b, op0=ALU.mult, op1=ALU.add))
        dv()
        opn("vector", lambda e, a=yv, b=stp, o=yv: e.tensor_tensor(out=o, in0=a, in1=b, op=ALU.mult))
        dv()
        return op("vector", lambda e, a=yv.rearrange("p (a b) -> p a b", b=32),
                  s_=scl.unsqueeze(2).broadcast_to((128, nb, 32)),
                  o=dst.rearrange("p (a b) -> p a b", b=32):
                  e.tensor_tensor(out=o, in0=a, in1=s_, op=ALU.mult))

    for ch in range(8):
        guard_dma("sync")
        t1 = dma("sync", SC[:, 0:D], wff[ch])
        wt("vector", "in", t1)
        emit_quant(SC[:, 0:D], SA[:, ch * 1024:(ch + 1) * 1024], D)
    wt("tensor", "v", now("v"))
    wt("tensor", "in", t_c)
    for fi in range(8):
        wt("tensor", "g", now("g"))
        for fo in range(8):
            op("tensor", lambda e, i=SA[:, fo * 1024 + fi * 128:fo * 1024 + (fi + 1) * 128],
               o=PS_T[:, fo * 128:(fo + 1) * 128]:
               e.transpose(out=o, in_=i, identity=idn[:, :]))
        wt("vector", "t", now("t"))
        tv_ = op("vector", lambda e, i=PS_T[:, :], o=wst[:, 0:1024]: e.tensor_copy(out=o, in_=i))
        wt("gpsimd", "v", tv_)
        dma("gpsimd", w_bf["wffT"][fi], wst[:, 0:1024])

    wt("vector", "in", t_x)

    def ln_tile(tt, out_ap):
        xt = x_res[:, tt * D:(tt + 1) * D]
        opn("vector", lambda e, i=xt, o=SEf[:, 0:1]: e.reduce_sum(out=o, in_=i, axis=AX.X))
        dv()
        opn("vector", lambda e, i=SEf[:, 0:1], o=SEf[:, 1:2]:
            e.tensor_scalar(out=o, in0=i, scalar1=1.0 / 1024.0, scalar2=None, op0=ALU.mult))
        dv()
        xm = SC[:, 0:D]
        tv = op("vector", lambda e, i=xt, m=SEf[:, 1:2], o=xm:
                e.tensor_scalar(out=o, in0=i, scalar1=m, scalar2=None, op0=ALU.subtract))
        wt("scalar", "v", tv)
        tsq = op("scalar", lambda e, i=xm, o=SC[:, 1024:2048]:
                 e.activation(out=o, in_=i, func=AF.Square))
        wt("vector", "s", tsq)
        tss = op("vector", lambda e, i=SC[:, 1024:2048], o=SEf[:, 2:3]:
                 e.reduce_sum(out=o, in_=i, axis=AX.X))
        wt("scalar", "v", tss)
        t2 = op("scalar", lambda e, i=SEf[:, 2:3], o=SEf[:, 3:4]:
                e.activation(out=o, in_=i, func=AF.Sqrt, scale=1.0 / 1024.0, bias=1e-5))
        wt("vector", "s", t2)
        opn("vector", lambda e, i=SEf[:, 3:4], o=SEf[:, 4:5]: e.reciprocal(out=o, in_=i))
        dv()
        rr = op("vector", lambda e, i=xm, r=SEf[:, 4:5], o=out_ap:
                e.tensor_scalar(out=o, in0=i, scalar1=r, scalar2=None, op0=ALU.mult))
        return rr

    def emit_h_production():
        for tt in range(NT):
            wt("vector", "t", now("t"))
            tln = ln_tile(tt, SD[:, 0:D])
            wt("tensor", "v", tln)
            for ch in range(8):
                op("tensor", lambda e, i=SD[:, ch * 128:(ch + 1) * 128],
                   o=PS_T[:, ch * 128:(ch + 1) * 128]:
                   e.transpose(out=o, in_=i, identity=idn[:, :]))
            wt("vector", "t", now("t"))
            for ch in range(8):
                op("vector", lambda e, i=PS_T[:, ch * 128:(ch + 1) * 128],
                   o=SA[:, ch * 2048 + tt * 128:ch * 2048 + (tt + 1) * 128]:
                   e.tensor_copy(out=o, in_=i))
        wt("gpsimd", "v", now("v"))
        h_own = SA[:, 0:16384].rearrange("p (c t) -> p c t", c=8)
        sw_dma(lambda e, p: e.dma_start(out=h_sh[p].rearrange("c p t -> p c t"),
                                        in_=h_own).then_inc(sem["g"], 16))
        sw_dma(lambda e, p: e.dma_start(
            out=h_full[:, :, p * SH:(p + 1) * SH].rearrange("c p t -> p c t"),
            in_=h_own).then_inc(sem["g"], 16))
        wt("gpsimd", "g", now("g"))
        bv = barrier()
        wt("gpsimd", "bar", bv)
        sw_dma(lambda e, p: e.dma_start(out=h_full[0:4, :, (1 - p) * SH:(2 - p) * SH],
                                        in_=h_sh[1 - p][4:8]).then_inc(sem["g"], 16))
        tg = sw_dma(lambda e, p: e.dma_start(out=h_full[4:8, :, (1 - p) * SH:(2 - p) * SH],
                                             in_=h_sh[1 - p][0:4]).then_inc(sem["g"], 16))
        return tg

    def emit_vproj(wname, dst, t_hready):
        for ch in range(8):
            guard_dma("sync")
            dma("sync", SD[:, ch * 512:(ch + 1) * 512], w_bf[wname][ch], sname="in2")
        wt("sync", "g", t_hready)
        wst_rd = [now("t"), now("t")]
        smb_wb = 0
        for tt in range(NTF):
            buf = tt % 2
            wt("sync", "t", wst_rd[buf])
            th = dma("sync", wst[:, buf * 1024:(buf + 1) * 1024].rearrange(
                "p (c q) -> p c q", c=8),
                h_full[:, :, tt * 128:(tt + 1) * 128].rearrange("c p q -> p c q"),
                sname="in2")
            wt("tensor", "in2", th)
            for ch in range(8):
                op("tensor", lambda e, i=wst[:, buf * 1024 + ch * 128:buf * 1024 + (ch + 1) * 128],
                   r=SD[:, ch * 512:(ch + 1) * 512], o=PS_C[:, :], s0=(ch == 0), s1=(ch == 7):
                   e.matmul(out=o, lhsT=i, rhs=r, start=s0, stop=s1))
            wst_rd[buf] = now("t")
            wt("scalar", "t", now("t"))
            wt("scalar", "g", smb_wb)
            te = op("scalar", lambda e, i=PS_C[:, :], o=smb[:, :]:
                    e.activation(out=o, in_=i, func=AF.Identity))
            wt("gpsimd", "s", te)
            smb_wb = dma("gpsimd", dst[tt], smb[:, :])
            wt("tensor", "s", te)
        return now("g")

    def emit_kqproj_b(t_hready):
        for ch in range(8):
            guard_dma("sync")
            dma("sync", SD[:, ch * 512:(ch + 1) * 512], w_bf["wk_b"][ch], sname="in2")
            dma("sync", SD[:, 4096 + ch * 512:4096 + (ch + 1) * 512], w_bf["wv_b"][ch] if False else w_bf["wq_b"][ch], sname="in2")
        wt("sync", "g", t_hready)
        wst_rd = [now("t")] * 4
        sa_wb = [t_hready] * 4
        ps_ev = [0] * 4
        for kq in range(2):
            base = 0 if kq == 0 else 4096
            dd = kb_d if kq == 0 else qb_d
            for sl in range(8):
                for ch in range(8):
                    buf = ch % 4
                    eng, sn = ("sync", "in2") if ch % 2 == 0 else ("gpsimd", "g")
                    wt(eng, "t", wst_rd[buf])
                    th = dma(eng, wst[:, buf * 512:(buf + 1) * 512],
                             h_full[ch, :, sl * 512:(sl + 1) * 512], sname=sn)
                    wt("tensor", sn, th)
                    if ch == 0:
                        for pr in range(4):
                            wt("tensor", "s", ps_ev[pr])
                    for pr in range(4):
                        op("tensor", lambda e,
                           i=SD[:, base + ch * 512 + pr * 128:base + ch * 512 + (pr + 1) * 128],
                           r=wst[:, buf * 512:(buf + 1) * 512],
                           o=PS_BIG[:, pr * 512:(pr + 1) * 512],
                           s0=(ch == 0), s1=(ch == 7):
                           e.matmul(out=o, lhsT=i, rhs=r, start=s0, stop=s1))
                    wst_rd[buf] = now("t")
                for pr in range(4):
                    wt("scalar", "t", now("t"))
                    wt("scalar", "g", sa_wb[pr])
                    te = op("scalar", lambda e, i=PS_BIG[:, pr * 512:(pr + 1) * 512],
                            o=SA[:, pr * 512:(pr + 1) * 512],
                            sc=(1.0 if kq == 0 else 0.125):
                            e.activation(out=o, in_=i, func=AF.Identity, scale=sc))
                    ps_ev[pr] = te
                    wt("gpsimd", "s", te)
                    sa_wb[pr] = dma("gpsimd", dd[pr][:, sl * 512:(sl + 1) * 512],
                                    SA[:, pr * 512:(pr + 1) * 512])
        return now("g")

    def emit_trig_a(t_hready):
        wt("sync", "g", t_hready)
        n = 1024
        for ch in range(4):
            for blk in range(4):
                guard_dma("sync")
                th = dma("sync", SD[:, 0:n], h_full[ch, :, blk * n:(blk + 1) * n], sname="in2")
                for side in range(2):
                    iw, ib = (0, 4) if side == 0 else (8, 12)
                    wt("scalar", "in2", th)
                    t1 = op("scalar", lambda e, i=SD[:, 0:n], o=SC[:, 0:n],
                            w=prw[:, iw + ch:iw + ch + 1], b=prw[:, ib + ch:ib + ch + 1]:
                            e.activation(out=o, in_=i, func=AF.Identity, scale=w, bias=b))
                    wt("vector", "s", t1)
                    opn("vector", lambda e, i=SC[:, 0:n], o=SI[:, 0:n]: e.tensor_copy(out=o, in_=i))
                    dv()
                    kf = SC[:, 1024:1024 + n]
                    opn("vector", lambda e, i=SI[:, 0:n], o=kf: e.tensor_copy(out=o, in_=i))
                    dv()
                    gp = SC[:, 2048:2048 + n]
                    opn("vector", lambda e, i=kf, o=gp:
                        e.tensor_scalar(out=o, in0=i, scalar1=2048.0, scalar2=None, op0=ALU.is_gt))
                    dv()
                    opn("vector", lambda e, a=gp, b=kf, o=kf:
                        e.scalar_tensor_tensor(out=o, in0=a, scalar=-4096.0, in1=b,
                                               op0=ALU.mult, op1=ALU.add))
                    dv()
                    opn("vector", lambda e, i=kf, o=gp:
                        e.tensor_scalar(out=o, in0=i, scalar1=-2048.0, scalar2=None, op0=ALU.is_lt))
                    dv()
                    opn("vector", lambda e, a=gp, b=kf, o=kf:
                        e.scalar_tensor_tensor(out=o, in0=a, scalar=4096.0, in1=b,
                                               op0=ALU.mult, op1=ALU.add))
                    dv()
                    k2 = SC[:, 0:n]
                    opn("vector", lambda e, i=kf, o=gp:
                        e.tensor_scalar(out=o, in0=i, scalar1=1024.0, scalar2=None, op0=ALU.is_gt))
                    dv()
                    tv = op("vector", lambda e, a=gp, b=kf, o=k2:
                            e.scalar_tensor_tensor(out=o, in0=a, scalar=-4096.0, in1=b,
                                                   op0=ALU.mult, op1=ALU.add))
                    wt("scalar", "v", tv)
                    sn = SD[:, 1024:1024 + n]
                    cs = SD[:, 2048:2048 + n]
                    op("scalar", lambda e, i=kf, o=sn:
                       e.activation(out=o, in_=i, func=AF.Sin, scale=DELTA))
                    t3 = op("scalar", lambda e, i=k2, o=cs:
                            e.activation(out=o, in_=i, func=AF.Sin, scale=DELTA, bias=BIAS_COS))
                    wt("gpsimd", "s", t3)
                    dd = qa_d if side == 0 else ka_d
                    dma("gpsimd", dd[2 * ch][0:64, blk * n:(blk + 1) * n], cs[0:64, :])
                    dma("gpsimd", dd[2 * ch][64:128, blk * n:(blk + 1) * n], sn[0:64, :])
                    dma("gpsimd", dd[2 * ch + 1][0:64, blk * n:(blk + 1) * n], cs[64:128, :])
                    tg = dma("gpsimd", dd[2 * ch + 1][64:128, blk * n:(blk + 1) * n], sn[64:128, :])
                    wt("scalar", "g", tg)
                    wt("vector", "g", tg)
        return now("g")

    def emit_o_exchange():
        wt("gpsimd", "v", now("v"))
        wt("gpsimd", "g", now("g"))
        om4 = OM.rearrange("p (c t) -> p c t", c=4)
        sw_dma(lambda e, p: e.dma_start(
            out=o_sh[4 * p:4 * (p + 1)].rearrange("c p t -> p c t"),
            in_=om4).then_inc(sem["g"], 16))
        bv = barrier()
        wt("gpsimd", "bar", bv)
        def _oread(e, p):
            for c in range(4):
                e.dma_start(out=SA[:, c * 2048:(c + 1) * 2048],
                            in_=o_sh[4 * p + c][:, p * SH:(p + 1) * SH]).then_inc(sem["g"], 16)
                e.dma_start(out=SA[:, 8192 + c * 2048:8192 + (c + 1) * 2048],
                            in_=o_sh[4 * (1 - p) + c][:, p * SH:(p + 1) * SH]).then_inc(sem["g"], 16)
        cnt["g"] += 16 * 8
        def _run_oread(e):
            for arm in e.switch_core_id(n=128):
                _oread(e, (int(arm) % 128) % 2)
        em("gpsimd", _run_oread)
        tg = cnt["g"]
        return tg

    def emit_gemm_residual(wname, rhs_fn, bias, t_rhs_ready):
        wt("tensor", "g", t_rhs_ready)
        for do in range(8):
            guard_dma("sync")
            tw = 0
            for ch in range(8):
                tw = dma("sync", wst[:, ch * 128:(ch + 1) * 128],
                         w_bf[wname][ch][:, do * 128:(do + 1) * 128], sname="in2")
            wt("tensor", "in2", tw)
            for ts4 in range(4):
                wt("tensor", "s", now("s"))
                for ch in range(8):
                    op("tensor", lambda e, i=wst[:, ch * 128:(ch + 1) * 128],
                       r=rhs_fn(ch)[:, ts4 * 512:(ts4 + 1) * 512], o=PS_C[:, :],
                       s0=(ch == 0), s1=(ch == 7):
                       e.matmul(out=o, lhsT=i, rhs=r, start=s0, stop=s1))
                wt("scalar", "t", now("t"))
                if bias is not None:
                    te = op("scalar", lambda e, i=PS_C[:, :], o=smb[:, :], b=bias[:, do:do + 1]:
                            e.activation(out=o, in_=i, func=AF.Identity, bias=b))
                else:
                    te = op("scalar", lambda e, i=PS_C[:, :], o=smb[:, :]:
                            e.activation(out=o, in_=i, func=AF.Identity))
                wt("tensor", "s", te)
                wt("tensor", "v", now("v"))
                for q in range(4):
                    op("tensor", lambda e, i=smb[:, q * 128:(q + 1) * 128],
                       o=PS_T[:, q * 128:(q + 1) * 128]:
                       e.transpose(out=o, in_=i, identity=idn[:, :]))
                wt("vector", "t", now("t"))
                for q in range(4):
                    tt_ = ts4 * 4 + q
                    op("vector", lambda e,
                       a=x_res[:, tt_ * D + do * 128:tt_ * D + (do + 1) * 128],
                       b=PS_T[:, q * 128:(q + 1) * 128],
                       o=x_res[:, tt_ * D + do * 128:tt_ * D + (do + 1) * 128]:
                       e.tensor_tensor(out=o, in0=a, in1=b, op=ALU.add))

    def emit_attn_a():
        t_h = emit_h_production()
        t_v = emit_vproj("vw_a", va_d, t_h)
        t_t = emit_trig_a(t_h)
        half_s = [0, 0]
        half_v = [0, 0]
        pv_last = [0, 0, 0, 0]
        t_sd_free = 0
        for h in range(8):
            wt("sync", "g", t_t)
            wt("sync", "t", t_sd_free)
            wt("gpsimd", "t", t_sd_free)
            tk_ch = []
            tq_ch = []
            for c in range(8):
                tk_ch.append(dma("sync", SD[:, c * 512:(c + 1) * 512],
                                 ka_d[h][:, c * 512:(c + 1) * 512], sname="in2"))
                tq_ch.append(dma("gpsimd", SD[:, 4096 + c * 512:4096 + (c + 1) * 512],
                                 qa_d[h][:, c * 512:(c + 1) * 512], sname="g"))
            for qc in range(8):
                nkt = 4 * qc + 4
                wt("tensor", "in2", tk_ch[qc])
                wt("tensor", "g", tq_ch[qc])
                for g in range((nkt + 1) // 2):
                    half = g % 2
                    base = half * 1024
                    kts = [2 * g + j for j in range(2) if 2 * g + j < nkt]
                    wt("tensor", "v", half_v[half])
                    wt("tensor", "s", half_s[half])
                    for j, kt in enumerate(kts):
                        op("tensor", lambda e, i=SD[:, kt * 128:(kt + 1) * 128],
                           r=SD[:, 4096 + qc * 512:4096 + (qc + 1) * 512],
                           o=PS_BIG[:, base + j * 512:base + (j + 1) * 512]:
                           e.matmul(out=o, lhsT=i, rhs=r, start=True, stop=True))
                    wt("scalar", "t", now("t"))
                    ts = op("scalar", lambda e, i=PS_BIG[:, base:base + len(kts) * 512],
                            o=SA[:, kts[0] * 512:(kts[0] + len(kts)) * 512]:
                            e.activation(out=o, in_=i, func=AF.Exp, scale=SCALE_A))
                    half_s[half] = ts
                    half_v[half] = now("v")
                    for kt in kts:
                        if kt >= 4 * qc:
                            r = kt - 4 * qc
                            wt("vector", "s", ts)
                            half_v[half] = op("vector",
                               lambda e, a=SA[:, kt * 512:(kt + 1) * 512],
                               m=msk[:, r * 512:(r + 1) * 512],
                               o=SA[:, kt * 512:(kt + 1) * 512]:
                               e.tensor_tensor(out=o, in0=a, in1=m, op=ALU.mult))
                if qc == 7:
                    t_sd_free = now("t")
                wt("tensor", "v", now("v"))
                wt("tensor", "s", now("s"))
                for kt in range(nkt):
                    buf = kt % 4
                    wt("sync", "t", pv_last[buf])
                    tvd = dma("sync", pvs[:, buf * 65:buf * 65 + 64],
                              va_d[kt][:, h * 64:(h + 1) * 64], sname="in2")
                    wt("tensor", "in2", tvd)
                    pv_last[buf] = op("tensor", lambda e, i=pvs[:, buf * 65:(buf + 1) * 65],
                       r=SA[:, kt * 512:(kt + 1) * 512], o=PS_A[0:65, :],
                       s0=(kt == 0), s1=(kt == nkt - 1):
                       e.matmul(out=o, lhsT=i, rhs=r, start=s0, stop=s1))
                wt("vector", "t", now("t"))
                opn("vector", lambda e, i=PS_A[64:65, :], o=smf[0:1, :]:
                    e.reciprocal(out=o, in_=i))
                dv()
                tvr = op("vector", lambda e, i=PS_A[0:64, :], o=smb[0:64, :]:
                         e.tensor_copy(out=o, in_=i))
                wt("tensor", "v", tvr)
                tb = op("tensor", lambda e, i=on1f[:, 0:64], r=smf[0:1, :], o=PS_B[0:64, :]:
                        e.matmul(out=o, lhsT=i, rhs=r, start=True, stop=True))
                wt("vector", "t", tb)
                if h % 2 == 0:
                    dst = OM[0:64, (h // 2) * S + qc * 512:(h // 2) * S + (qc + 1) * 512]
                    op("vector", lambda e, a=smb[0:64, :], b=PS_B[0:64, :], o=dst:
                       e.tensor_tensor(out=o, in0=a, in1=b, op=ALU.mult))
                else:
                    tvo = op("vector", lambda e, a=smb[0:64, :], b=PS_B[0:64, :], o=smf[0:64, :]:
                             e.tensor_tensor(out=o, in0=a, in1=b, op=ALU.mult))
                    tvo2 = op("vector", lambda e, i=smf[0:64, :], o=smb[0:64, :]:
                              e.tensor_copy(out=o, in_=i))
                    wt("gpsimd", "v", tvo2)
                    tg = dma("gpsimd", OM[64:128, (h // 2) * S + qc * 512:
                                          (h // 2) * S + (qc + 1) * 512], smb[0:64, :])
                    wt("vector", "g", tg)
                wt("tensor", "v", now("v"))
        t_o = emit_o_exchange()
        emit_gemm_residual("wo_a", lambda ch: SA[:, ch * 2048:(ch + 1) * 2048], None, t_o)

    def emit_ffn():
        for tt in range(NT):
            wt("vector", "t", now("t"))
            wt("vector", "s", now("s"))
            tln = ln_tile(tt, SC[:, 0:D])
            emit_quant(SC[:, 0:D], SD[:, 0:D], D)
            wt("tensor", "v", now("v"))
            for ch in range(8):
                op("tensor", lambda e, i=SD[:, ch * 128:(ch + 1) * 128],
                   o=PS_T[:, ch * 128:(ch + 1) * 128]:
                   e.transpose(out=o, in_=i, identity=idn[:, :]))
            wt("vector", "t", now("t"))
            for ch in range(8):
                op("vector", lambda e, i=PS_T[:, ch * 128:(ch + 1) * 128],
                   o=SA[:, ch * 2048 + tt * 128:ch * 2048 + (tt + 1) * 128]:
                   e.tensor_copy(out=o, in_=i))
        emit_gemm_residual("wffT", lambda ch: SA[:, ch * 2048:(ch + 1) * 2048], bfc, now("g"))

    def emit_attn_b():
        t_h = emit_h_production()
        emit_vproj("wv_b", vb_d, t_h)
        t_kq = emit_kqproj_b(t_h)
        pv_wst_rd = now("t")
        ps_rd_s = [now("s"), now("s")]
        ps_rd_v = [now("v"), now("v")]
        sa_rd_v = [0, 0]
        pay_rd = [0, 0]
        pv_ev = 0
        t_sd_free = 0
        t_kq = now("g")
        for pr in range(4):
            wt("sync", "g", t_kq)
            wt("sync", "t", t_sd_free)
            wt("gpsimd", "t", t_sd_free)
            tk_ch = 0
            tq_ch = []
            for c in range(8):
                tk_ch = dma("sync", SD[:, c * 512:(c + 1) * 512],
                            kb_d[pr][:, c * 512:(c + 1) * 512], sname="in2")
                tq_ch.append(dma("gpsimd", SD[:, 4096 + c * 512:4096 + (c + 1) * 512],
                                 qb_d[pr][:, c * 512:(c + 1) * 512], sname="g"))
            wt("tensor", "in2", tk_ch)
            for h01 in range(2):
                wt("sync", "t", pv_wst_rd)
                tvc = 0
                for kt in range(32):
                    tvc = dma("sync", wst[:, kt * 64:(kt + 1) * 64],
                              vb_d[kt][:, (pr * 2 + h01) * 64:(pr * 2 + h01 + 1) * 64],
                              sname="in2")
                for qc in range(8):
                    q0 = 4096 + qc * 512
                    wt("tensor", "g", tq_ch[qc])
                    if bstage < 1:
                        continue
                    tile = 0
                    for qt2 in range(4):
                        qt = 4 * qc + qt2
                        for hf in range(2):
                            for kh in range(2):
                                pp = tile % 2
                                tile += 1
                                wt("tensor", "v", ps_rd_v[pp])
                                wt("tensor", "s", ps_rd_s[pp])
                                for ksl in range(2):
                                    op("tensor", lambda e,
                                       i=SD[:, 4096 + qt * 128:4096 + (qt + 1) * 128][h01 * 64:(h01 + 1) * 64, :],
                                       r=SD[:, hf * 2048 + kh * 1024 + ksl * 512:hf * 2048 + kh * 1024 + (ksl + 1) * 512][h01 * 64:(h01 + 1) * 64, :],
                                       o=PS_BIG[:, pp * 1024 + ksl * 512:pp * 1024 + (ksl + 1) * 512]:
                                       e.matmul(out=o, lhsT=i, rhs=r, start=True, stop=True))
                                t_mm = now("t")
                                wt("scalar", "t", t_mm)
                                wt("scalar", "v", sa_rd_v[pp])
                                tex = op("scalar", lambda e, i=PS_BIG[:, pp * 1024:(pp + 1) * 1024],
                                         o=SA[:, pp * 1024:(pp + 1) * 1024]:
                                         e.activation(out=o, in_=i, func=AF.Exp))
                                ps_rd_s[pp] = tex
                                wt("vector", "s", tex)
                                opn("vector", lambda e,
                                   i=SA[:, pp * 1024:(pp + 1) * 1024].rearrange("p (a b) -> p a b", b=64),
                                   o=SEb[:, qt2 * 64 + hf * 32 + kh * 16:qt2 * 64 + hf * 32 + kh * 16 + 16]:
                                   e.reduce_max(out=o, in_=i, axis=AX.X))
                                sa_rd_v[pp] = op("vector", lambda e,
                                   i=SA[:, pp * 1024:(pp + 1) * 1024].rearrange("p (a b) -> p a b", b=64),
                                   o=SEf[:, 768 + qt2 * 64 + hf * 32 + kh * 16:768 + qt2 * 64 + hf * 32 + kh * 16 + 16]:
                                   e.reduce_sum(out=o, in_=i, axis=AX.X))
                    if bstage < 4:
                        continue
                    LC = SEf[:, 0:256]
                    MR = SEf[:, 256:512]
                    LNL = SEf[:, 512:768]
                    D4 = SEf[:, 768:1024]
                    dv()
                    t_lc = 0
                    for qt2 in range(4):
                        t_lc = op("vector", lambda e, i=D4[:, qt2 * 64:(qt2 + 1) * 64],
                                  o=LC[:, qt2 * 64:(qt2 + 1) * 64]:
                                  e.tensor_tensor_scan(out=o, data0=i, data1=i, initial=0.0,
                                                       op0=ALU.add, op1=ALU.bypass))
                        dv()
                    t_mr = 0
                    for qt2 in range(4):
                        t_mr = op("vector", lambda e, i=SEb[:, qt2 * 64:(qt2 + 1) * 64],
                                  o=MR[:, qt2 * 64:(qt2 + 1) * 64]:
                                  e.tensor_tensor_scan(out=o, data0=i, data1=i, initial=0.0,
                                                       op0=ALU.max, op1=ALU.bypass))
                        dv()
                    if bstage == 41:
                        continue
                    wt("scalar", "v", t_mr)
                    t_ln = op("scalar", lambda e, i=LC, o=LNL:
                              e.activation(out=o, in_=i, func=AF.Ln))
                    t_ln2 = op("scalar", lambda e, i=MR, o=D4:
                               e.activation(out=o, in_=i, func=AF.Ln))
                    wt("vector", "s", t_ln2)
                    dv()
                    opn("vector", lambda e, a=D4, b=LNL, o=D4:
                        e.tensor_tensor(out=o, in0=a, in1=b, op=ALU.subtract))
                    dv()
                    if bstage == 42:
                        continue
                    for qt2 in range(4):
                        opn("vector", lambda e, i=D4[:, qt2 * 64:(qt2 + 1) * 64],
                            o=LC[:, qt2 * 64:(qt2 + 1) * 64]:
                            e.tensor_tensor_scan(out=o, data0=i, data1=i, initial=0.0,
                                                 op0=ALU.add, op1=ALU.bypass))
                        dv()
                    opn("vector", lambda e, a=LC, b=D4, o=MR:
                        e.tensor_tensor(out=o, in0=a, in1=b, op=ALU.subtract))
                    dv()
                    if bstage == 43:
                        continue
                    for qt2 in range(4):
                        opn("vector", lambda e, a=MR[:, qt2 * 64:(qt2 + 1) * 64],
                            s_=LC[:, qt2 * 64 + 63:qt2 * 64 + 64].broadcast_to((128, 64)),
                            o=MR[:, qt2 * 64:(qt2 + 1) * 64]:
                            e.scalar_tensor_tensor(out=o, in0=a, scalar=-1.0, in1=s_,
                                                   op0=ALU.mult, op1=ALU.add))
                        dv()
                    for qt2 in range(4):
                        opn("vector", lambda e, i=LNL[:, qt2 * 64 + 63:qt2 * 64 + 64],
                            o=SEf[:, 4 + qt2:5 + qt2]:
                            e.tensor_scalar(out=o, in0=i, scalar1=-1.0, scalar2=None, op0=ALU.mult))
                    for qt2 in range(4):
                        opn("vector", lambda e, i=D4[:, qt2 * 64 + 63:qt2 * 64 + 64],
                            o=SEf[:, qt2:qt2 + 1]: e.tensor_copy(out=o, in_=i))
                    dv()
                    if bstage == 44:
                        continue
                    for qt2 in range(4):
                        wt("vector", "t", pay_rd[0])
                        if bstage == 448:
                            opn("vector", lambda e, i=MR[:, qt2 * 64:(qt2 + 1) * 64],
                                o=smf[:, 0:64]: e.tensor_copy(out=o, in_=i))
                            continue
                        if bstage == 449:
                            opn("vector", lambda e, i=MR[:, qt2 * 64:(qt2 + 1) * 64],
                                o=smf[:, 0:64]:
                                e.tensor_scalar(out=o, in0=i, scalar1=-1.0, scalar2=None,
                                                op0=ALU.subtract))
                            continue
                        opn("vector", lambda e, i=MR[:, qt2 * 64:(qt2 + 1) * 64],
                            c=SEf[:, qt2:qt2 + 1],
                            o=(smf[:, 0:64] if bstage == 450 else pay[:, 0:64]):
                            e.tensor_scalar(out=o, in0=i, scalar1=c, scalar2=None,
                                            op0=ALU.subtract))
                        if bstage in (450, 451):
                            continue
                        opn("vector", lambda e, i=SEf[:, 4 + qt2:5 + qt2],
                            o=pay[:, 64:65]: e.tensor_copy(out=o, in_=i))
                        if bstage == 452:
                            continue
                        dv()
                        tv3 = op("vector", lambda e, a=SEf[:, 4 + qt2:5 + qt2],
                                 b=pay[:, 64:65], o=pay[:, 65:66]:
                                 e.tensor_tensor(out=o, in0=a, in1=b, op=ALU.subtract))
                        if bstage == 45:
                            continue
                        wt("tensor", "v", tv3)
                        tt4 = op("tensor", lambda e, i=pay[:, 0:128],
                                 o=PS_T[:, qt2 * 128:(qt2 + 1) * 128]:
                                 e.transpose(out=o, in_=i, identity=idn[:, :]))
                        pay_rd[0] = tt4
                        if bstage == 46:
                            continue
                        wt("vector", "t", tt4)
                        op("vector", lambda e, i=PS_T[:, qt2 * 128:(qt2 + 1) * 128][0:66, :],
                           o=smb[0:66, qt2 * 128:(qt2 + 1) * 128]:
                           e.tensor_copy(out=o, in_=i))
                    if bstage < 5:
                        continue
                    dh_s = [now("s"), now("s")]
                    wt("tensor", "v", now("v"))
                    for g in range(16):
                        half = g % 2
                        base = half * 1024
                        wt("tensor", "s", dh_s[half])
                        for j in range(2):
                            kt = 2 * g + j
                            op("tensor", lambda e,
                               i=SD[:, kt * 128:(kt + 1) * 128][h01 * 64:(h01 + 1) * 64, :],
                               r=SD[:, q0:q0 + 512][h01 * 64:(h01 + 1) * 64, :],
                               o=PS_BIG[:, base + j * 512:base + (j + 1) * 512]:
                               e.matmul(out=o, lhsT=i, rhs=r, start=True, stop=False))
                        for j in range(2):
                            kt = 2 * g + j
                            op("tensor", lambda e, i=selb[0:66, kt * 128:(kt + 1) * 128],
                               r=smb[0:66, 0:512],
                               o=PS_BIG[:, base + j * 512:base + (j + 1) * 512]:
                               e.matmul(out=o, lhsT=i, rhs=r, start=False, stop=True))
                        wt("scalar", "t", now("t"))
                        tsx = op("scalar", lambda e, i=PS_BIG[:, base:base + 1024],
                                 o=SA[:, g * 1024:(g + 1) * 1024]:
                                 e.activation(out=o, in_=i, func=AF.Exp))
                        dh_s[half] = tsx
                    ps_rd_s[0] = dh_s[0]
                    ps_rd_s[1] = dh_s[1]
                    if bstage < 60:
                        continue
                    wt("tensor", "s", now("s"))
                    wt("tensor", "v", pv_ev)
                    if qc == 0:
                        wt("tensor", "in2", tvc)
                    for kt in range(32):
                        op("tensor", lambda e, i=wst[:, kt * 64:(kt + 1) * 64],
                           r=SA[:, kt * 512:(kt + 1) * 512], o=PS_A[0:64, :],
                           s0=(kt == 0), s1=(kt == 31):
                           e.matmul(out=o, lhsT=i, rhs=r, start=s0, stop=s1))
                    wt("vector", "t", now("t"))
                    pv_ev = op("vector", lambda e, i=PS_A[0:64, :],
                               o=OM[h01 * 64:(h01 + 1) * 64, pr * S + qc * 512:pr * S + (qc + 1) * 512]:
                               e.tensor_copy(out=o, in_=i))
                pv_wst_rd = now("t")
                t_sd_free = now("t")
        t_o = emit_o_exchange()
        emit_gemm_residual("wo_b", lambda ch: SA[:, ch * 2048:(ch + 1) * 2048], None, t_o)

    if n_sub == -3:
        emit_attn_b()
    if n_sub >= 1:
        emit_attn_a()
    if n_sub >= 2:
        emit_ffn()
    if n_sub >= 3:
        emit_attn_b()
    if n_sub >= 4:
        emit_ffn()

    wt("gpsimd", "v", now("v"))
    for tt in range(NT):
        dma("gpsimd", y_out.rearrange("(n p) d -> n p d", p=128)[tt],
            x_res[:, tt * D:(tt + 1) * D])
    wt("gpsimd", "g", now("g"))

    blk = st.enter_context(nc.Block())

    @blk.sync
    def _(e):
        for fn in EM["sync"]:
            fn(e)

    @blk.vector
    def _(e):
        for fn in EM["vector"]:
            fn(e)

    @blk.scalar
    def _(e):
        for fn in EM["scalar"]:
            fn(e)

    @blk.tensor
    def _(e):
        for fn in EM["tensor"]:
            fn(e)

    @blk.gpsimd
    def _(e):
        for fn in EM["gpsimd"]:
            fn(e)

    st.close()
    global _LAST_CNT
    _LAST_CNT = dict(cnt)
    _LAST_CNT["_em"] = {k: len(v) for k, v in EM.items()}
    try:
        _LAST_CNT["_insts"] = len(list(nc.all_instructions()))
    except Exception:
        pass
    return nc


def _host_prep(inputs, core):
    p = core % 2
    b = core // 2
    perm = np.arange(D)
    if p == 1:
        perm = np.concatenate([perm[512:], perm[:512]])
    hsl = slice(p * 8, p * 8 + 8)

    x = inputs["L0_attn_a_x"]
    xc = np.ascontiguousarray(x[b, p * SH:(p + 1) * SH, :][:, perm], np.float32)

    def chunkT(w):
        return np.ascontiguousarray(w.reshape(8, 128, -1), np.float32)

    wq = inputs["L0_attn_a_w_q"][hsl].reshape(512)
    bq = inputs["L0_attn_a_b_q"][hsl].reshape(512)
    wk = inputs["L0_attn_a_w_k"][hsl].reshape(512)
    bk = inputs["L0_attn_a_b_k"][hsl].reshape(512)
    tprm = np.stack([wq.reshape(4, 128), bq.reshape(4, 128),
                     wk.reshape(4, 128), bk.reshape(4, 128)], 0)
    tprm = np.ascontiguousarray(tprm.reshape(16, 128).T, np.float32)

    vw = inputs["L0_attn_a_v_w"][p * 512:(p + 1) * 512, :][:, perm]
    vw_a = chunkT(np.ascontiguousarray(vw.T))
    ow = inputs["L0_attn_a_out_w"]
    wo_a = chunkT(np.ascontiguousarray(ow.T[perm][:, perm]))

    wqb = inputs["L0_attn_b_Wq"][perm][:, p * 512:(p + 1) * 512]
    wkb = inputs["L0_attn_b_Wk"][perm][:, p * 512:(p + 1) * 512]
    wvb = inputs["L0_attn_b_Wv"][perm][:, p * 512:(p + 1) * 512]
    wob = inputs["L0_attn_b_Wo"][perm][:, perm]
    wff = inputs["L0_ffn_weight"][perm][:, perm]
    bff = inputs["L0_ffn_bias"][perm]

    masks = np.zeros((4, 128, 512), ml_dtypes.bfloat16)
    for r in range(4):
        i_ = np.arange(128)[:, None] + 128 * r
        j_ = np.arange(512)[None, :]
        masks[r] = (i_ <= j_).astype(ml_dtypes.bfloat16)
    ident = np.eye(128, dtype=ml_dtypes.bfloat16)
    ind2 = np.zeros((2, 128), ml_dtypes.bfloat16)
    ind2[0, 0:64] = 1
    ind2[1, 64:128] = 1
    ones2 = np.ones((2, 128), ml_dtypes.bfloat16)
    ones1 = np.ones((1, 128), ml_dtypes.bfloat16)
    bigm = np.zeros((128, 192), ml_dtypes.bfloat16)
    bigm[0:64, 126] = 1
    bigm[64:128, 127] = 1
    selb = np.zeros((128, 4096), ml_dtypes.bfloat16)
    for kt in range(32):
        for c in range(128):
            selb[2 * kt + (c >= 64), kt * 128 + c] = 1
        selb[64, kt * 128:(kt + 1) * 128] = 1
        selb[65, kt * 128:(kt + 1) * 128] = 1

    return {
        "x_in": xc, "tprm": tprm,
        "bffi": np.ascontiguousarray(bff.reshape(8, 128).T, np.float32),
        "vw_a": vw_a, "wo_a": wo_a,
        "wq_b": chunkT(wqb), "wk_b": chunkT(wkb), "wv_b": chunkT(wvb),
        "wo_b": chunkT(wob), "wff": chunkT(wff),
        "masks_i": masks, "ident_i": ident, "ind2_i": ind2, "ones2_i": ones2,
        "ones1_i": ones1, "ones1f_i": np.ones((1, 128), np.float32), "bigm_i": bigm,
        "selb_i": selb,
        "cvals_i": np.ascontiguousarray(
            np.broadcast_to(np.array([1e-5, BIAS_COS, 0.0, 0.0], np.float32), (128, 4))),
    }, perm, b, p


def kernel(**inputs):
    key = ("nc", SUBLAYERS, ATTNB_STAGE)
    if key not in _CACHE:
        _CACHE[key] = build_nc(SUBLAYERS, ATTNB_STAGE)
    nc = _CACHE[key]
    in_maps = []
    metas = []
    for c in range(8):
        m, perm, b, p = _host_prep(inputs, c)
        in_maps.append(m)
        metas.append((perm, b, p))
    res = run_bass_kernel_spmd(nc, in_maps, core_ids=list(range(8)))
    out = np.empty((B, S, D), np.float32)
    for c in range(8):
        perm, b, p = metas[c]
        inv = np.argsort(perm)
        out[b, p * SH:(p + 1) * SH, :] = res.results[c]["y_out"][:, inv]
    return out


if __name__ == "__main__":
    nc = build_nc()
    print("built ok")
